# version 06
# baseline (speedup 1.0000x reference)
import numpy as np
from contextlib import ExitStack
import concourse.bass as bass
import concourse.mybir as mybir
from concourse.bass import IndirectOffsetOnAxis
from concourse.bass_utils import run_bass_kernel_spmd

F32 = mybir.dt.float32
BF16 = mybir.dt.bfloat16
I32 = mybir.dt.int32
AF = mybir.ActivationFunctionType
ALU = mybir.AluOpType
AX = mybir.AxisListType.X

DM = 1024
DH = 1024
NCH = 8
DS = 16
HF = 4096
NE = 8
CAP = 4096
NCORES = 8
BIGI = 1 << 20



_regc = [0]


def rn(p):
    _regc[0] += 1
    return f"{p}{_regc[0]}"

class Sem:
    def __init__(self, h):
        self.h = h
        self.c = 0

    def inc(self, instr, n=1):
        instr.then_inc(self.h, n)
        self.c += n
        return self.c


def w(eng, *sems):
    for s in sems:
        if s.c:
            eng.wait_ge(s.h, s.c)


def slot_branch(e, sslot, arm0, arm1, regname="rs"):
    with e.register(rn(regname)) as rs_:
        e.reg_load(rs_, sslot[0:1, 0:1])
        with e.If_eq(rs_, 0):
            arm0()
        with e.Else():
            arm1()


def poll_partner(e, rpoll, rtmp, flag_ap_pt, sflag, snonce_sb, pp_h):
    with e.register(rn("rf")) as rf, e.register(rn("rc")) as rc, \
         e.register(rn("rnn")) as rnn:
        e.reg_load(rnn, snonce_sb[0:1, 0:1])
        e.reg_mov(rc, 1)
        with e.While(rc):
            e.reg_add(rpoll, rpoll, 16)
            e.dma_start(out=sflag[:], in_=flag_ap_pt).then_inc(pp_h, 16)
            e.wait_ge(pp_h, e.snap(rpoll))
            e.reg_load(rf, sflag[0:1, 0:1])
            e.reg_alu(rc, rf, e.snap(rnn), ALU.not_equal)


def build_L1(T=8192, MT=512, ST=1024, WST=None, dbg=False, stop='full'):
    TH = T // 2
    NMT = T // MT
    NST = T // ST
    WST = WST or ST
    NWS = T // WST
    NRG = TH // 512
    nc = bass.Bass("TRN2", target_bir_lowering=False, debug=False,
                   detect_race_conditions=False)

    ein = lambda n, s, d=F32: nc.dram_tensor(n, s, d, kind="ExternalInput").ap()
    xT = ein("xT", [DM, T])
    x_tm = ein("x_tm", [TH, DM])
    winT = ein("winT", [DM, 2 * DH])
    cwv = ein("cwv", [128, NCH * 4])
    cbv = ein("cbv", [128, NCH])
    wxT = ein("wxT", [DH, 33])
    wdtv = ein("wdtv", [128, NCH])
    bdtv = ein("bdtv", [128, NCH])
    alogv = ein("alogv", [128, NCH * DS])
    dpv = ein("dpv", [128, NCH])
    woutT = ein("woutT", [DH, DM])
    rw1 = ein("rw1", [DM, HF])
    rb1v = ein("rb1v", [128, HF // 128])
    rw2 = ein("rw2", [HF, NE])
    rb2v = ein("rb2v", [NE, 1])
    identv = ein("identv", [128, 128])
    iotav = ein("iotav", [1, NE])
    noncev = ein("noncev", [1, 1], I32)
    slotv = ein("slotv", [1, 1], I32)
    lncv = ein("lncv", [128, 2])

    x1out = nc.dram_tensor("x1out", [TH, DM], F32, kind="ExternalOutput").ap()
    ivout = nc.dram_tensor("ivout", [TH, 4], F32, kind="ExternalOutput").ap()

    ssm_sh = nc.dram_tensor("ssm_sh", [2, 33, T], F32, addr_space="Shared")
    mix_sh = nc.dram_tensor("mix_sh", [2 * T, DM], F32, addr_space="Shared")
    flag_sh = nc.dram_tensor("flag_sh", [2, 2, 1], I32, addr_space="Shared")

    if dbg:
        dbg_u = nc.dram_tensor("dbg_u", [NCH, 128, T], F32, kind="ExternalOutput").ap()
        dbg_z = nc.dram_tensor("dbg_z", [NCH, 128, T], F32, kind="ExternalOutput").ap()
        dbg_y2 = nc.dram_tensor("dbg_y2", [NCH, 128, T], F32, kind="ExternalOutput").ap()
        dbg_ssm = nc.dram_tensor("dbg_ssm", [33, T], F32, kind="ExternalOutput").ap()
        dbg_mix = nc.dram_tensor("dbg_mix", [2 * T, DM], F32, kind="ExternalOutput").ap()
        dbg_ln = nc.dram_tensor("dbg_ln", [128, 8 + 2 * DM], F32, kind="ExternalOutput").ap()
    zsil_d = nc.dram_tensor("zsil_d", [NCH, 128, T], F32)
    u_d = nc.dram_tensor("u_d", [NCH, 128, T], F32)
    y2_d = nc.dram_tensor("y2_d", [NCH, 128, T], F32)
    ssmb_d = nc.dram_tensor("ssmb_d", [33, T], BF16)
    ssm0_d = nc.dram_tensor("ssm0_d", [1, T], F32)

    es = ExitStack()
    sb = lambda n, s, d=F32: es.enter_context(nc.sbuf_tensor(n, s, d))
    ps = lambda n, s: es.enter_context(nc.psum_tensor(n, s, F32))
    cw = sb("cw", [128, NCH, 4]); cb = sb("cb", [128, NCH])
    wdtc = sb("wdtc", [128, NCH]); bdtc = sb("bdtc", [128, NCH])
    ac = sb("ac", [128, NCH * DS]); dpc = sb("dpc", [128, NCH])
    id16 = sb("id16", [128, 128], BF16); idf = sb("idf", [128, 128])
    io8b = sb("io8b", [128, NE]); snonce = sb("snonce", [1, 1], I32)
    sflag = sb("sflag", [1, 1], I32); sslot = sb("sslot", [1, 1], I32)
    hcarry = sb("hcarry", [128, NCH * DS])
    lncst = sb("lncst", [128, 2])
    _qL = es.enter_context(nc.semaphore("qL"))
    _qG = es.enter_context(nc.semaphore("qG"))
    _sV = es.enter_context(nc.semaphore("sV"))
    _sA = es.enter_context(nc.semaphore("sA"))
    _sT = es.enter_context(nc.semaphore("sT"))
    _pp = es.enter_context(nc.semaphore("pp"))
    _pm = es.enter_context(nc.semaphore("pm"))
    block = es.enter_context(nc.Block())
    if True:
        qL, qG, sV, sA, sT = Sem(_qL), Sem(_qG), Sem(_sV), Sem(_sA), Sem(_sT)
        pm = Sem(_pm)

        rpoll_box = {}

        @block.gpsimd
        def _(e):
            rpoll_box["r"] = e.alloc_register(rn("rpoll"))
            e.reg_mov(rpoll_box["r"], 0)

        @block.sync
        def _(e):
            for t_, src in ((cw, cwv), (cb, cbv), (wdtc, wdtv), (bdtc, bdtv),
                            (ac, alogv), (dpc, dpv), (idf, identv),
                            (snonce, noncev), (sslot, slotv), (lncst, lncv)):
                qL.inc(e.dma_start(out=t_[:], in_=src), 16)
            qL.inc(e.dma_start(out=io8b[:], in_=iotav.partition_broadcast(128)), 16)

        @block.vector
        def _(e):
            w(e, qL)
            sV.inc(e.tensor_copy(out=id16[:], in_=idf[:]))
            sV.inc(e.memset(hcarry[:], 0.0))

        @block.scalar
        def _(e):
            w(e, qL)
            sA.inc(e.activation(out=ac[:], in_=ac[:], func=AF.Exp))

        @block.vector
        def _(e):
            w(e, sA)
            sV.inc(e.tensor_scalar(out=ac[:], in0=ac[:], scalar1=-1.0,
                                   scalar2=None, op0=ALU.mult))

        esM = ExitStack()
        w16 = esM.enter_context(nc.sbuf_tensor("w16", [128, 8, 2 * DH], BF16))
        wx16 = esM.enter_context(nc.sbuf_tensor("wx16", [128, NCH, 33], BF16))
        ssma = esM.enter_context(nc.sbuf_tensor("ssma", [33, T], F32))
        if True:
            with nc.sbuf_tensor("wtmp", [128, 8, 2 * DH], F32) as wtmp:
                @block.sync
                def _(e):
                    w(e, sV)
                    qL.inc(e.dma_start(
                        out=wtmp[:],
                        in_=winT.rearrange("(k p) n -> p k n", p=128)), 16)

                @block.vector
                def _(e):
                    w(e, qL)
                    sV.inc(e.tensor_copy(out=w16[:], in_=wtmp[:]))

                @block.sync
                def _(e):
                    w(e, sV)
                    qL.inc(e.dma_start(
                        out=wtmp[:, 0, 0:NCH * 33].rearrange(
                            "p (k n) -> p k n", k=NCH),
                        in_=wxT.rearrange("(k p) n -> p k n", p=128)), 16)

                @block.vector
                def _(e):
                    w(e, qL)
                    sV.inc(e.tensor_copy(
                        out=wx16[:],
                        in_=wtmp[:, 0, 0:NCH * 33].rearrange(
                            "p (k n) -> p k n", k=NCH)))

            esMt = ExitStack()
            xt32 = esMt.enter_context(nc.sbuf_tensor("xt32", [128, 8, MT], F32))
            xt16 = esMt.enter_context(nc.sbuf_tensor("xt16", [128, 8, MT], BF16))
            xdh = esMt.enter_context(nc.sbuf_tensor("xdh", [128, NCH, MT + 3], F32))
            cacc = esMt.enter_context(nc.sbuf_tensor("cacc", [128, 2, MT], F32))
            ut = esMt.enter_context(nc.sbuf_tensor("ut", [128, NCH, MT], F32))
            ut16 = esMt.enter_context(nc.sbuf_tensor("ut16", [128, NCH, MT], BF16))
            zst = esMt.enter_context(nc.sbuf_tensor("zst", [128, NCH, MT], F32))
            psa = esMt.enter_context(nc.psum_tensor("psa", [128, 2, 512], F32))
            psm = esMt.enter_context(nc.psum_tensor("psm", [33, 512], F32))
            if True:
                NSUB = MT // 512

                @block.vector
                def _(e):
                    sV.inc(e.memset(xdh[:], 0.0))

                for i in range(NMT):
                    @block.sync
                    def _(e, i=i):
                        w(e, sV, sT)
                        qL.inc(e.dma_start(
                            out=xt32[:],
                            in_=xT[:, i * MT:(i + 1) * MT]
                                .rearrange("(k p) t -> p k t", p=128)), 16)

                    @block.vector
                    def _(e, i=i):
                        w(e, qL)
                        sV.inc(e.tensor_copy(out=xt16[:], in_=xt32[:]))

                    mk = {}

                    @block.tensor
                    def _(e, i=i, mk=mk):
                        w(e, sV, sA)
                        saw = sA.c
                        planned = {}
                        for j in range(16):
                            if j < 8:
                                planned[j] = saw + 2 * j + 2
                            else:
                                planned[j] = saw + 16 + (j - 8) + 1
                        for o in range(16):
                            if o >= 2:
                                e.wait_ge(sA.h, planned[o - 2])
                            for k in range(8):
                                ins = e.matmul(psa[:, o % 2, 0:MT],
                                               w16[:, k, o * 128:(o + 1) * 128],
                                               xt16[:, k, :],
                                               start=(k == 0), stop=(k == 7))
                            sT.inc(ins)
                            mk[("xz", o)] = sT.c

                    @block.scalar
                    def _(e, i=i, mk=mk):
                        w(e, qG)
                        for c in range(NCH):
                            e.wait_ge(sT.h, mk[("xz", c)])
                            sA.inc(e.copy(out=xdh[:, c, 0:3],
                                          in_=xdh[:, c, MT:MT + 3]))
                            e.wait_ge(sA.h, sA.c)
                            sA.inc(e.copy(out=xdh[:, c, 3:MT + 3],
                                          in_=psa[:, c % 2, 0:MT]))
                            mk[("evac", c)] = sA.c
                        for o in range(8, 16):
                            e.wait_ge(sT.h, mk[("xz", o)])
                            sA.inc(e.activation(out=zst[:, o - 8, :],
                                                in_=psa[:, o % 2, 0:MT],
                                                func=AF.Silu))
                            mk[("zev", o)] = sA.c

                    @block.vector
                    def _(e, i=i, mk=mk):
                        saw = sA.c
                        for c in range(NCH):
                            e.wait_ge(sA.h, mk[("evac", c)])
                            if c >= 1:
                                e.wait_ge(sA.h, saw + c)
                            sV.inc(e.tensor_scalar(out=cacc[:, 0, :],
                                            in0=xdh[:, c, 0:MT],
                                            scalar1=cw[:, c, 0:1], scalar2=None,
                                            op0=ALU.mult))
                            for j in (1, 2, 3):
                                e.wait_ge(sV.h, sV.c)
                                sV.inc(e.scalar_tensor_tensor(
                                    out=cacc[:, j % 2, :],
                                    in0=xdh[:, c, j:j + MT],
                                    scalar=cw[:, c, j:j + 1],
                                    in1=cacc[:, (j + 1) % 2, :],
                                    op0=ALU.mult, op1=ALU.add))
                            mk[("conv", c)] = sV.c

                    @block.scalar
                    def _(e, i=i, mk=mk):
                        for c in range(NCH):
                            e.wait_ge(sV.h, mk[("conv", c)])
                            sA.inc(e.activation(out=ut[:, c, :],
                                                in_=cacc[:, 1, :], func=AF.Silu,
                                                bias=cb[:, c:c + 1], scale=1.0))
                            mk[("usil", c)] = sA.c

                    @block.vector
                    def _(e, i=i):
                        w(e, sA, sT)
                        sV.inc(e.tensor_copy(out=ut16[:], in_=ut[:]))

                    @block.tensor
                    def _(e, i=i):
                        w(e, sV, sA)
                        for sub in range(MT // 512):
                            for c in range(NCH):
                                ins = e.matmul(
                                    psm[:, :], wx16[:, c, :],
                                    ut16[:, c, sub * 512:(sub + 1) * 512],
                                    start=(c == 0), stop=(c == NCH - 1))
                            sT.inc(ins)

                    @block.scalar
                    def _(e, i=i):
                        base_t = sT.c - MT // 512
                        for sub in range(MT // 512):
                            e.wait_ge(sT.h, base_t + sub + 1)
                            sA.inc(e.copy(
                                out=ssma[:, i * MT + sub * 512:
                                         i * MT + (sub + 1) * 512],
                                in_=psm[:, :]))

                    @block.gpsimd
                    def _(e, i=i):
                        w(e, sA, sV)
                        for c in range(NCH):
                            qG.inc(e.dma_start(
                                out=zsil_d.ap()[c, :, i * MT:(i + 1) * MT],
                                in_=zst[:, c, :]), 16)
                            qG.inc(e.dma_start(
                                out=u_d.ap()[c, :, i * MT:(i + 1) * MT],
                                in_=ut[:, c, :]), 16)

            esMt.close()
            with (
                nc.sbuf_tensor("stmp", [33, T], F32) as stmp,
                nc.sbuf_tensor("ssmb16", [33, T], BF16) as ssmb16,
            ):
                @block.gpsimd
                def _(e):
                    w(e, sA, qG)
                    rpoll = rpoll_box["r"]

                    def arm(me):
                        pt = 1 - me
                        e.dma_start(out=ssm_sh.ap()[me, :, :],
                                    in_=ssma[:]).then_inc(_qG, 16)
                        e.wait_ge(_qG, qG.c + 16)
                        e.dma_start(out=flag_sh.ap()[me, 0, :],
                                    in_=snonce[:]).then_inc(_qG, 16)
                        e.wait_ge(_qG, qG.c + 32)
                        poll_partner(e, rpoll, None,
                                     flag_sh.ap()[pt, 0, :],
                                     sflag, snonce, _pp)
                        e.reg_add(rpoll, rpoll, 16)
                        e.dma_start(out=stmp[:],
                                    in_=ssm_sh.ap()[pt, :, :]).then_inc(_pp, 16)
                        e.wait_ge(_pp, e.snap(rpoll))

                    slot_branch(e, sslot, lambda: arm(0), lambda: arm(1))
                    qG.c += 32
                    pm.inc(e.nop())

                @block.vector
                def _(e):
                    e.wait_ge(pm.h, pm.c)
                    sV.inc(e.tensor_add(out=ssma[:], in0=ssma[:], in1=stmp[:]))
                    e.wait_ge(sV.h, sV.c)
                    sV.inc(e.tensor_copy(out=ssmb16[:], in_=ssma[:]))

                @block.scalar
                def _(e):
                    w(e, sV)
                    sA.inc(e.dma_start(out=ssmb_d.ap(), in_=ssmb16[:]), 16)
                    sA.inc(e.dma_start(out=ssm0_d.ap(), in_=ssma[0:1, :]), 16)
                    if dbg:
                        sA.inc(e.dma_start(out=dbg_ssm, in_=ssma[:]), 16)

        esM.close()
        if stop == "x":
            @block.gpsimd
            def _(e):
                w(e, sA, sV)
                e.wait_ge(qG.h, qG.c)
            es.close()
            return nc
        esS = ExitStack()
        _e = esS.enter_context
        bcB = _e(nc.sbuf_tensor("bcB", [128, DS, ST], BF16))
        bcC = _e(nc.sbuf_tensor("bcC", [128, DS, ST], BF16))
        bcdt = _e(nc.sbuf_tensor("bcdt", [128, ST], F32))
        ustr = _e(nc.sbuf_tensor("ustr", [128, NCH, ST], F32))
        delta = _e(nc.sbuf_tensor("delta", [128, ST], F32))
        du = _e(nc.sbuf_tensor("du", [128, ST], F32))
        spr = _e(nc.sbuf_tensor("spr", [128, ST], F32))
        spl = _e(nc.sbuf_tensor("spl", [128, ST], F32))
        spl2 = _e(nc.sbuf_tensor("spl2", [128, ST], F32))
        dAb = _e(nc.sbuf_tensor("dAb", [128, 2, ST], F32))
        dBub = _e(nc.sbuf_tensor("dBub", [128, 2, ST], F32))
        hb = _e(nc.sbuf_tensor("hb", [128, 2, ST], F32))
        pr16 = _e(nc.sbuf_tensor("pr16", [128, 2, ST], BF16))
        y2s = _e(nc.sbuf_tensor("y2s", [128, 2, ST], F32))
        psy = _e(nc.psum_tensor("psy", [128, ST // 512, 512], F32))
        if True:
            import os as _osC
            CL = int(_osC.environ.get("CLIM", "0")) or NCH
            NYB = ST // 512
            sv_scan, sv_prod, sv_du, sv_y2 = {}, {}, {}, {}
            sa_dA, sa_carry, sa_mark = {}, {}, {}
            st_ymm, qg_y2, ql_strip = {}, {}, {}

            pv, pa, pt_, pg, pl = sV.c, sA.c, sT.c, qG.c, qL.c
            for si in range(NST):
                pl += 16 * (1 + 2 * DS + NCH)
                ql_strip[si] = pl
                for c in range(CL):
                    pa += 1
                    sa_mark[(si, c)] = pa
                    pv += 2
                    sv_du[(si, c)] = pv
                    for s in range(DS):
                        pa += 1
                        sa_dA[(si, c, s)] = pa
                        pv += 1
                        pv += 1
                        sv_scan[(si, c, s)] = pv
                        pv += 1
                        sv_prod[(si, c, s)] = pv
                        pa += 1
                        sa_carry[(si, c, s)] = pa
                        pt_ += NYB
                        st_ymm[(si, c, s)] = pt_
                    pv += 1
                    sv_y2[(si, c)] = pv
                    pg += 16
                    qg_y2[(si, c)] = pg

            for si in range(NST):
                t0 = si * ST

                @block.sync
                def _(e, si=si, t0=t0):
                    if si > 0:
                        e.wait_ge(sV.h, sv_y2[(si - 1, CL - 1)])
                        e.wait_ge(sA.h, sa_carry[(si - 1, CL - 1, DS - 1)])
                    else:
                        w(e, sA, sV)
                    qL.inc(e.dma_start(
                        out=bcdt[:],
                        in_=ssm0_d.ap()[:, t0:t0 + ST]
                            .partition_broadcast(128)), 16)
                    for s in range(DS):
                        qL.inc(e.dma_start(
                            out=bcB[:, s, :],
                            in_=ssmb_d.ap()[1 + s:2 + s, t0:t0 + ST]
                                .partition_broadcast(128)), 16)
                        qL.inc(e.dma_start(
                            out=bcC[:, s, :],
                            in_=ssmb_d.ap()[17 + s:18 + s, t0:t0 + ST]
                                .partition_broadcast(128)), 16)
                    for c in range(NCH):
                        qL.inc(e.dma_start(
                            out=ustr[:, c, :],
                            in_=u_d.ap()[c, :, t0:t0 + ST]), 16)
                    assert qL.c == ql_strip[si]

                @block.scalar
                def _(e, si=si):
                    e.wait_ge(qL.h, ql_strip[si])
                    for c in range(CL):
                        if c >= 1:
                            e.wait_ge(sV.h, sv_du[(si, c - 1)])
                        elif si >= 1:
                            e.wait_ge(sV.h, sv_du[(si - 1, CL - 1)])
                        e.activation(out=spr[:], in_=bcdt[:], func=AF.Relu,
                                     scale=wdtc[:, c:c + 1],
                                     bias=bdtc[:, c:c + 1])
                        e.activation(out=spl[:], in_=bcdt[:], func=AF.Abs,
                                     scale=wdtc[:, c:c + 1],
                                     bias=bdtc[:, c:c + 1])
                        e.activation(out=spl2[:], in_=spl[:], func=AF.Exp,
                                     scale=-1.0)
                        sA.inc(e.activation(out=spl[:], in_=spl2[:], func=AF.Ln,
                                            bias=1.0))
                        assert sA.c == sa_mark[(si, c)]
                        for s in range(DS):
                            if s >= 2:
                                e.wait_ge(sV.h, sv_scan[(si, c, s - 2)])
                            elif c >= 1:
                                e.wait_ge(sV.h, sv_scan[(si, c - 1, DS - 2 + s)])
                            elif si >= 1:
                                e.wait_ge(sV.h,
                                          sv_scan[(si - 1, CL - 1, DS - 2 + s)])
                            e.wait_ge(sV.h, sv_du[(si, c)])
                            sA.inc(e.activation(
                                out=dAb[:, s % 2, :], in_=delta[:], func=AF.Exp,
                                scale=ac[:, c * DS + s:c * DS + s + 1]))
                            assert sA.c == sa_dA[(si, c, s)]
                            e.wait_ge(sV.h, sv_scan[(si, c, s)])
                            sA.inc(e.copy(
                                out=hcarry[:, c * DS + s:c * DS + s + 1],
                                in_=hb[:, s % 2, ST - 1:ST]))
                            assert sA.c == sa_carry[(si, c, s)]

                @block.vector
                def _(e, si=si):
                    for c in range(CL):
                        e.wait_ge(sA.h, sa_mark[(si, c)])
                        sV.inc(e.tensor_add(out=delta[:], in0=spr[:],
                                            in1=spl[:]))
                        sV.inc(e.tensor_mul(out=du[:], in0=delta[:],
                                            in1=ustr[:, c, :]))
                        assert sV.c == sv_du[(si, c)]
                        for s in range(DS):
                            col = c * DS + s
                            sV.inc(e.tensor_tensor(out=dBub[:, s % 2, :],
                                                   in0=du[:],
                                                   in1=bcB[:, s, :],
                                                   op=ALU.mult))
                            e.wait_ge(sA.h, sa_dA[(si, c, s)])
                            if si == 0:
                                init = 0.0
                            else:
                                e.wait_ge(sA.h, sa_carry[(si - 1, c, s)])
                                init = hcarry[:, col:col + 1]
                            if s >= 2:
                                e.wait_ge(sT.h, st_ymm[(si, c, s - 2)])
                            elif c >= 1:
                                e.wait_ge(sT.h, st_ymm[(si, c - 1, DS - 2 + s)])
                            elif si >= 1:
                                e.wait_ge(sT.h,
                                          st_ymm[(si - 1, CL - 1, DS - 2 + s)])
                            sV.inc(e.tensor_tensor_scan(
                                out=hb[:, s % 2, :], data0=dAb[:, s % 2, :],
                                data1=dBub[:, s % 2, :], initial=init,
                                op0=ALU.mult, op1=ALU.add))
                            assert sV.c == sv_scan[(si, c, s)]
                            sV.inc(e.tensor_tensor(
                                out=pr16[:, s % 2, :], in0=hb[:, s % 2, :],
                                in1=bcC[:, s, :], op=ALU.mult))
                            assert sV.c == sv_prod[(si, c, s)]
                        e.wait_ge(sT.h, st_ymm[(si, c, DS - 1)])
                        if c >= 2:
                            e.wait_ge(qG.h, qg_y2[(si, c - 2)])
                        elif si >= 1 and CL - 2 + c >= 0:
                            e.wait_ge(qG.h, qg_y2[(si - 1, CL - 2 + c)])
                        sV.inc(e.scalar_tensor_tensor(
                            out=y2s[:, c % 2, :], in0=ustr[:, c, :],
                            scalar=dpc[:, c:c + 1],
                            in1=psy[:].rearrange("p a b -> p (a b)"),
                            op0=ALU.mult, op1=ALU.add))
                        assert sV.c == sv_y2[(si, c)]

                @block.tensor
                def _(e, si=si):
                    for c in range(CL):
                        if c >= 1:
                            e.wait_ge(sV.h, sv_y2[(si, c - 1)])
                        elif si >= 1:
                            e.wait_ge(sV.h, sv_y2[(si - 1, CL - 1)])
                        for s in range(DS):
                            e.wait_ge(sV.h, sv_prod[(si, c, s)])
                            for yb in range(NYB):
                                ins = e.matmul(
                                    psy[:, yb, :], id16[:],
                                    pr16[:, s % 2, yb * 512:(yb + 1) * 512],
                                    start=(s == 0), stop=(s == DS - 1))
                            sT.inc(ins, NYB)
                            assert sT.c == st_ymm[(si, c, s)]

                @block.gpsimd
                def _(e, si=si, t0=t0):
                    for c in range(CL):
                        e.wait_ge(sV.h, sv_y2[(si, c)])
                        qG.inc(e.dma_start(
                            out=y2_d.ap()[c, :, t0:t0 + ST],
                            in_=y2s[:, c % 2, :]), 16)
                        assert qG.c == qg_y2[(si, c)]

        if stop == "s":
            @block.gpsimd
            def _(e):
                w(e, sA, sV, sT)
                e.wait_ge(qG.h, qG.c)
            esS.close()
            es.close()
            return nc
        esS.close()
        esW = ExitStack()
        _e = esW.enter_context
        wo16 = _e(nc.sbuf_tensor("wo16", [128, NCH, DM], BF16))
        ywt = _e(nc.sbuf_tensor("ywt", [128, NCH, WST], F32))
        zwt = _e(nc.sbuf_tensor("zwt", [128, NCH, WST], F32))
        yg16 = _e(nc.sbuf_tensor("yg16", [128, NCH, WST], BF16))
        mixt = _e(nc.sbuf_tensor("mixt", [128, 2, 512], F32))
        psw = _e(nc.psum_tensor("psw", [128, 2, 512], F32))
        if True:
            with nc.sbuf_tensor("wotmp", [128, NCH, DM], F32) as wotmp:
                @block.sync
                def _(e):
                    w(e, sV, sA, sT)
                    qL.inc(e.dma_start(
                        out=wotmp[:],
                        in_=woutT.rearrange("(k p) n -> p k n", p=128)), 16)

                @block.vector
                def _(e):
                    w(e, qL)
                    sV.inc(e.tensor_copy(out=wo16[:], in_=wotmp[:]))

            for si in range(NWS):
                t0 = si * WST
                NW = 2 * (WST // 128)

                @block.sync
                def _(e, si=si, t0=t0):
                    w(e, sV, sT)
                    for c in range(NCH):
                        qL.inc(e.dma_start(out=ywt[:, c, :],
                                           in_=y2_d.ap()[c, :, t0:t0 + WST]),
                               16)
                        qL.inc(e.dma_start(out=zwt[:, c, :],
                                           in_=zsil_d.ap()[c, :, t0:t0 + WST]),
                               16)

                @block.vector
                def _(e, si=si):
                    w(e, qL)
                    for c in range(NCH):
                        ins = e.tensor_mul(out=yg16[:, c, :], in0=ywt[:, c, :],
                                           in1=zwt[:, c, :])
                    sV.inc(ins)

                @block.tensor
                def _(e, si=si):
                    w(e, sV)
                    saw = sA.c
                    for j in range(NW):
                        tsub, mh = j // 2, j % 2
                        if j >= 2:
                            e.wait_ge(sA.h, saw + j - 1)
                        for c in range(NCH):
                            ins = e.matmul(
                                psw[:, j % 2, :],
                                yg16[:, c, tsub * 128:(tsub + 1) * 128],
                                wo16[:, c, mh * 512:(mh + 1) * 512],
                                start=(c == 0), stop=(c == NCH - 1))
                        sT.inc(ins)

                @block.scalar
                def _(e, si=si):
                    base_t = sT.c - NW
                    qgw = qG.c
                    for j in range(NW):
                        e.wait_ge(sT.h, base_t + j + 1)
                        if j >= 2:
                            e.wait_ge(qG.h, qgw + 16 * (j - 1))
                        sA.inc(e.copy(out=mixt[:, j % 2, :],
                                      in_=psw[:, j % 2, :]))

                @block.gpsimd
                def _(e, si=si, t0=t0):
                    base_a = sA.c - NW

                    def arm(me):
                        for j in range(NW):
                            tsub, mh = j // 2, j % 2
                            e.wait_ge(sA.h, base_a + j + 1)
                            row = me * T + t0 + tsub * 128
                            e.dma_start(
                                out=mix_sh.ap()[row:row + 128,
                                                mh * 512:(mh + 1) * 512],
                                in_=mixt[:, j % 2, :]).then_inc(_qG, 16)

                    slot_branch(e, sslot, lambda: arm(0), lambda: arm(1))
                    qG.c += 16 * NW

            @block.gpsimd
            def _(e):
                w(e, qG)
                rpoll = rpoll_box["r"]

                def arm(me):
                    pt = 1 - me
                    e.dma_start(out=flag_sh.ap()[me, 1, :],
                                in_=snonce[:]).then_inc(_qG, 16)
                    e.wait_ge(_qG, qG.c + 16)
                    poll_partner(e, rpoll, None,
                                 flag_sh.ap()[pt, 1, :],
                                 sflag, snonce, _pp)

                slot_branch(e, sslot, lambda: arm(0), lambda: arm(1))
                qG.c += 16
                pm.inc(e.nop())

        if stop == "w":
            @block.gpsimd
            def _(e):
                w(e, sA, sV, sT)
                e.wait_ge(qG.h, qG.c)
                e.wait_ge(_pm, pm.c)
            esW.close()
            es.close()
            return nc
        esW.close()
        esR = ExitStack()
        _e = esR.enter_context
        rw1s = _e(nc.sbuf_tensor("rw1s", [128, 8, HF], BF16))
        rb1s = _e(nc.sbuf_tensor("rb1s", [128, HF // 128], F32))
        rw2s = _e(nc.sbuf_tensor("rw2s", [128, 32, NE], BF16))
        rb2s = _e(nc.sbuf_tensor("rb2s", [NE, 1], F32))
        lnA = _e(nc.sbuf_tensor("lnA", [128, 2, DM], F32))
        lnB = _e(nc.sbuf_tensor("lnB", [128, 2, DM], F32))
        lnc = _e(nc.sbuf_tensor("lnc", [128, 8], F32))
        pdif = _e(nc.sbuf_tensor("pdif", [128, 4], F32))
        pexp = _e(nc.sbuf_tensor("pexp", [128, 4], F32))
        x1b = _e(nc.sbuf_tensor("x1b", [128, 4, DM], BF16))
        x1T = _e(nc.sbuf_tensor("x1T", [128, 8, 512], BF16))
        h1b = _e(nc.sbuf_tensor("h1b", [128, 32, 512], BF16))
        lgct = _e(nc.sbuf_tensor("lgct", [NE, 512], F32))
        lgtm = _e(nc.sbuf_tensor("lgtm", [128, 4, NE], F32))
        tka = _e(nc.sbuf_tensor("tka", [128, NE], F32))
        tkb = _e(nc.sbuf_tensor("tkb", [128, NE], F32))
        ivt = _e(nc.sbuf_tensor("ivt", [128, 4, 4], F32))
        pst = _e(nc.psum_tensor("pst", [128, 2, 1024], BF16))
        psh = _e(nc.psum_tensor("psh", [128, 2, 512], F32))
        pslg = _e(nc.psum_tensor("pslg", [NE, 512], F32))
        pstf = _e(nc.psum_tensor("pstf", [128, 2, 512], F32))
        if True:
            with nc.sbuf_tensor("rwtmp", [128, HF], F32) as rwtmp:
                for k in range(8):
                    @block.sync
                    def _(e, k=k):
                        if k == 0:
                            e.wait_ge(pm.h, pm.c)
                            w(e, sV, sA, sT)
                        else:
                            w(e, sV)
                        qL.inc(e.dma_start(
                            out=rwtmp[:],
                            in_=rw1.rearrange("(k p) h -> p k h",
                                              p=128)[:, k, :]), 16)

                    @block.vector
                    def _(e, k=k):
                        w(e, qL)
                        sV.inc(e.tensor_copy(out=rw1s[:, k, :], in_=rwtmp[:]))

                @block.sync
                def _(e):
                    w(e, sV)
                    qL.inc(e.dma_start(
                        out=rwtmp[:, 0:32 * NE].rearrange(
                            "p (k n) -> p k n", k=32),
                        in_=rw2.rearrange("(k p) n -> p k n", p=128)), 16)
                    qL.inc(e.dma_start(out=rb1s[:], in_=rb1v), 16)
                    qL.inc(e.dma_start(out=rb2s[:], in_=rb2v), 16)

                @block.vector
                def _(e):
                    w(e, qL)
                    sV.inc(e.tensor_copy(
                        out=rw2s[:],
                        in_=rwtmp[:, 0:32 * NE]
                            .rearrange("p (k n) -> p k n", k=32)))

            for g in range(NRG):
                for tt in range(4):
                    tok0 = g * 512 + tt * 128

                    @block.sync
                    def _(e, tok0=tok0):
                        w(e, sV, qG, sT)

                        def arm(me):
                            pt = 1 - me
                            r0 = me * T + me * TH + tok0
                            r1 = pt * T + me * TH + tok0
                            e.dma_start(out=lnA[:, 0, :],
                                        in_=mix_sh.ap()[r0:r0 + 128, :]
                                        ).then_inc(_qL, 16)
                            e.dma_start(out=lnA[:, 1, :],
                                        in_=mix_sh.ap()[r1:r1 + 128, :]
                                        ).then_inc(_qL, 16)

                        slot_branch(e, sslot, lambda: arm(0), lambda: arm(1))
                        qL.c += 32
                        qL.inc(e.dma_start(
                            out=lnB[:, 0, :],
                            in_=x_tm[tok0:tok0 + 128, :]), 16)

                    @block.vector
                    def _(e):
                        w(e, qL)
                        sV.inc(e.tensor_add(out=lnA[:, 0, :],
                                            in0=lnA[:, 0, :],
                                            in1=lnA[:, 1, :]))
                        e.wait_ge(sV.h, sV.c)
                        sV.inc(e.tensor_add(out=lnA[:, 0, :],
                                            in0=lnA[:, 0, :],
                                            in1=lnB[:, 0, :]))
                        e.wait_ge(sV.h, sV.c)
                        sV.inc(e.reduce_sum(out=lnc[:, 0:1], in_=lnA[:, 0, :],
                                            axis=AX))
                        e.wait_ge(sV.h, sV.c)
                        sV.inc(e.tensor_scalar(out=lnc[:, 1:2], in0=lnc[:, 0:1],
                                               scalar1=1.0 / DM, scalar2=None,
                                               op0=ALU.mult))
                        e.wait_ge(sV.h, sV.c)
                        sV.inc(e.tensor_scalar(out=lnB[:, 1, :],
                                               in0=lnA[:, 0, :],
                                               scalar1=lnc[:, 1:2],
                                               scalar2=None,
                                               op0=ALU.subtract))

                    @block.scalar
                    def _(e):
                        w(e, sV)
                        sA.inc(e.activation(out=lnA[:, 1, :], in_=lnB[:, 1, :],
                                            func=AF.Square))

                    @block.vector
                    def _(e):
                        w(e, sA)
                        sV.inc(e.reduce_sum(out=lnc[:, 2:3], in_=lnA[:, 1, :],
                                            axis=AX))
                        e.wait_ge(sV.h, sV.c)
                        sV.inc(e.tensor_scalar(out=lnc[:, 3:4], in0=lnc[:, 2:3],
                                               scalar1=1.0 / DM, scalar2=None,
                                               op0=ALU.mult))

                    @block.scalar
                    def _(e):
                        w(e, sV)
                        sA.inc(e.activation(out=lnc[:, 4:5], in_=lnc[:, 3:4],
                                            func=AF.Ln, bias=lncst[:, 0:1]))
                        e.wait_ge(sA.h, sA.c)
                        sA.inc(e.activation(out=lnc[:, 5:6], in_=lnc[:, 4:5],
                                            func=AF.Exp, scale=lncst[:, 1:2]))

                    @block.vector
                    def _(e, tt=tt):
                        w(e, sA)
                        sV.inc(e.tensor_scalar(out=lnA[:, 0, :],
                                               in0=lnB[:, 1, :],
                                               scalar1=lnc[:, 5:6],
                                               scalar2=None,
                                               op0=ALU.mult))
                        e.wait_ge(sV.h, sV.c)
                        sV.inc(e.tensor_copy(out=x1b[:, tt, :],
                                             in_=lnA[:, 0, :]))

                    @block.gpsimd
                    def _(e, tok0=tok0, g=g, tt=tt):
                        w(e, sV)
                        qG.inc(e.dma_start(out=x1out[tok0:tok0 + 128, :],
                                           in_=lnA[:, 0, :]), 16)
                        if dbg and g == 0 and tt == 0:
                            qG.inc(e.dma_start(out=dbg_ln[:, 0:8],
                                               in_=lnc[:]), 16)
                            qG.inc(e.dma_start(out=dbg_ln[:, 8:8 + DM],
                                               in_=lnB[:, 1, :]), 16)
                            qG.inc(e.dma_start(out=dbg_ln[:, 8 + DM:],
                                               in_=lnA[:, 1, :]), 16)

                @block.tensor
                def _(e, g=g):
                    w(e, sV)
                    svw = sV.c
                    for j in range(32):
                        tt, mc = j // 8, j % 8
                        if j >= 2:
                            e.wait_ge(sV.h, svw + j - 1)
                        sT.inc(e.transpose(pst[:, j % 2, 0:128],
                                           x1b[:, tt, mc * 128:(mc + 1) * 128],
                                           id16[:]))

                @block.vector
                def _(e, g=g):
                    base_t = sT.c - 32
                    for j in range(32):
                        tt, mc = j // 8, j % 8
                        e.wait_ge(sT.h, base_t + j + 1)
                        sV.inc(e.tensor_copy(
                            out=x1T[:, mc, tt * 128:(tt + 1) * 128],
                            in_=pst[:, j % 2, 0:128]))

                @block.tensor
                def _(e, g=g):
                    w(e, sV)
                    saw = sA.c
                    for hc in range(32):
                        if hc >= 2:
                            e.wait_ge(sA.h, saw + hc - 1)
                        for k in range(8):
                            ins = e.matmul(psh[:, hc % 2, :],
                                           rw1s[:, k, hc * 128:(hc + 1) * 128],
                                           x1T[:, k, :],
                                           start=(k == 0), stop=(k == 7))
                        sT.inc(ins)

                @block.scalar
                def _(e, g=g):
                    base_t = sT.c - 32
                    for hc in range(32):
                        e.wait_ge(sT.h, base_t + hc + 1)
                        sA.inc(e.activation(out=h1b[:, hc, :],
                                            in_=psh[:, hc % 2, :],
                                            func=AF.Relu,
                                            bias=rb1s[:, hc:hc + 1]))

                @block.tensor
                def _(e, g=g):
                    base_a = sA.c - 32
                    for k2 in range(32):
                        e.wait_ge(sA.h, base_a + k2 + 1)
                        ins = e.matmul(pslg[:], rw2s[:, k2, :], h1b[:, k2, :],
                                       start=(k2 == 0), stop=(k2 == 31))
                    sT.inc(ins)

                @block.vector
                def _(e, g=g):
                    w(e, sT)
                    sV.inc(e.tensor_scalar(out=lgct[:], in0=pslg[:],
                                           scalar1=rb2s[:, 0:1], scalar2=None,
                                           op0=ALU.add))

                @block.tensor
                def _(e, g=g):
                    w(e, sV)
                    svw = sV.c
                    for tt in range(4):
                        if tt >= 2:
                            e.wait_ge(sV.h, svw + tt - 1)
                        sT.inc(e.transpose(pstf[:, tt % 2, 0:NE],
                                           lgct[:, tt * 128:(tt + 1) * 128],
                                           idf[0:NE, 0:NE]))

                @block.vector
                def _(e, g=g):
                    base_t = sT.c - 4
                    for tt in range(4):
                        e.wait_ge(sT.h, base_t + tt + 1)
                        sV.inc(e.tensor_copy(out=lgtm[:, tt, :],
                                             in_=pstf[:, tt % 2, 0:NE]))

                @block.vector
                def _(e, g=g):
                    def b(ins):
                        sV.inc(ins)
                        e.wait_ge(sV.h, sV.c)
                    for tt in range(4):
                        lg = lgtm[:, tt, :]
                        b(e.reduce_max(out=lnc[:, 0:1], in_=lg, axis=AX))
                        b(e.tensor_scalar(out=tka[:], in0=lg,
                                          scalar1=lnc[:, 0:1],
                                          scalar2=None, op0=ALU.is_ge))
                        b(e.scalar_tensor_tensor(out=tka[:], in0=tka[:],
                                                 scalar=-1024.0, in1=io8b[:],
                                                 op0=ALU.mult, op1=ALU.add))
                        b(e.tensor_reduce(out=lnc[:, 1:2], in_=tka[:], axis=AX,
                                          op=ALU.min))
                        b(e.tensor_scalar(out=ivt[:, tt, 0:1], in0=lnc[:, 1:2],
                                          scalar1=1024.0, scalar2=None,
                                          op0=ALU.add))
                        b(e.tensor_scalar(out=tka[:], in0=io8b[:],
                                          scalar1=ivt[:, tt, 0:1], scalar2=None,
                                          op0=ALU.is_equal))
                        b(e.scalar_tensor_tensor(out=tkb[:], in0=tka[:],
                                                 scalar=-2e30, in1=lg,
                                                 op0=ALU.mult, op1=ALU.add))
                        b(e.reduce_max(out=lnc[:, 2:3], in_=tkb[:], axis=AX))
                        b(e.tensor_scalar(out=tka[:], in0=tkb[:],
                                          scalar1=lnc[:, 2:3], scalar2=None,
                                          op0=ALU.is_ge))
                        b(e.scalar_tensor_tensor(out=tka[:], in0=tka[:],
                                                 scalar=-1024.0, in1=io8b[:],
                                                 op0=ALU.mult, op1=ALU.add))
                        b(e.tensor_reduce(out=lnc[:, 3:4], in_=tka[:], axis=AX,
                                          op=ALU.min))
                        b(e.tensor_scalar(out=ivt[:, tt, 1:2], in0=lnc[:, 3:4],
                                          scalar1=1024.0, scalar2=None,
                                          op0=ALU.add))
                        b(e.tensor_sub(out=pdif[:, tt:tt + 1],
                                       in0=lnc[:, 2:3], in1=lnc[:, 0:1]))

                @block.scalar
                def _(e, g=g):
                    w(e, sV)
                    sA.inc(e.activation(out=pexp[:], in_=pdif[:], func=AF.Exp))

                @block.vector
                def _(e, g=g):
                    w(e, sA)
                    sV.inc(e.tensor_scalar(out=tka[:, 0:4], in0=pexp[:],
                                           scalar1=1.0, scalar2=None,
                                           op0=ALU.add))
                    e.wait_ge(sV.h, sV.c)
                    sV.inc(e.reciprocal(out=pdif[:], in_=tka[:, 0:4]))
                    e.wait_ge(sV.h, sV.c)
                    for tt in range(4):
                        e.tensor_copy(out=ivt[:, tt, 2:3],
                                      in_=pdif[:, tt:tt + 1])
                        ins = e.tensor_mul(out=ivt[:, tt, 3:4],
                                           in0=pexp[:, tt:tt + 1],
                                           in1=pdif[:, tt:tt + 1])
                    sV.inc(ins)

                @block.gpsimd
                def _(e, g=g):
                    w(e, sV)
                    for tt in range(4):
                        tok0 = g * 512 + tt * 128
                        qG.inc(e.dma_start(out=ivout[tok0:tok0 + 128, :],
                                           in_=ivt[:, tt, :]), 16)

            if dbg:
                @block.sync
                def _(e):
                    w(e, qG, sA, sV)
                    qL.inc(e.dma_start(out=dbg_u, in_=u_d.ap()), 16)
                    qL.inc(e.dma_start(out=dbg_z, in_=zsil_d.ap()), 16)
                    qL.inc(e.dma_start(out=dbg_y2, in_=y2_d.ap()), 16)
                    qL.inc(e.dma_start(out=dbg_mix, in_=mix_sh.ap()), 16)
                    e.wait_ge(qL.h, qL.c)

            @block.gpsimd
            def _(e):
                e.wait_ge(qG.h, qG.c)

        esR.close()
    es.close()
    return nc


def build_L2(N=32768, stop='full'):
    NK = 2 * N
    NSTRIP = 16
    SW = NK // NSTRIP
    nc = bass.Bass("TRN2", target_bir_lowering=False, debug=False,
                   detect_race_conditions=False)

    ein = lambda n, s, d=F32: nc.dram_tensor(n, s, d, kind="ExternalInput").ap()
    x1f = ein("x1f", [N, DM])
    idsf = ein("idsf", [1, NK])
    tokv = ein("tokv", [1, NK], I32)
    ew1in = ein("ew1in", [DM, HF])
    ew2in = ein("ew2in", [HF, DM])
    eb1v = ein("eb1v", [128, 32])
    eb2v = ein("eb2v", [1, DM])
    emodv = ein("emodv", [128, 1])
    spatv = ein("spatv", [128, 128])
    selv = ein("selv", [128, 16])
    identv = ein("identv", [128, 128])

    eoout = nc.dram_tensor("eoout", [CAP, DM], F32, kind="ExternalOutput").ap()

    slot_d = nc.dram_tensor("slot_d", [NK, 1], I32)
    src_d = nc.dram_tensor("src_d", [CAP + 128, 1], I32)
    disp_d = nc.dram_tensor("disp_d", [CAP, DM], BF16)
    ew1b_d = nc.dram_tensor("ew1b_d", [8, 128, HF], BF16)
    ew2b_d = nc.dram_tensor("ew2b_d", [32, 128, DM], BF16)

    es = ExitStack()
    _e = es.enter_context
    emod = _e(nc.sbuf_tensor("emod", [128, 1], F32))
    spat = _e(nc.sbuf_tensor("spat", [128, 128], F32))
    sel = _e(nc.sbuf_tensor("sel", [128, 16], F32))
    id16 = _e(nc.sbuf_tensor("id16", [128, 128], BF16))
    idf = _e(nc.sbuf_tensor("idf", [128, 128], F32))
    eb1s = _e(nc.sbuf_tensor("eb1s", [128, 32], F32))
    eb2b = _e(nc.sbuf_tensor("eb2b", [128, DM], F32))
    _qL = _e(nc.semaphore("qL")); _qG = _e(nc.semaphore("qG"))
    _sV = _e(nc.semaphore("sV")); _sA = _e(nc.semaphore("sA"))
    _sT = _e(nc.semaphore("sT"))
    block = _e(nc.Block())
    if True:
        qL, qG, sV, sA, sT = Sem(_qL), Sem(_qG), Sem(_sV), Sem(_sA), Sem(_sT)
        rbnd_box = {}

        @block.gpsimd
        def _(e):
            r1 = e.alloc_register(rn("rbndcap"))
            e.reg_mov(r1, CAP - 1)
            r2 = e.alloc_register(rn("rbndn"))
            e.reg_mov(r2, N - 1)
            rbnd_box["cap"] = r1
            rbnd_box["n"] = r2

        @block.sync
        def _(e):
            for t_, src in ((emod, emodv), (spat, spatv), (sel, selv),
                            (idf, identv), (eb1s, eb1v)):
                qL.inc(e.dma_start(out=t_[:], in_=src), 16)
            qL.inc(e.dma_start(out=eb2b[:],
                               in_=eb2v.partition_broadcast(128)), 16)

        @block.vector
        def _(e):
            w(e, qL)
            sV.inc(e.tensor_copy(out=id16[:], in_=idf[:]))

        with nc.sbuf_tensor("wt32", [128, HF], F32) as wt32, \
             nc.sbuf_tensor("wt16", [128, HF], BF16) as wt16:
            for k in range(8):
                @block.sync
                def _(e, k=k):
                    w(e, sV, qG)
                    qL.inc(e.dma_start(
                        out=wt32[:],
                        in_=ew1in.rearrange("(k p) h -> p k h", p=128)[:, k, :]),
                        16)

                @block.vector
                def _(e, k=k):
                    w(e, qL)
                    sV.inc(e.tensor_copy(out=wt16[:], in_=wt32[:]))

                @block.gpsimd
                def _(e, k=k):
                    w(e, sV)
                    qG.inc(e.dma_start(out=ew1b_d.ap()[k, :, :], in_=wt16[:]),
                           16)

            for k2 in range(32):
                @block.sync
                def _(e, k2=k2):
                    w(e, sV, qG)
                    qL.inc(e.dma_start(
                        out=wt32[:, 0:DM],
                        in_=ew2in.rearrange("(k p) m -> p k m",
                                            p=128)[:, k2, :]), 16)

                @block.vector
                def _(e, k2=k2):
                    w(e, qL)
                    sV.inc(e.tensor_copy(out=wt16[:, 0:DM], in_=wt32[:, 0:DM]))

                @block.gpsimd
                def _(e, k2=k2):
                    w(e, sV)
                    qG.inc(e.dma_start(out=ew2b_d.ap()[k2, :, :],
                                       in_=wt16[:, 0:DM]), 16)

        esr = ExitStack()
        _e2 = esr.enter_context
        idt = _e2(nc.sbuf_tensor("idt", [128, SW], F32))
        ones = _e2(nc.sbuf_tensor("ones", [128, SW], F32))
        oneh = _e2(nc.sbuf_tensor("oneh", [128, SW], F32))
        cums = _e2(nc.sbuf_tensor("cums", [128, SW], F32))
        ocol = _e2(nc.sbuf_tensor("ocol", [128, 1], F32))
        mpos = _e2(nc.sbuf_tensor("mpos", [16, SW], F32))
        mhot = _e2(nc.sbuf_tensor("mhot", [16, SW], F32))
        slotf = _e2(nc.sbuf_tensor("slotf", [16, SW], F32))
        sloti = _e2(nc.sbuf_tensor("sloti", [16, SW], I32))
        ztile = _e2(nc.sbuf_tensor("ztile", [1, CAP + 128], I32))
        pso = _e2(nc.psum_tensor("pso", [128, 1], F32))
        psel = _e2(nc.psum_tensor("psel", [16, 512], F32))
        if True:
            @block.sync
            def _(e):
                w(e, sV, qG)
                for st in range(NSTRIP):
                    qL.inc(e.dma_start(
                        out=idt[st * 8:(st + 1) * 8, :],
                        in_=idsf[:, st * SW:(st + 1) * SW]
                            .partition_broadcast(8)), 16)

            @block.vector
            def _(e):
                w(e, qL)
                sV.inc(e.memset(ones[:], 1.0))
                e.wait_ge(sV.h, sV.c)
                sV.inc(e.tensor_scalar(out=oneh[:], in0=idt[:],
                                       scalar1=emod[:, 0:1],
                                       scalar2=None, op0=ALU.is_equal))
                e.wait_ge(sV.h, sV.c)
                sV.inc(e.tensor_tensor_scan(out=cums[:], data0=ones[:],
                                            data1=oneh[:], initial=0.0,
                                            op0=ALU.mult, op1=ALU.add))

            @block.tensor
            def _(e):
                w(e, sV)
                sT.inc(e.matmul(pso[:], spat[:], cums[:, SW - 1:SW],
                                start=True, stop=True))

            @block.vector
            def _(e):
                w(e, sT)
                sV.inc(e.tensor_copy(out=ocol[:], in_=pso[:]))
                e.wait_ge(sV.h, sV.c)
                sV.inc(e.tensor_scalar(out=cums[:], in0=cums[:],
                                       scalar1=ocol[:, 0:1], scalar2=None,
                                       op0=ALU.add))
                e.wait_ge(sV.h, sV.c)
                sV.inc(e.tensor_mul(out=ones[:], in0=oneh[:], in1=cums[:]))

            @block.tensor
            def _(e):
                w(e, sV)
                for j in range(SW // 512):
                    sT.inc(e.matmul(psel[:], sel[:],
                                    ones[:, j * 512:(j + 1) * 512],
                                    start=True, stop=True))
                    e.wait_ge(sV.h, sV.c + j + 1)
                for j in range(SW // 512):
                    sT.inc(e.matmul(psel[:], sel[:],
                                    oneh[:, j * 512:(j + 1) * 512],
                                    start=True, stop=True))
                    e.wait_ge(sV.h, sV.c + SW // 512 + j + 1)

            @block.vector
            def _(e):
                base_t = sT.c - 2 * (SW // 512)
                for j in range(SW // 512):
                    e.wait_ge(sT.h, base_t + j + 1)
                    sV.inc(e.tensor_copy(out=mpos[:, j * 512:(j + 1) * 512],
                                         in_=psel[:]))
                for j in range(SW // 512):
                    e.wait_ge(sT.h, base_t + SW // 512 + j + 1)
                    sV.inc(e.tensor_copy(out=mhot[:, j * 512:(j + 1) * 512],
                                         in_=psel[:]))

            @block.vector
            def _(e):
                w(e, sV)
                def b(ins):
                    sV.inc(ins)
                    e.wait_ge(sV.h, sV.c)
                b(e.tensor_scalar(out=mpos[:], in0=mpos[:], scalar1=1.0,
                                  scalar2=None, op0=ALU.subtract))
                b(e.tensor_scalar(out=slotf[:], in0=mpos[:],
                                  scalar1=float(CAP),
                                  scalar2=None, op0=ALU.is_lt))
                b(e.tensor_mul(out=slotf[:], in0=slotf[:], in1=mhot[:]))
                b(e.tensor_mul(out=mpos[:], in0=mpos[:], in1=slotf[:]))
                b(e.tensor_scalar(out=slotf[:], in0=slotf[:], scalar1=-1.0,
                                  scalar2=1.0, op0=ALU.mult, op1=ALU.add))
                b(e.scalar_tensor_tensor(out=slotf[:], in0=slotf[:],
                                         scalar=BIGI * 1.0, in1=mpos[:],
                                         op0=ALU.mult, op1=ALU.add))
                b(e.tensor_copy(out=sloti[:], in_=slotf[:]))
                sV.inc(e.memset(ztile[:], 0))

            @block.gpsimd
            def _(e):
                w(e, sV)
                qG.inc(e.dma_start(
                    out=slot_d.ap().rearrange("(s f) b -> s (f b)", s=16),
                    in_=sloti[:]), 16)
                qG.inc(e.dma_start(out=src_d.ap().rearrange("a b -> b a"),
                                   in_=ztile[:]), 16)

        if stop == "r":
            @block.gpsimd
            def _(e):
                w(e, sA, sV, sT)
                e.wait_ge(qG.h, qG.c)
            esr.close()
            es.close()
            return nc
        esr.close()
        NG8 = NK // 1024
        ess = ExitStack()
        ic8 = ess.enter_context(nc.sbuf_tensor("ic8", [128, 2, 8], I32))
        vc8 = ess.enter_context(nc.sbuf_tensor("vc8", [128, 2, 8], I32))
        if True:
            slot_v = slot_d.ap().rearrange("(G p f) b -> p G (f b)",
                                           p=128, f=8)
            tok_v = tokv.rearrange("a (G p f) -> a p G f", p=128, f=8)
            for g8 in range(NG8):
                @block.sync
                def _(e, g8=g8):
                    if g8 == 0:
                        w(e, qG)
                    elif g8 >= 2:
                        e.wait_ge(qG.h, qG.c - 16 * 8)
                    qL.inc(e.dma_start(
                        out=ic8[:, g8 % 2, :],
                        in_=slot_v[:, g8, :]), 16)
                    qL.inc(e.dma_start(
                        out=vc8[:, g8 % 2, :],
                        in_=tok_v[0, :, g8, :]), 16)

                @block.gpsimd
                def _(e, g8=g8):
                    w(e, qL)
                    for f in range(8):
                        qG.inc(e.indirect_dma_start(
                            out=src_d.ap(),
                            out_offset=IndirectOffsetOnAxis(
                                ap=ic8[:, g8 % 2, f:f + 1], axis=0),
                            in_=vc8[:, g8 % 2, f:f + 1], in_offset=None,
                            bounds_check=rbnd_box["cap"], oob_is_err=False),
                            16)

        if stop == "sc":
            @block.gpsimd
            def _(e):
                w(e, sA, sV, sT)
                e.wait_ge(qG.h, qG.c)
            ess.close()
            es.close()
            return nc
        ess.close()
        esg = ExitStack()
        sidx = esg.enter_context(nc.sbuf_tensor("sidx", [128, 32], I32))
        xg = esg.enter_context(nc.sbuf_tensor("xg", [128, 2, DM], F32))
        xg16 = esg.enter_context(nc.sbuf_tensor("xg16", [128, 2, DM], BF16))
        if True:
            @block.sync
            def _(e):
                w(e, qG)
                for sg in range(32):
                    qL.inc(e.dma_start(
                        out=sidx[:, sg:sg + 1],
                        in_=src_d.ap()[sg * 128:(sg + 1) * 128, :]), 16)

            with nc.sbuf_tensor("eq0", [128, 32], I32) as eq0:
                @block.vector
                def _(e):
                    w(e, qL)
                    sV.inc(e.tensor_scalar(out=eq0[:], in0=sidx[:], scalar1=0,
                                           scalar2=None, op0=ALU.is_equal))
                    e.wait_ge(sV.h, sV.c)
                    sV.inc(e.scalar_tensor_tensor(out=sidx[:], in0=eq0[:],
                                                  scalar=BIGI, in1=sidx[:],
                                                  op0=ALU.mult, op1=ALU.add))
                    e.wait_ge(sV.h, sV.c)
                    sV.inc(e.tensor_scalar(out=sidx[:], in0=sidx[:], scalar1=1,
                                           scalar2=None, op0=ALU.subtract))

            for sg in range(32):
                @block.vector
                def _(e, sg=sg):
                    if sg >= 2:
                        w(e, qG)
                    sV.inc(e.memset(xg[:, sg % 2, :], 0.0))

                @block.gpsimd
                def _(e, sg=sg):
                    w(e, sV)
                    qG.inc(e.indirect_dma_start(
                        out=xg[:, sg % 2, :], out_offset=None,
                        in_=x1f,
                        in_offset=IndirectOffsetOnAxis(ap=sidx[:, sg:sg + 1],
                                                       axis=0),
                        bounds_check=rbnd_box["n"], oob_is_err=False), 16)

                @block.vector
                def _(e, sg=sg):
                    w(e, qG)
                    sV.inc(e.tensor_copy(out=xg16[:, sg % 2, :],
                                         in_=xg[:, sg % 2, :]))

                @block.gpsimd
                def _(e, sg=sg):
                    w(e, sV)
                    qG.inc(e.dma_start(
                        out=disp_d.ap()[sg * 128:(sg + 1) * 128, :],
                        in_=xg16[:, sg % 2, :]), 16)

        if stop == "g":
            @block.gpsimd
            def _(e):
                w(e, sA, sV, sT)
                e.wait_ge(qG.h, qG.c)
            esg.close()
            es.close()
            return nc
        esg.close()
        esx = ExitStack()
        _e3 = esx.enter_context
        ew1s = _e3(nc.sbuf_tensor("ew1s", [128, 8, HF], BF16))
        dtile = _e3(nc.sbuf_tensor("dtile", [128, 4, DM], BF16))
        dT = _e3(nc.sbuf_tensor("dT", [128, 8, 512], BF16))
        h1s = _e3(nc.sbuf_tensor("h1s", [128, 32, 512], BF16))
        ew2r = _e3(nc.sbuf_tensor("ew2r", [128, 32, DM], BF16))
        eot = _e3(nc.sbuf_tensor("eot", [128, 2, 512], F32))
        pst2 = _e3(nc.psum_tensor("pst2", [128, 2, 128], BF16))
        psh2 = _e3(nc.psum_tensor("psh2", [128, 2, 512], F32))
        pse = _e3(nc.psum_tensor("pse", [128, 2, 512], F32))
        if True:
            @block.sync
            def _(e):
                w(e, qG)
                for k in range(8):
                    qL.inc(e.dma_start(out=ew1s[:, k, :],
                                       in_=ew1b_d.ap()[k, :, :]), 16)
                for k2 in range(32):
                    qL.inc(e.dma_start(out=ew2r[:, k2, :],
                                       in_=ew2b_d.ap()[k2, :, :]), 16)

            for st in range(8):
                @block.sync
                def _(e, st=st):
                    w(e, sT)
                    for j in range(4):
                        qL.inc(e.dma_start(
                            out=dtile[:, j, :],
                            in_=disp_d.ap()[st * 512 + j * 128:
                                            st * 512 + (j + 1) * 128, :]), 16)

                @block.sync
                def _(e, st=st):
                    w(e, sT, qL)
                    for j in range(32):
                        tt, mc = j // 8, j % 8
                        qL.inc(e.dma_start(
                            out=dT[:, mc, tt * 128:(tt + 1) * 128],
                            in_=dtile[:, tt, mc * 128:(mc + 1) * 128],
                            transpose=True), 16)

                @block.tensor
                def _(e, st=st):
                    w(e, sV, qL)
                    saw = sA.c
                    for hc in range(32):
                        if hc >= 2:
                            e.wait_ge(sA.h, saw + hc - 1)
                        for k in range(8):
                            ins = e.matmul(psh2[:, hc % 2, :],
                                           ew1s[:, k, hc * 128:(hc + 1) * 128],
                                           dT[:, k, :],
                                           start=(k == 0), stop=(k == 7))
                        sT.inc(ins)

                @block.scalar
                def _(e, st=st):
                    base_t = sT.c - 32
                    for hc in range(32):
                        e.wait_ge(sT.h, base_t + hc + 1)
                        sA.inc(e.activation(out=h1s[:, hc, :],
                                            in_=psh2[:, hc % 2, :],
                                            func=AF.Relu,
                                            bias=eb1s[:, hc:hc + 1]))

                for mh in range(2):
                    @block.tensor
                    def _(e, st=st, mh=mh):
                        w(e, qL, sA)
                        svw = sV.c
                        for ss in range(4):
                            if ss >= 2:
                                e.wait_ge(sV.h, svw + ss - 1)
                            for k2 in range(32):
                                ins = e.matmul(
                                    pse[:, ss % 2, :],
                                    h1s[:, k2, ss * 128:(ss + 1) * 128],
                                    ew2r[:, k2, mh * 512:(mh + 1) * 512],
                                    start=(k2 == 0), stop=(k2 == 31))
                            sT.inc(ins)

                    @block.vector
                    def _(e, st=st, mh=mh):
                        base_t = sT.c - 4
                        for ss in range(4):
                            e.wait_ge(sT.h, base_t + ss + 1)
                            if ss >= 2:
                                w(e, qG)
                            sV.inc(e.tensor_add(
                                out=eot[:, ss % 2, :], in0=pse[:, ss % 2, :],
                                in1=eb2b[:, mh * 512:(mh + 1) * 512]))

                    @block.gpsimd
                    def _(e, st=st, mh=mh):
                        base_v = sV.c - 4
                        for ss in range(4):
                            e.wait_ge(sV.h, base_v + ss + 1)
                            qG.inc(e.dma_start(
                                out=eoout[st * 512 + ss * 128:
                                          st * 512 + (ss + 1) * 128,
                                          mh * 512:(mh + 1) * 512],
                                in_=eot[:, ss % 2, :]), 16)

            @block.gpsimd
            def _(e):
                w(e, sA, sV, sT, qL)
                e.wait_ge(qG.h, qG.c)

        esx.close()
    es.close()
    return nc


def build_L3(N=32768, NTOK=4096):
    NK = 2 * N
    NSTRIP = 16
    SW = NK // NSTRIP
    nc = bass.Bass("TRN2", target_bir_lowering=False, debug=False,
                   detect_race_conditions=False)

    ein = lambda n, s, d=F32: nc.dram_tensor(n, s, d, kind="ExternalInput").ap()
    eof = ein("eof", [NE * CAP, DM])
    x1s = ein("x1s", [NTOK, DM])
    idsf = ein("idsf", [1, NK])
    p0s = ein("p0s", [16, N // 16])
    p1s = ein("p1s", [16, N // 16])
    emodv = ein("emodv", [128, 1])
    spatv = ein("spatv", [128, 128])
    sumpv = ein("sumpv", [128, 16])
    selpv = ein("selpv", [128, 16])
    stripev = ein("stripev", [1, 1], I32)
    iotacv = ein("iotacv", [128, 1], I32)
    lncv3 = ein("lncv3", [128, 2])

    out = nc.dram_tensor("out", [NTOK, DM], F32, kind="ExternalOutput").ap()

    idx0_d = nc.dram_tensor("idx0_d", [N, 1], I32)
    idx1_d = nc.dram_tensor("idx1_d", [N, 1], I32)
    pk0_d = nc.dram_tensor("pk0_d", [N, 1], F32)
    pk1_d = nc.dram_tensor("pk1_d", [N, 1], F32)

    es = ExitStack()
    _e = es.enter_context
    emod = _e(nc.sbuf_tensor("emod", [128, 1], F32))
    spat = _e(nc.sbuf_tensor("spat", [128, 128], F32))
    sump = _e(nc.sbuf_tensor("sump", [128, 16], F32))
    selp = _e(nc.sbuf_tensor("selp", [128, 16], F32))
    lncst3 = _e(nc.sbuf_tensor("lncst3", [128, 2], F32))
    _qL = _e(nc.semaphore("qL")); _qG = _e(nc.semaphore("qG"))
    _sV = _e(nc.semaphore("sV")); _sA = _e(nc.semaphore("sA"))
    _sT = _e(nc.semaphore("sT"))
    block = _e(nc.Block())
    if True:
        qL, qG, sV, sA, sT = Sem(_qL), Sem(_qG), Sem(_sV), Sem(_sA), Sem(_sT)

        @block.sync
        def _(e):
            for t_, src in ((emod, emodv), (spat, spatv), (sump, sumpv),
                            (selp, selpv), (lncst3, lncv3)):
                qL.inc(e.dma_start(out=t_[:], in_=src), 16)

        esr = ExitStack()
        _e2 = esr.enter_context
        idt = _e2(nc.sbuf_tensor("idt", [128, SW], F32))
        ones = _e2(nc.sbuf_tensor("ones", [128, SW], F32))
        oneh = _e2(nc.sbuf_tensor("oneh", [128, SW], F32))
        cums = _e2(nc.sbuf_tensor("cums", [128, SW], F32))
        ocol = _e2(nc.sbuf_tensor("ocol", [128, 1], F32))
        post = _e2(nc.sbuf_tensor("post", [16, SW], F32))
        ids16 = _e2(nc.sbuf_tensor("ids16", [16, SW], F32))
        keep = _e2(nc.sbuf_tensor("keep", [16, SW], F32))
        dei = _e2(nc.sbuf_tensor("dei", [16, SW], F32))
        deii = _e2(nc.sbuf_tensor("deii", [16, SW], I32))
        pso = _e2(nc.psum_tensor("pso", [128, 1], F32))
        psel = _e2(nc.psum_tensor("psel", [16, 512], F32))
        if True:
            @block.sync
            def _(e):
                w(e, qL)
                for st in range(NSTRIP):
                    qL.inc(e.dma_start(
                        out=idt[st * 8:(st + 1) * 8, :],
                        in_=idsf[:, st * SW:(st + 1) * SW]
                            .partition_broadcast(8)), 16)

            @block.vector
            def _(e):
                w(e, qL)
                sV.inc(e.memset(ones[:], 1.0))
                e.wait_ge(sV.h, sV.c)
                sV.inc(e.tensor_scalar(out=oneh[:], in0=idt[:],
                                       scalar1=emod[:, 0:1],
                                       scalar2=None, op0=ALU.is_equal))
                e.wait_ge(sV.h, sV.c)
                sV.inc(e.tensor_tensor_scan(out=cums[:], data0=ones[:],
                                            data1=oneh[:], initial=0.0,
                                            op0=ALU.mult, op1=ALU.add))

            @block.tensor
            def _(e):
                w(e, sV)
                sT.inc(e.matmul(pso[:], spat[:], cums[:, SW - 1:SW],
                                start=True, stop=True))

            @block.vector
            def _(e):
                w(e, sT)
                sV.inc(e.tensor_copy(out=ocol[:], in_=pso[:]))
                e.wait_ge(sV.h, sV.c)
                sV.inc(e.tensor_scalar(out=cums[:], in0=cums[:],
                                       scalar1=ocol[:, 0:1],
                                       scalar2=None, op0=ALU.add))
                e.wait_ge(sV.h, sV.c)
                sV.inc(e.tensor_mul(out=ones[:], in0=oneh[:], in1=cums[:]))

            @block.tensor
            def _(e):
                w(e, sV)
                for j in range(SW // 512):
                    sT.inc(e.matmul(psel[:], sump[:],
                                    ones[:, j * 512:(j + 1) * 512],
                                    start=True, stop=True))
                    e.wait_ge(sV.h, sV.c + j + 1)
                for j in range(SW // 512):
                    sT.inc(e.matmul(psel[:], selp[:],
                                    idt[:, j * 512:(j + 1) * 512],
                                    start=True, stop=True))
                    e.wait_ge(sV.h, sV.c + SW // 512 + j + 1)

            @block.vector
            def _(e):
                base_t = sT.c - 2 * (SW // 512)
                for j in range(SW // 512):
                    e.wait_ge(sT.h, base_t + j + 1)
                    sV.inc(e.tensor_copy(out=post[:, j * 512:(j + 1) * 512],
                                         in_=psel[:]))
                for j in range(SW // 512):
                    e.wait_ge(sT.h, base_t + SW // 512 + j + 1)
                    sV.inc(e.tensor_copy(out=ids16[:, j * 512:(j + 1) * 512],
                                         in_=psel[:]))

            @block.vector
            def _(e):
                w(e, sV)
                def b(ins):
                    sV.inc(ins)
                    e.wait_ge(sV.h, sV.c)
                b(e.tensor_scalar(out=post[:], in0=post[:], scalar1=1.0,
                                  scalar2=None, op0=ALU.subtract))
                b(e.tensor_scalar(out=keep[:], in0=post[:], scalar1=float(CAP),
                                  scalar2=None, op0=ALU.is_lt))
                b(e.tensor_scalar(out=post[:], in0=post[:],
                                  scalar1=float(CAP - 1),
                                  scalar2=None, op0=ALU.min))
                b(e.scalar_tensor_tensor(out=post[:], in0=ids16[:],
                                         scalar=float(CAP), in1=post[:],
                                         op0=ALU.mult, op1=ALU.add))

            @block.vector
            def _(e):
                sV.inc(e.tensor_copy(out=dei[:, 0:SW // 2],
                                     in_=post[:].rearrange("p (f k) -> p k f",
                                                           k=2)[:, 0, :]))
                e.wait_ge(sV.h, sV.c)
                sV.inc(e.tensor_copy(out=deii[:, 0:SW // 2],
                                     in_=dei[:, 0:SW // 2]))

            @block.gpsimd
            def _(e):
                w(e, sV)
                qG.inc(e.dma_start(
                    out=idx0_d.ap().rearrange("(s f) b -> s (f b)", s=16),
                    in_=deii[:, 0:SW // 2]), 16)

            @block.vector
            def _(e):
                w(e, qG)
                sV.inc(e.tensor_copy(out=dei[:, 0:SW // 2],
                                     in_=post[:].rearrange("p (f k) -> p k f",
                                                           k=2)[:, 1, :]))
                e.wait_ge(sV.h, sV.c)
                sV.inc(e.tensor_copy(out=deii[:, 0:SW // 2],
                                     in_=dei[:, 0:SW // 2]))

            @block.gpsimd
            def _(e):
                w(e, sV, qG)
                qG.inc(e.dma_start(
                    out=idx1_d.ap().rearrange("(s f) b -> s (f b)", s=16),
                    in_=deii[:, 0:SW // 2]), 16)

            pin = esr.enter_context(nc.sbuf_tensor("pin", [16, SW // 2], F32))
            if True:
                @block.sync
                def _(e):
                    w(e, sV)
                    qL.inc(e.dma_start(out=pin[:], in_=p0s), 16)

                @block.vector
                def _(e):
                    w(e, qL, qG)
                    sV.inc(e.tensor_copy(out=dei[:, 0:SW // 2],
                                         in_=keep[:].rearrange(
                                             "p (f k) -> p k f", k=2)[:, 0, :]))
                    e.wait_ge(sV.h, sV.c)
                    sV.inc(e.tensor_mul(out=dei[:, 0:SW // 2],
                                        in0=dei[:, 0:SW // 2], in1=pin[:]))

                @block.gpsimd
                def _(e):
                    w(e, sV)
                    qG.inc(e.dma_start(
                        out=pk0_d.ap().rearrange("(s f) b -> s (f b)", s=16),
                        in_=dei[:, 0:SW // 2]), 16)

                @block.sync
                def _(e):
                    w(e, sV, qG)
                    qL.inc(e.dma_start(out=pin[:], in_=p1s), 16)

                @block.vector
                def _(e):
                    w(e, qL, qG)
                    sV.inc(e.tensor_copy(out=dei[:, 0:SW // 2],
                                         in_=keep[:].rearrange(
                                             "p (f k) -> p k f", k=2)[:, 1, :]))
                    e.wait_ge(sV.h, sV.c)
                    sV.inc(e.tensor_mul(out=dei[:, 0:SW // 2],
                                        in0=dei[:, 0:SW // 2], in1=pin[:]))

                @block.gpsimd
                def _(e):
                    w(e, sV)
                    qG.inc(e.dma_start(
                        out=pk1_d.ap().rearrange("(s f) b -> s (f b)", s=16),
                        in_=dei[:, 0:SW // 2]), 16)

        esr.close()
        esa = ExitStack()
        _e3 = esa.enter_context
        g0 = _e3(nc.sbuf_tensor("g0", [128, DM], F32))
        g1 = _e3(nc.sbuf_tensor("g1", [128, DM], F32))
        yv = _e3(nc.sbuf_tensor("yv", [128, DM], F32))
        x1t = _e3(nc.sbuf_tensor("x1t", [128, DM], F32))
        ic0 = _e3(nc.sbuf_tensor("ic0", [128, 1], I32))
        ic1 = _e3(nc.sbuf_tensor("ic1", [128, 1], I32))
        pc0 = _e3(nc.sbuf_tensor("pc0", [128, 1], F32))
        pc1 = _e3(nc.sbuf_tensor("pc1", [128, 1], F32))
        lncc = _e3(nc.sbuf_tensor("lncc", [128, 8], F32))
        sstr = _e3(nc.sbuf_tensor("sstr", [1, 1], I32))
        iocol = _e3(nc.sbuf_tensor("iocol", [128, 1], I32))
        strc = _e3(nc.sbuf_tensor("strc", [128, 1], I32))
        bcol = _e3(nc.sbuf_tensor("bcol", [128, 1], I32))
        tcol = _e3(nc.sbuf_tensor("tcol", [128, 2, 1], I32))
        if True:
            @block.sync
            def _(e):
                qL.inc(e.dma_start(out=sstr[:], in_=stripev), 16)
                qL.inc(e.dma_start(out=iocol[:], in_=iotacv), 16)
                qL.inc(e.dma_start(out=strc[:],
                                   in_=stripev.partition_broadcast(128)), 16)

            @block.vector
            def _(e):
                w(e, qL)
                sV.inc(e.tensor_scalar(out=bcol[:], in0=strc[:], scalar1=NTOK,
                                       scalar2=None, op0=ALU.mult))
                e.wait_ge(sV.h, sV.c)
                sV.inc(e.tensor_tensor(out=bcol[:], in0=bcol[:], in1=iocol[:],
                                       op=ALU.add))

            for tt in range(NTOK // 128):
                @block.sync
                def _(e, tt=tt):
                    w(e, sV, qG)
                    qL.inc(e.dma_start(out=x1t[:],
                                       in_=x1s[tt * 128:(tt + 1) * 128, :]), 16)

                @block.vector
                def _(e, tt=tt):
                    w(e, qG)
                    sV.inc(e.tensor_scalar(out=tcol[:, tt % 2, :], in0=bcol[:],
                                           scalar1=tt * 128, scalar2=None,
                                           op0=ALU.add))

                @block.gpsimd
                def _(e, tt=tt):
                    w(e, qL, sV)
                    tc = tcol[:, tt % 2, :]
                    qG.inc(e.indirect_dma_start(
                        out=ic0[:], out_offset=None, in_=idx0_d.ap(),
                        in_offset=IndirectOffsetOnAxis(ap=tc, axis=0)), 16)
                    qG.inc(e.indirect_dma_start(
                        out=ic1[:], out_offset=None, in_=idx1_d.ap(),
                        in_offset=IndirectOffsetOnAxis(ap=tc, axis=0)), 16)
                    qG.inc(e.indirect_dma_start(
                        out=pc0[:], out_offset=None, in_=pk0_d.ap(),
                        in_offset=IndirectOffsetOnAxis(ap=tc, axis=0)), 16)
                    qG.inc(e.indirect_dma_start(
                        out=pc1[:], out_offset=None, in_=pk1_d.ap(),
                        in_offset=IndirectOffsetOnAxis(ap=tc, axis=0)), 16)
                    e.wait_ge(qG.h, qG.c)
                    qG.inc(e.indirect_dma_start(
                        out=g0[:], out_offset=None, in_=eof,
                        in_offset=IndirectOffsetOnAxis(ap=ic0[:], axis=0)), 16)
                    qG.inc(e.indirect_dma_start(
                        out=g1[:], out_offset=None, in_=eof,
                        in_offset=IndirectOffsetOnAxis(ap=ic1[:], axis=0)), 16)

                @block.vector
                def _(e, tt=tt):
                    w(e, qG)
                    sV.inc(e.tensor_scalar(out=g0[:], in0=g0[:],
                                            scalar1=pc0[:, 0:1],
                                            scalar2=None, op0=ALU.mult))
                    e.wait_ge(sV.h, sV.c)
                    sV.inc(e.scalar_tensor_tensor(out=yv[:], in0=g1[:],
                                                  scalar=pc1[:, 0:1],
                                                  in1=g0[:],
                                                  op0=ALU.mult, op1=ALU.add))
                    e.wait_ge(sV.h, sV.c)
                    sV.inc(e.tensor_add(out=yv[:], in0=yv[:], in1=x1t[:]))
                    e.wait_ge(sV.h, sV.c)
                    sV.inc(e.reduce_sum(out=lncc[:, 0:1], in_=yv[:], axis=AX))
                    e.wait_ge(sV.h, sV.c)
                    sV.inc(e.tensor_scalar(out=lncc[:, 1:2], in0=lncc[:, 0:1],
                                           scalar1=1.0 / DM, scalar2=None,
                                           op0=ALU.mult))
                    e.wait_ge(sV.h, sV.c)
                    sV.inc(e.tensor_scalar(out=g0[:], in0=yv[:],
                                           scalar1=lncc[:, 1:2], scalar2=None,
                                           op0=ALU.subtract))

                @block.scalar
                def _(e, tt=tt):
                    w(e, sV)
                    sA.inc(e.activation(out=g1[:], in_=g0[:], func=AF.Square))

                @block.vector
                def _(e, tt=tt):
                    w(e, sA)
                    sV.inc(e.reduce_sum(out=lncc[:, 2:3], in_=g1[:], axis=AX))
                    e.wait_ge(sV.h, sV.c)
                    sV.inc(e.tensor_scalar(out=lncc[:, 3:4], in0=lncc[:, 2:3],
                                           scalar1=1.0 / DM, scalar2=None,
                                           op0=ALU.mult))

                @block.scalar
                def _(e, tt=tt):
                    w(e, sV)
                    sA.inc(e.activation(out=lncc[:, 4:5], in_=lncc[:, 3:4],
                                        func=AF.Ln, bias=lncst3[:, 0:1]))
                    e.wait_ge(sA.h, sA.c)
                    sA.inc(e.activation(out=lncc[:, 5:6], in_=lncc[:, 4:5],
                                        func=AF.Exp, scale=lncst3[:, 1:2]))

                @block.vector
                def _(e, tt=tt):
                    w(e, sA)
                    sV.inc(e.tensor_scalar(out=yv[:], in0=g0[:],
                                           scalar1=lncc[:, 5:6], scalar2=None,
                                           op0=ALU.mult))

                @block.gpsimd
                def _(e, tt=tt):
                    w(e, sV)
                    qG.inc(e.dma_start(out=out[tt * 128:(tt + 1) * 128, :],
                                       in_=yv[:]), 16)

            @block.gpsimd
            def _(e):
                e.wait_ge(qG.h, qG.c)

        esa.close()
    es.close()
    return nc


_cache = {}


def _get(key, builder):
    if key not in _cache:
        _cache[key] = builder()
    return _cache[key]


def _rs(a, nch=NCH, inner=None):
    a = np.asarray(a)
    k = a.shape[1] if a.ndim == 2 else 1
    a = a.reshape(nch, 128, -1).transpose(1, 0, 2).reshape(128, -1)
    return np.ascontiguousarray(a)


def kernel(L0_attn_x, L0_attn_W_in, L0_attn_conv_w, L0_attn_conv_b,
           L0_attn_W_x, L0_attn_W_dt, L0_attn_b_dt, L0_attn_A_log,
           L0_attn_D_param, L0_attn_W_out, L0_ffn_rw1, L0_ffn_rb1,
           L0_ffn_rw2, L0_ffn_rb2, L0_ffn_ew1, L0_ffn_eb1, L0_ffn_ew2,
           L0_ffn_eb2):
    x = np.asarray(L0_attn_x, np.float32)
    B, T, _ = x.shape
    TH = T // 2
    N = B * T
    NK = 2 * N

    ident = np.eye(128, dtype=np.float32)
    iota8 = np.arange(NE, dtype=np.float32)[None, :]
    nonce = np.array([[np.random.randint(1, 1 << 30)]], np.int32)

    nc1 = _get(("L1", T), lambda: build_L1(T=T))
    in1 = []
    for c in range(NCORES):
        b, dh = c >> 1, c & 1
        sl = slice(dh * DH, (dh + 1) * DH)
        m = dict(
            xT=np.ascontiguousarray(x[b].T),
            x_tm=np.ascontiguousarray(x[b, dh * TH:(dh + 1) * TH]),
            winT=np.ascontiguousarray(
                np.concatenate([L0_attn_W_in[sl], L0_attn_W_in[2048:][sl]],
                               0).T.astype(np.float32)),
            cwv=_rs(np.asarray(L0_attn_conv_w)[sl, 0, :]),
            cbv=_rs(np.asarray(L0_attn_conv_b)[sl, None]),
            wxT=np.ascontiguousarray(np.asarray(L0_attn_W_x)[:, sl].T),
            wdtv=_rs(np.asarray(L0_attn_W_dt)[sl]),
            bdtv=_rs(np.asarray(L0_attn_b_dt)[sl, None]),
            alogv=_rs(np.asarray(L0_attn_A_log)[sl]),
            dpv=_rs(np.asarray(L0_attn_D_param)[sl, None]),
            woutT=np.ascontiguousarray(np.asarray(L0_attn_W_out)[:, sl].T),
            rw1=np.asarray(L0_ffn_rw1, np.float32),
            rb1v=np.ascontiguousarray(
                np.asarray(L0_ffn_rb1).reshape(32, 128).T),
            rw2=np.asarray(L0_ffn_rw2, np.float32),
            rb2v=np.asarray(L0_ffn_rb2, np.float32).reshape(NE, 1),
            identv=ident, iotav=iota8, noncev=nonce,
            slotv=np.array([[dh]], np.int32),
            lncv=np.tile(np.array([[1e-5, -0.5]], np.float32), (128, 1)),
        )
        in1.append({k: np.ascontiguousarray(v) for k, v in m.items()})
    r1 = run_bass_kernel_spmd(nc1, in1, core_ids=list(range(NCORES)))
    x1f = np.concatenate([r1.results[c]["x1out"] for c in range(NCORES)], 0)
    iv = np.concatenate([r1.results[c]["ivout"] for c in range(NCORES)], 0)

    idsf = np.ascontiguousarray(
        np.stack([iv[:, 0], iv[:, 1]], 1).reshape(1, NK).astype(np.float32))
    tokv = np.ascontiguousarray(
        (np.arange(NK, dtype=np.int64) // 2 + 1).astype(np.int32)[None, :])
    emodv = (np.arange(128, dtype=np.float32) % 8)[:, None]
    spatv = np.zeros((128, 128), np.float32)
    for rp in range(128):
        for r in range(128):
            if rp % 8 == r % 8 and rp // 8 < r // 8:
                spatv[rp, r] = 1.0

    nc2 = _get(("L2", N), lambda: build_L2(N=N))
    in2 = []
    for e_ in range(NCORES):
        selv = np.zeros((128, 16), np.float32)
        for q in range(16):
            selv[q * 8 + e_, q] = 1.0
        m = dict(
            x1f=x1f, idsf=idsf, tokv=tokv,
            ew1in=np.asarray(L0_ffn_ew1)[e_].astype(np.float32),
            ew2in=np.asarray(L0_ffn_ew2)[e_].astype(np.float32),
            eb1v=np.ascontiguousarray(
                np.asarray(L0_ffn_eb1)[e_].reshape(32, 128).T),
            eb2v=np.asarray(L0_ffn_eb2)[e_].astype(np.float32)[None, :],
            emodv=emodv, spatv=spatv, selv=selv, identv=ident,
        )
        in2.append({k: np.ascontiguousarray(v) for k, v in m.items()})
    r2 = run_bass_kernel_spmd(nc2, in2, core_ids=list(range(NCORES)))
    eof = np.concatenate([r2.results[e_]["eoout"] for e_ in range(NCORES)], 0)

    sumpv = np.zeros((128, 16), np.float32)
    selpv = np.zeros((128, 16), np.float32)
    for q in range(16):
        for e_ in range(8):
            sumpv[q * 8 + e_, q] = 1.0
        selpv[q * 8, q] = 1.0
    p0s = np.ascontiguousarray(iv[:, 2].reshape(16, N // 16))
    p1s = np.ascontiguousarray(iv[:, 3].reshape(16, N // 16))

    nc3 = _get(("L3", N), lambda: build_L3(N=N, NTOK=N // NCORES))
    NTOK = N // NCORES
    in3 = []
    for c in range(NCORES):
        m = dict(
            eof=eof, x1s=x1f[c * NTOK:(c + 1) * NTOK], idsf=idsf,
            p0s=p0s, p1s=p1s, emodv=emodv, spatv=spatv,
            sumpv=sumpv, selpv=selpv,
            stripev=np.array([[c]], np.int32),
            iotacv=np.arange(128, dtype=np.int32)[:, None],
            lncv3=np.tile(np.array([[1e-5, -0.5]], np.float32), (128, 1)),
        )
        in3.append({k: np.ascontiguousarray(v) for k, v in m.items()})
    r3 = run_bass_kernel_spmd(nc3, in3, core_ids=list(range(NCORES)))
    y = np.concatenate([r3.results[c]["out"] for c in range(NCORES)], 0)
    return y.reshape(B, T, DM)


# version 07
# speedup vs baseline: 1.0882x; 1.0882x over previous
import numpy as np
from contextlib import ExitStack
import concourse.bass as bass
import concourse.mybir as mybir
from concourse.bass import IndirectOffsetOnAxis
from concourse.bass_utils import run_bass_kernel_spmd

F32 = mybir.dt.float32
BF16 = mybir.dt.bfloat16
I32 = mybir.dt.int32
AF = mybir.ActivationFunctionType
ALU = mybir.AluOpType
AX = mybir.AxisListType.X

DM = 1024
DH = 1024
NCH = 8
DS = 16
HF = 4096
NE = 8
CAP = 4096
NCORES = 8
BIGI = 1 << 20



_regc = [0]


def rn(p):
    _regc[0] += 1
    return f"{p}{_regc[0]}"

class Sem:
    def __init__(self, h):
        self.h = h
        self.c = 0

    def inc(self, instr, n=1):
        instr.then_inc(self.h, n)
        self.c += n
        return self.c


def w(eng, *sems):
    for s in sems:
        if s.c:
            eng.wait_ge(s.h, s.c)


def slot_branch(e, sslot, arm0, arm1, regname="rs"):
    with e.register(rn(regname)) as rs_:
        e.reg_load(rs_, sslot[0:1, 0:1])
        with e.If_eq(rs_, 0):
            arm0()
        with e.Else():
            arm1()


def poll_partner(e, rpoll, rtmp, flag_ap_pt, sflag, snonce_sb, pp_h):
    with e.register(rn("rf")) as rf, e.register(rn("rc")) as rc, \
         e.register(rn("rnn")) as rnn:
        e.reg_load(rnn, snonce_sb[0:1, 0:1])
        e.reg_mov(rc, 1)
        with e.While(rc):
            e.reg_add(rpoll, rpoll, 16)
            e.dma_start(out=sflag[:], in_=flag_ap_pt).then_inc(pp_h, 16)
            e.wait_ge(pp_h, e.snap(rpoll))
            e.reg_load(rf, sflag[0:1, 0:1])
            e.reg_alu(rc, rf, e.snap(rnn), ALU.not_equal)


def build_L1(T=8192, MT=512, ST=1024, WST=None, dbg=False, stop='full'):
    TH = T // 2
    NMT = T // MT
    NST = T // ST
    WST = WST or ST
    NWS = T // WST
    NRG = TH // 512
    nc = bass.Bass("TRN2", target_bir_lowering=False, debug=False,
                   detect_race_conditions=False)

    ein = lambda n, s, d=F32: nc.dram_tensor(n, s, d, kind="ExternalInput").ap()
    xT = ein("xT", [DM, T])
    x_tm = ein("x_tm", [TH, DM])
    winT = ein("winT", [DM, 2 * DH])
    cwv = ein("cwv", [128, NCH * 4])
    cbv = ein("cbv", [128, NCH])
    wxT = ein("wxT", [DH, 33])
    wdtv = ein("wdtv", [128, NCH])
    bdtv = ein("bdtv", [128, NCH])
    alogv = ein("alogv", [128, NCH * DS])
    dpv = ein("dpv", [128, NCH])
    woutT = ein("woutT", [DH, DM])
    rw1 = ein("rw1", [DM, HF])
    rb1v = ein("rb1v", [128, HF // 128])
    rw2 = ein("rw2", [HF, NE])
    rb2v = ein("rb2v", [NE, 1])
    identv = ein("identv", [128, 128])
    iotav = ein("iotav", [1, NE])
    noncev = ein("noncev", [1, 1], I32)
    slotv = ein("slotv", [1, 1], I32)
    lncv = ein("lncv", [128, 2])

    x1out = nc.dram_tensor("x1out", [TH, DM], F32, kind="ExternalOutput").ap()
    ivout = nc.dram_tensor("ivout", [TH, 4], F32, kind="ExternalOutput").ap()

    ssm_sh = nc.dram_tensor("ssm_sh", [2, 33, T], F32, addr_space="Shared")
    mix_sh = nc.dram_tensor("mix_sh", [2 * T, DM], F32, addr_space="Shared")
    flag_sh = nc.dram_tensor("flag_sh", [2, 2, 1], I32, addr_space="Shared")

    if dbg:
        dbg_u = nc.dram_tensor("dbg_u", [NCH, 128, T], F32, kind="ExternalOutput").ap()
        dbg_z = nc.dram_tensor("dbg_z", [NCH, 128, T], F32, kind="ExternalOutput").ap()
        dbg_y2 = nc.dram_tensor("dbg_y2", [NCH, 128, T], F32, kind="ExternalOutput").ap()
        dbg_ssm = nc.dram_tensor("dbg_ssm", [33, T], F32, kind="ExternalOutput").ap()
        dbg_mix = nc.dram_tensor("dbg_mix", [2 * T, DM], F32, kind="ExternalOutput").ap()
        dbg_ln = nc.dram_tensor("dbg_ln", [128, 8 + 2 * DM], F32, kind="ExternalOutput").ap()
    zsil_d = nc.dram_tensor("zsil_d", [NCH, 128, T], F32)
    u_d = nc.dram_tensor("u_d", [NCH, 128, T], F32)
    y2_d = nc.dram_tensor("y2_d", [NCH, 128, T], F32)
    ssmb_d = nc.dram_tensor("ssmb_d", [33, T], BF16)
    ssm0_d = nc.dram_tensor("ssm0_d", [1, T], F32)

    es = ExitStack()
    sb = lambda n, s, d=F32: es.enter_context(nc.sbuf_tensor(n, s, d))
    ps = lambda n, s: es.enter_context(nc.psum_tensor(n, s, F32))
    cw = sb("cw", [128, NCH, 4]); cb = sb("cb", [128, NCH])
    wdtc = sb("wdtc", [128, NCH]); bdtc = sb("bdtc", [128, NCH])
    ac = sb("ac", [128, NCH * DS]); dpc = sb("dpc", [128, NCH])
    id16 = sb("id16", [128, 128], BF16); idf = sb("idf", [128, 128])
    io8b = sb("io8b", [128, NE]); snonce = sb("snonce", [1, 1], I32)
    sflag = sb("sflag", [1, 1], I32); sslot = sb("sslot", [1, 1], I32)
    hcarry = sb("hcarry", [128, NCH * DS])
    lncst = sb("lncst", [128, 2])
    _qL = es.enter_context(nc.semaphore("qL"))
    _qG = es.enter_context(nc.semaphore("qG"))
    _sV = es.enter_context(nc.semaphore("sV"))
    _sA = es.enter_context(nc.semaphore("sA"))
    _sT = es.enter_context(nc.semaphore("sT"))
    _pp = es.enter_context(nc.semaphore("pp"))
    _pm = es.enter_context(nc.semaphore("pm"))
    block = es.enter_context(nc.Block())
    if True:
        qL, qG, sV, sA, sT = Sem(_qL), Sem(_qG), Sem(_sV), Sem(_sA), Sem(_sT)
        pm = Sem(_pm)

        rpoll_box = {}

        @block.gpsimd
        def _(e):
            rpoll_box["r"] = e.alloc_register(rn("rpoll"))
            e.reg_mov(rpoll_box["r"], 0)

        @block.sync
        def _(e):
            for t_, src in ((cw, cwv), (cb, cbv), (wdtc, wdtv), (bdtc, bdtv),
                            (ac, alogv), (dpc, dpv), (idf, identv),
                            (snonce, noncev), (sslot, slotv), (lncst, lncv)):
                qL.inc(e.dma_start(out=t_[:], in_=src), 16)
            qL.inc(e.dma_start(out=io8b[:], in_=iotav.partition_broadcast(128)), 16)

        @block.vector
        def _(e):
            w(e, qL)
            sV.inc(e.tensor_copy(out=id16[:], in_=idf[:]))
            sV.inc(e.memset(hcarry[:], 0.0))

        @block.scalar
        def _(e):
            w(e, qL)
            sA.inc(e.activation(out=ac[:], in_=ac[:], func=AF.Exp))

        @block.vector
        def _(e):
            w(e, sA)
            sV.inc(e.tensor_scalar(out=ac[:], in0=ac[:], scalar1=-1.0,
                                   scalar2=None, op0=ALU.mult))

        esM = ExitStack()
        w16 = esM.enter_context(nc.sbuf_tensor("w16", [128, 8, 2 * DH], BF16))
        wx16 = esM.enter_context(nc.sbuf_tensor("wx16", [128, NCH, 33], BF16))
        ssma = esM.enter_context(nc.sbuf_tensor("ssma", [33, T], F32))
        if True:
            with nc.sbuf_tensor("wtmp", [128, 8, 2 * DH], F32) as wtmp:
                @block.sync
                def _(e):
                    w(e, sV)
                    qL.inc(e.dma_start(
                        out=wtmp[:],
                        in_=winT.rearrange("(k p) n -> p k n", p=128)), 16)

                @block.vector
                def _(e):
                    w(e, qL)
                    sV.inc(e.tensor_copy(out=w16[:], in_=wtmp[:]))

                @block.sync
                def _(e):
                    w(e, sV)
                    qL.inc(e.dma_start(
                        out=wtmp[:, 0, 0:NCH * 33].rearrange(
                            "p (k n) -> p k n", k=NCH),
                        in_=wxT.rearrange("(k p) n -> p k n", p=128)), 16)

                @block.vector
                def _(e):
                    w(e, qL)
                    sV.inc(e.tensor_copy(
                        out=wx16[:],
                        in_=wtmp[:, 0, 0:NCH * 33].rearrange(
                            "p (k n) -> p k n", k=NCH)))

            esMt = ExitStack()
            xt32 = esMt.enter_context(nc.sbuf_tensor("xt32", [128, 8, MT], F32))
            xt16 = esMt.enter_context(nc.sbuf_tensor("xt16", [128, 8, MT], BF16))
            xdh = esMt.enter_context(nc.sbuf_tensor("xdh", [128, NCH, MT + 3], F32))
            cacc = esMt.enter_context(nc.sbuf_tensor("cacc", [128, 2, MT], F32))
            ut = esMt.enter_context(nc.sbuf_tensor("ut", [128, NCH, MT], F32))
            ut16 = esMt.enter_context(nc.sbuf_tensor("ut16", [128, NCH, MT], BF16))
            zst = esMt.enter_context(nc.sbuf_tensor("zst", [128, NCH, MT], F32))
            psa = esMt.enter_context(nc.psum_tensor("psa", [128, 2, 512], F32))
            psm = esMt.enter_context(nc.psum_tensor("psm", [33, 512], F32))
            if True:
                NSUB = MT // 512

                @block.vector
                def _(e):
                    sV.inc(e.memset(xdh[:], 0.0))

                for i in range(NMT):
                    @block.sync
                    def _(e, i=i):
                        w(e, sV, sT)
                        qL.inc(e.dma_start(
                            out=xt32[:],
                            in_=xT[:, i * MT:(i + 1) * MT]
                                .rearrange("(k p) t -> p k t", p=128)), 16)

                    @block.vector
                    def _(e, i=i):
                        w(e, qL)
                        sV.inc(e.tensor_copy(out=xt16[:], in_=xt32[:]))

                    mk = {}

                    @block.tensor
                    def _(e, i=i, mk=mk):
                        w(e, sV, sA)
                        saw = sA.c
                        planned = {}
                        for j in range(16):
                            if j < 8:
                                planned[j] = saw + 2 * j + 2
                            else:
                                planned[j] = saw + 16 + (j - 8) + 1
                        for o in range(16):
                            if o >= 2:
                                e.wait_ge(sA.h, planned[o - 2])
                            for k in range(8):
                                ins = e.matmul(psa[:, o % 2, 0:MT],
                                               w16[:, k, o * 128:(o + 1) * 128],
                                               xt16[:, k, :],
                                               start=(k == 0), stop=(k == 7))
                            sT.inc(ins)
                            mk[("xz", o)] = sT.c

                    @block.scalar
                    def _(e, i=i, mk=mk):
                        w(e, qG)
                        for c in range(NCH):
                            e.wait_ge(sT.h, mk[("xz", c)])
                            sA.inc(e.copy(out=xdh[:, c, 0:3],
                                          in_=xdh[:, c, MT:MT + 3]))
                            e.wait_ge(sA.h, sA.c)
                            sA.inc(e.copy(out=xdh[:, c, 3:MT + 3],
                                          in_=psa[:, c % 2, 0:MT]))
                            mk[("evac", c)] = sA.c
                        for o in range(8, 16):
                            e.wait_ge(sT.h, mk[("xz", o)])
                            sA.inc(e.activation(out=zst[:, o - 8, :],
                                                in_=psa[:, o % 2, 0:MT],
                                                func=AF.Silu))
                            mk[("zev", o)] = sA.c

                    @block.vector
                    def _(e, i=i, mk=mk):
                        saw = sA.c
                        for c in range(NCH):
                            e.wait_ge(sA.h, mk[("evac", c)])
                            if c >= 1:
                                e.wait_ge(sA.h, saw + c)
                            sV.inc(e.tensor_scalar(out=cacc[:, 0, :],
                                            in0=xdh[:, c, 0:MT],
                                            scalar1=cw[:, c, 0:1], scalar2=None,
                                            op0=ALU.mult))
                            for j in (1, 2, 3):
                                e.wait_ge(sV.h, sV.c)
                                sV.inc(e.scalar_tensor_tensor(
                                    out=cacc[:, j % 2, :],
                                    in0=xdh[:, c, j:j + MT],
                                    scalar=cw[:, c, j:j + 1],
                                    in1=cacc[:, (j + 1) % 2, :],
                                    op0=ALU.mult, op1=ALU.add))
                            mk[("conv", c)] = sV.c

                    @block.scalar
                    def _(e, i=i, mk=mk):
                        for c in range(NCH):
                            e.wait_ge(sV.h, mk[("conv", c)])
                            sA.inc(e.activation(out=ut[:, c, :],
                                                in_=cacc[:, 1, :], func=AF.Silu,
                                                bias=cb[:, c:c + 1], scale=1.0))
                            mk[("usil", c)] = sA.c

                    @block.vector
                    def _(e, i=i):
                        w(e, sA, sT)
                        sV.inc(e.tensor_copy(out=ut16[:], in_=ut[:]))

                    @block.tensor
                    def _(e, i=i):
                        w(e, sV, sA)
                        for sub in range(MT // 512):
                            for c in range(NCH):
                                ins = e.matmul(
                                    psm[:, :], wx16[:, c, :],
                                    ut16[:, c, sub * 512:(sub + 1) * 512],
                                    start=(c == 0), stop=(c == NCH - 1))
                            sT.inc(ins)

                    @block.scalar
                    def _(e, i=i):
                        base_t = sT.c - MT // 512
                        for sub in range(MT // 512):
                            e.wait_ge(sT.h, base_t + sub + 1)
                            sA.inc(e.copy(
                                out=ssma[:, i * MT + sub * 512:
                                         i * MT + (sub + 1) * 512],
                                in_=psm[:, :]))

                    @block.gpsimd
                    def _(e, i=i):
                        w(e, sA, sV)
                        for c in range(NCH):
                            qG.inc(e.dma_start(
                                out=zsil_d.ap()[c, :, i * MT:(i + 1) * MT],
                                in_=zst[:, c, :]), 16)
                            qG.inc(e.dma_start(
                                out=u_d.ap()[c, :, i * MT:(i + 1) * MT],
                                in_=ut[:, c, :]), 16)

            esMt.close()
            with (
                nc.sbuf_tensor("stmp", [33, T], F32) as stmp,
                nc.sbuf_tensor("ssmb16", [33, T], BF16) as ssmb16,
            ):
                @block.gpsimd
                def _(e):
                    w(e, sA, qG)
                    rpoll = rpoll_box["r"]

                    def arm(me):
                        pt = 1 - me
                        e.dma_start(out=ssm_sh.ap()[me, :, :],
                                    in_=ssma[:]).then_inc(_qG, 16)
                        e.wait_ge(_qG, qG.c + 16)
                        e.dma_start(out=flag_sh.ap()[me, 0, :],
                                    in_=snonce[:]).then_inc(_qG, 16)
                        e.wait_ge(_qG, qG.c + 32)
                        poll_partner(e, rpoll, None,
                                     flag_sh.ap()[pt, 0, :],
                                     sflag, snonce, _pp)
                        e.reg_add(rpoll, rpoll, 16)
                        e.dma_start(out=stmp[:],
                                    in_=ssm_sh.ap()[pt, :, :]).then_inc(_pp, 16)
                        e.wait_ge(_pp, e.snap(rpoll))

                    slot_branch(e, sslot, lambda: arm(0), lambda: arm(1))
                    qG.c += 32
                    pm.inc(e.nop())

                @block.vector
                def _(e):
                    e.wait_ge(pm.h, pm.c)
                    sV.inc(e.tensor_add(out=ssma[:], in0=ssma[:], in1=stmp[:]))
                    e.wait_ge(sV.h, sV.c)
                    sV.inc(e.tensor_copy(out=ssmb16[:], in_=ssma[:]))

                @block.scalar
                def _(e):
                    w(e, sV)
                    sA.inc(e.dma_start(out=ssmb_d.ap(), in_=ssmb16[:]), 16)
                    sA.inc(e.dma_start(out=ssm0_d.ap(), in_=ssma[0:1, :]), 16)
                    if dbg:
                        sA.inc(e.dma_start(out=dbg_ssm, in_=ssma[:]), 16)

        esM.close()
        if stop == "x":
            @block.gpsimd
            def _(e):
                w(e, sA, sV)
                e.wait_ge(qG.h, qG.c)
            es.close()
            return nc
        esS = ExitStack()
        _e = esS.enter_context
        bcB = _e(nc.sbuf_tensor("bcB", [128, DS, ST], BF16))
        bcC = _e(nc.sbuf_tensor("bcC", [128, DS, ST], BF16))
        bcdt = _e(nc.sbuf_tensor("bcdt", [128, ST], F32))
        ustr = _e(nc.sbuf_tensor("ustr", [128, NCH, ST], F32))
        delta = _e(nc.sbuf_tensor("delta", [128, ST], F32))
        du = _e(nc.sbuf_tensor("du", [128, ST], F32))
        spr = _e(nc.sbuf_tensor("spr", [128, ST], F32))
        spl = _e(nc.sbuf_tensor("spl", [128, ST], F32))
        spl2 = _e(nc.sbuf_tensor("spl2", [128, ST], F32))
        dAb = _e(nc.sbuf_tensor("dAb", [128, 2, ST], F32))
        dBub = _e(nc.sbuf_tensor("dBub", [128, 2, ST], F32))
        hb = _e(nc.sbuf_tensor("hb", [128, 2, ST], F32))
        pr16 = _e(nc.sbuf_tensor("pr16", [128, 2, ST], BF16))
        y2s = _e(nc.sbuf_tensor("y2s", [128, 2, ST], F32))
        psy = _e(nc.psum_tensor("psy", [128, ST // 512, 512], F32))
        if True:
            import os as _osC
            CL = int(_osC.environ.get("CLIM", "0")) or NCH
            NYB = ST // 512
            sv_scan, sv_prod, sv_du, sv_y2 = {}, {}, {}, {}
            sa_dA, sa_carry, sa_mark = {}, {}, {}
            st_ymm, qg_y2, ql_strip = {}, {}, {}

            pv, pa, pt_, pg, pl = sV.c, sA.c, sT.c, qG.c, qL.c
            for si in range(NST):
                pl += 16 * (1 + 2 * DS + NCH)
                ql_strip[si] = pl
                for c in range(CL):
                    pa += 1
                    sa_mark[(si, c)] = pa
                    pv += 2
                    sv_du[(si, c)] = pv
                    for s in range(DS):
                        pa += 1
                        sa_dA[(si, c, s)] = pa
                        pv += 1
                        pv += 1
                        sv_scan[(si, c, s)] = pv
                        pv += 1
                        sv_prod[(si, c, s)] = pv
                        pa += 1
                        sa_carry[(si, c, s)] = pa
                        pt_ += NYB
                        st_ymm[(si, c, s)] = pt_
                    pv += 1
                    sv_y2[(si, c)] = pv
                    pg += 16
                    qg_y2[(si, c)] = pg

            for si in range(NST):
                t0 = si * ST

                @block.sync
                def _(e, si=si, t0=t0):
                    if si > 0:
                        e.wait_ge(sV.h, sv_y2[(si - 1, CL - 1)])
                        e.wait_ge(sA.h, sa_carry[(si - 1, CL - 1, DS - 1)])
                    else:
                        w(e, sA, sV)
                    qL.inc(e.dma_start(
                        out=bcdt[:],
                        in_=ssm0_d.ap()[:, t0:t0 + ST]
                            .partition_broadcast(128)), 16)
                    for s in range(DS):
                        qL.inc(e.dma_start(
                            out=bcB[:, s, :],
                            in_=ssmb_d.ap()[1 + s:2 + s, t0:t0 + ST]
                                .partition_broadcast(128)), 16)
                        qL.inc(e.dma_start(
                            out=bcC[:, s, :],
                            in_=ssmb_d.ap()[17 + s:18 + s, t0:t0 + ST]
                                .partition_broadcast(128)), 16)
                    for c in range(NCH):
                        qL.inc(e.dma_start(
                            out=ustr[:, c, :],
                            in_=u_d.ap()[c, :, t0:t0 + ST]), 16)
                    assert qL.c == ql_strip[si]

                @block.scalar
                def _(e, si=si):
                    e.wait_ge(qL.h, ql_strip[si])
                    for c in range(CL):
                        if c >= 1:
                            e.wait_ge(sV.h, sv_du[(si, c - 1)])
                        elif si >= 1:
                            e.wait_ge(sV.h, sv_du[(si - 1, CL - 1)])
                        e.activation(out=spr[:], in_=bcdt[:], func=AF.Relu,
                                     scale=wdtc[:, c:c + 1],
                                     bias=bdtc[:, c:c + 1])
                        e.activation(out=spl[:], in_=bcdt[:], func=AF.Abs,
                                     scale=wdtc[:, c:c + 1],
                                     bias=bdtc[:, c:c + 1])
                        e.activation(out=spl2[:], in_=spl[:], func=AF.Exp,
                                     scale=-1.0)
                        sA.inc(e.activation(out=spl[:], in_=spl2[:], func=AF.Ln,
                                            bias=1.0))
                        assert sA.c == sa_mark[(si, c)]
                        for s in range(DS):
                            if s >= 2:
                                e.wait_ge(sV.h, sv_scan[(si, c, s - 2)])
                            elif c >= 1:
                                e.wait_ge(sV.h, sv_scan[(si, c - 1, DS - 2 + s)])
                            elif si >= 1:
                                e.wait_ge(sV.h,
                                          sv_scan[(si - 1, CL - 1, DS - 2 + s)])
                            e.wait_ge(sV.h, sv_du[(si, c)])
                            sA.inc(e.activation(
                                out=dAb[:, s % 2, :], in_=delta[:], func=AF.Exp,
                                scale=ac[:, c * DS + s:c * DS + s + 1]))
                            assert sA.c == sa_dA[(si, c, s)]
                            e.wait_ge(sV.h, sv_scan[(si, c, s)])
                            sA.inc(e.copy(
                                out=hcarry[:, c * DS + s:c * DS + s + 1],
                                in_=hb[:, s % 2, ST - 1:ST]))
                            assert sA.c == sa_carry[(si, c, s)]

                @block.vector
                def _(e, si=si):
                    for c in range(CL):
                        e.wait_ge(sA.h, sa_mark[(si, c)])
                        sV.inc(e.tensor_add(out=delta[:], in0=spr[:],
                                            in1=spl[:]))
                        sV.inc(e.tensor_mul(out=du[:], in0=delta[:],
                                            in1=ustr[:, c, :]))
                        assert sV.c == sv_du[(si, c)]
                        for s in range(DS):
                            col = c * DS + s
                            sV.inc(e.tensor_tensor(out=dBub[:, s % 2, :],
                                                   in0=du[:],
                                                   in1=bcB[:, s, :],
                                                   op=ALU.mult))
                            e.wait_ge(sA.h, sa_dA[(si, c, s)])
                            if si == 0:
                                init = 0.0
                            else:
                                e.wait_ge(sA.h, sa_carry[(si - 1, c, s)])
                                init = hcarry[:, col:col + 1]
                            if s >= 2:
                                e.wait_ge(sT.h, st_ymm[(si, c, s - 2)])
                            elif c >= 1:
                                e.wait_ge(sT.h, st_ymm[(si, c - 1, DS - 2 + s)])
                            elif si >= 1:
                                e.wait_ge(sT.h,
                                          st_ymm[(si - 1, CL - 1, DS - 2 + s)])
                            sV.inc(e.tensor_tensor_scan(
                                out=hb[:, s % 2, :], data0=dAb[:, s % 2, :],
                                data1=dBub[:, s % 2, :], initial=init,
                                op0=ALU.mult, op1=ALU.add))
                            assert sV.c == sv_scan[(si, c, s)]
                            sV.inc(e.tensor_tensor(
                                out=pr16[:, s % 2, :], in0=hb[:, s % 2, :],
                                in1=bcC[:, s, :], op=ALU.mult))
                            assert sV.c == sv_prod[(si, c, s)]
                        e.wait_ge(sT.h, st_ymm[(si, c, DS - 1)])
                        if c >= 2:
                            e.wait_ge(qG.h, qg_y2[(si, c - 2)])
                        elif si >= 1 and CL - 2 + c >= 0:
                            e.wait_ge(qG.h, qg_y2[(si - 1, CL - 2 + c)])
                        sV.inc(e.scalar_tensor_tensor(
                            out=y2s[:, c % 2, :], in0=ustr[:, c, :],
                            scalar=dpc[:, c:c + 1],
                            in1=psy[:].rearrange("p a b -> p (a b)"),
                            op0=ALU.mult, op1=ALU.add))
                        assert sV.c == sv_y2[(si, c)]

                @block.tensor
                def _(e, si=si):
                    for c in range(CL):
                        if c >= 1:
                            e.wait_ge(sV.h, sv_y2[(si, c - 1)])
                        elif si >= 1:
                            e.wait_ge(sV.h, sv_y2[(si - 1, CL - 1)])
                        for s in range(DS):
                            e.wait_ge(sV.h, sv_prod[(si, c, s)])
                            for yb in range(NYB):
                                ins = e.matmul(
                                    psy[:, yb, :], id16[:],
                                    pr16[:, s % 2, yb * 512:(yb + 1) * 512],
                                    start=(s == 0), stop=(s == DS - 1))
                            sT.inc(ins, NYB)
                            assert sT.c == st_ymm[(si, c, s)]

                @block.gpsimd
                def _(e, si=si, t0=t0):
                    for c in range(CL):
                        e.wait_ge(sV.h, sv_y2[(si, c)])
                        qG.inc(e.dma_start(
                            out=y2_d.ap()[c, :, t0:t0 + ST],
                            in_=y2s[:, c % 2, :]), 16)
                        assert qG.c == qg_y2[(si, c)]

        if stop == "s":
            @block.gpsimd
            def _(e):
                w(e, sA, sV, sT)
                e.wait_ge(qG.h, qG.c)
            esS.close()
            es.close()
            return nc
        esS.close()
        esW = ExitStack()
        _e = esW.enter_context
        wo16 = _e(nc.sbuf_tensor("wo16", [128, NCH, DM], BF16))
        ywt = _e(nc.sbuf_tensor("ywt", [128, NCH, WST], F32))
        zwt = _e(nc.sbuf_tensor("zwt", [128, NCH, WST], F32))
        yg16 = _e(nc.sbuf_tensor("yg16", [128, NCH, WST], BF16))
        mixt = _e(nc.sbuf_tensor("mixt", [128, 2, 512], F32))
        psw = _e(nc.psum_tensor("psw", [128, 2, 512], F32))
        if True:
            with nc.sbuf_tensor("wotmp", [128, NCH, DM], F32) as wotmp:
                @block.sync
                def _(e):
                    w(e, sV, sA, sT)
                    qL.inc(e.dma_start(
                        out=wotmp[:],
                        in_=woutT.rearrange("(k p) n -> p k n", p=128)), 16)

                @block.vector
                def _(e):
                    w(e, qL)
                    sV.inc(e.tensor_copy(out=wo16[:], in_=wotmp[:]))

            for si in range(NWS):
                t0 = si * WST
                NW = 2 * (WST // 128)

                @block.sync
                def _(e, si=si, t0=t0):
                    w(e, sV, sT)
                    for c in range(NCH):
                        qL.inc(e.dma_start(out=ywt[:, c, :],
                                           in_=y2_d.ap()[c, :, t0:t0 + WST]),
                               16)
                        qL.inc(e.dma_start(out=zwt[:, c, :],
                                           in_=zsil_d.ap()[c, :, t0:t0 + WST]),
                               16)

                @block.vector
                def _(e, si=si):
                    w(e, qL)
                    for c in range(NCH):
                        ins = e.tensor_mul(out=yg16[:, c, :], in0=ywt[:, c, :],
                                           in1=zwt[:, c, :])
                    sV.inc(ins)

                @block.tensor
                def _(e, si=si):
                    w(e, sV)
                    saw = sA.c
                    for j in range(NW):
                        tsub, mh = j // 2, j % 2
                        if j >= 2:
                            e.wait_ge(sA.h, saw + j - 1)
                        for c in range(NCH):
                            ins = e.matmul(
                                psw[:, j % 2, :],
                                yg16[:, c, tsub * 128:(tsub + 1) * 128],
                                wo16[:, c, mh * 512:(mh + 1) * 512],
                                start=(c == 0), stop=(c == NCH - 1))
                        sT.inc(ins)

                @block.scalar
                def _(e, si=si):
                    base_t = sT.c - NW
                    qgw = qG.c
                    for j in range(NW):
                        e.wait_ge(sT.h, base_t + j + 1)
                        if j >= 2:
                            e.wait_ge(qG.h, qgw + 16 * (j - 1))
                        sA.inc(e.copy(out=mixt[:, j % 2, :],
                                      in_=psw[:, j % 2, :]))

                @block.gpsimd
                def _(e, si=si, t0=t0):
                    base_a = sA.c - NW

                    def arm(me):
                        for j in range(NW):
                            tsub, mh = j // 2, j % 2
                            e.wait_ge(sA.h, base_a + j + 1)
                            row = me * T + t0 + tsub * 128
                            e.dma_start(
                                out=mix_sh.ap()[row:row + 128,
                                                mh * 512:(mh + 1) * 512],
                                in_=mixt[:, j % 2, :]).then_inc(_qG, 16)

                    slot_branch(e, sslot, lambda: arm(0), lambda: arm(1))
                    qG.c += 16 * NW

            @block.gpsimd
            def _(e):
                w(e, qG)
                rpoll = rpoll_box["r"]

                def arm(me):
                    pt = 1 - me
                    e.dma_start(out=flag_sh.ap()[me, 1, :],
                                in_=snonce[:]).then_inc(_qG, 16)
                    e.wait_ge(_qG, qG.c + 16)
                    poll_partner(e, rpoll, None,
                                 flag_sh.ap()[pt, 1, :],
                                 sflag, snonce, _pp)

                slot_branch(e, sslot, lambda: arm(0), lambda: arm(1))
                qG.c += 16
                pm.inc(e.nop())

        if stop == "w":
            @block.gpsimd
            def _(e):
                w(e, sA, sV, sT)
                e.wait_ge(qG.h, qG.c)
                e.wait_ge(_pm, pm.c)
            esW.close()
            es.close()
            return nc
        esW.close()
        esR = ExitStack()
        _e = esR.enter_context
        rw1s = _e(nc.sbuf_tensor("rw1s", [128, 8, HF], BF16))
        rb1s = _e(nc.sbuf_tensor("rb1s", [128, HF // 128], F32))
        rw2s = _e(nc.sbuf_tensor("rw2s", [128, 32, NE], BF16))
        rb2s = _e(nc.sbuf_tensor("rb2s", [NE, 1], F32))
        lnA = _e(nc.sbuf_tensor("lnA", [128, 2, DM], F32))
        lnB = _e(nc.sbuf_tensor("lnB", [128, 2, DM], F32))
        lnc = _e(nc.sbuf_tensor("lnc", [128, 8], F32))
        pdif = _e(nc.sbuf_tensor("pdif", [128, 4], F32))
        pexp = _e(nc.sbuf_tensor("pexp", [128, 4], F32))
        x1b = _e(nc.sbuf_tensor("x1b", [128, 4, DM], BF16))
        x1T = _e(nc.sbuf_tensor("x1T", [128, 8, 512], BF16))
        h1b = _e(nc.sbuf_tensor("h1b", [128, 32, 512], BF16))
        lgct = _e(nc.sbuf_tensor("lgct", [NE, 512], F32))
        lgtm = _e(nc.sbuf_tensor("lgtm", [128, 4, NE], F32))
        tka = _e(nc.sbuf_tensor("tka", [128, NE], F32))
        tkb = _e(nc.sbuf_tensor("tkb", [128, NE], F32))
        ivt = _e(nc.sbuf_tensor("ivt", [128, 4, 4], F32))
        pst = _e(nc.psum_tensor("pst", [128, 2, 1024], BF16))
        psh = _e(nc.psum_tensor("psh", [128, 2, 512], F32))
        pslg = _e(nc.psum_tensor("pslg", [NE, 512], F32))
        pstf = _e(nc.psum_tensor("pstf", [128, 2, 512], F32))
        if True:
            with nc.sbuf_tensor("rwtmp", [128, HF], F32) as rwtmp:
                for k in range(8):
                    @block.sync
                    def _(e, k=k):
                        if k == 0:
                            e.wait_ge(pm.h, pm.c)
                            w(e, sV, sA, sT)
                        else:
                            w(e, sV)
                        qL.inc(e.dma_start(
                            out=rwtmp[:],
                            in_=rw1.rearrange("(k p) h -> p k h",
                                              p=128)[:, k, :]), 16)

                    @block.vector
                    def _(e, k=k):
                        w(e, qL)
                        sV.inc(e.tensor_copy(out=rw1s[:, k, :], in_=rwtmp[:]))

                @block.sync
                def _(e):
                    w(e, sV)
                    qL.inc(e.dma_start(
                        out=rwtmp[:, 0:32 * NE].rearrange(
                            "p (k n) -> p k n", k=32),
                        in_=rw2.rearrange("(k p) n -> p k n", p=128)), 16)
                    qL.inc(e.dma_start(out=rb1s[:], in_=rb1v), 16)
                    qL.inc(e.dma_start(out=rb2s[:], in_=rb2v), 16)

                @block.vector
                def _(e):
                    w(e, qL)
                    sV.inc(e.tensor_copy(
                        out=rw2s[:],
                        in_=rwtmp[:, 0:32 * NE]
                            .rearrange("p (k n) -> p k n", k=32)))

            for g in range(NRG):
                for tt in range(4):
                    tok0 = g * 512 + tt * 128

                    @block.sync
                    def _(e, tok0=tok0):
                        w(e, sV, qG, sT)

                        def arm(me):
                            pt = 1 - me
                            r0 = me * T + me * TH + tok0
                            r1 = pt * T + me * TH + tok0
                            e.dma_start(out=lnA[:, 0, :],
                                        in_=mix_sh.ap()[r0:r0 + 128, :]
                                        ).then_inc(_qL, 16)
                            e.dma_start(out=lnA[:, 1, :],
                                        in_=mix_sh.ap()[r1:r1 + 128, :]
                                        ).then_inc(_qL, 16)

                        slot_branch(e, sslot, lambda: arm(0), lambda: arm(1))
                        qL.c += 32
                        qL.inc(e.dma_start(
                            out=lnB[:, 0, :],
                            in_=x_tm[tok0:tok0 + 128, :]), 16)

                    @block.vector
                    def _(e):
                        w(e, qL)
                        sV.inc(e.tensor_add(out=lnA[:, 0, :],
                                            in0=lnA[:, 0, :],
                                            in1=lnA[:, 1, :]))
                        sV.inc(e.tensor_add(out=lnA[:, 0, :],
                                            in0=lnA[:, 0, :],
                                            in1=lnB[:, 0, :]))
                        sV.inc(e.reduce_sum(out=lnc[:, 0:1], in_=lnA[:, 0, :],
                                            axis=AX))
                        e.wait_ge(sV.h, sV.c)
                        sV.inc(e.tensor_scalar(out=lnc[:, 1:2], in0=lnc[:, 0:1],
                                               scalar1=1.0 / DM, scalar2=None,
                                               op0=ALU.mult))
                        e.wait_ge(sV.h, sV.c)
                        sV.inc(e.tensor_scalar(out=lnB[:, 1, :],
                                               in0=lnA[:, 0, :],
                                               scalar1=lnc[:, 1:2],
                                               scalar2=None,
                                               op0=ALU.subtract))
                        sV.inc(e.tensor_mul(out=lnA[:, 1, :],
                                            in0=lnB[:, 1, :],
                                            in1=lnB[:, 1, :]))
                        sV.inc(e.reduce_sum(out=lnc[:, 2:3], in_=lnA[:, 1, :],
                                            axis=AX))
                        e.wait_ge(sV.h, sV.c)
                        sV.inc(e.tensor_scalar(out=lnc[:, 3:4], in0=lnc[:, 2:3],
                                               scalar1=1.0 / DM, scalar2=None,
                                               op0=ALU.mult))

                    @block.scalar
                    def _(e):
                        w(e, sV)
                        sA.inc(e.activation(out=lnc[:, 4:5], in_=lnc[:, 3:4],
                                            func=AF.Ln, bias=lncst[:, 0:1]))
                        e.wait_ge(sA.h, sA.c)
                        sA.inc(e.activation(out=lnc[:, 5:6], in_=lnc[:, 4:5],
                                            func=AF.Exp, scale=lncst[:, 1:2]))

                    @block.vector
                    def _(e, tt=tt):
                        w(e, sA)
                        sV.inc(e.tensor_scalar(out=lnA[:, 0, :],
                                               in0=lnB[:, 1, :],
                                               scalar1=lnc[:, 5:6],
                                               scalar2=None,
                                               op0=ALU.mult))
                        e.wait_ge(sV.h, sV.c)
                        sV.inc(e.tensor_copy(out=x1b[:, tt, :],
                                             in_=lnA[:, 0, :]))

                    @block.gpsimd
                    def _(e, tok0=tok0, g=g, tt=tt):
                        w(e, sV)
                        qG.inc(e.dma_start(out=x1out[tok0:tok0 + 128, :],
                                           in_=lnA[:, 0, :]), 16)
                        if dbg and g == 0 and tt == 0:
                            qG.inc(e.dma_start(out=dbg_ln[:, 0:8],
                                               in_=lnc[:]), 16)
                            qG.inc(e.dma_start(out=dbg_ln[:, 8:8 + DM],
                                               in_=lnB[:, 1, :]), 16)
                            qG.inc(e.dma_start(out=dbg_ln[:, 8 + DM:],
                                               in_=lnA[:, 1, :]), 16)

                @block.tensor
                def _(e, g=g):
                    w(e, sV)
                    svw = sV.c
                    for j in range(32):
                        tt, mc = j // 8, j % 8
                        if j >= 2:
                            e.wait_ge(sV.h, svw + j - 1)
                        sT.inc(e.transpose(pst[:, j % 2, 0:128],
                                           x1b[:, tt, mc * 128:(mc + 1) * 128],
                                           id16[:]))

                @block.vector
                def _(e, g=g):
                    base_t = sT.c - 32
                    for j in range(32):
                        tt, mc = j // 8, j % 8
                        e.wait_ge(sT.h, base_t + j + 1)
                        sV.inc(e.tensor_copy(
                            out=x1T[:, mc, tt * 128:(tt + 1) * 128],
                            in_=pst[:, j % 2, 0:128]))

                @block.tensor
                def _(e, g=g):
                    w(e, sV)
                    saw = sA.c
                    for hc in range(32):
                        if hc >= 2:
                            e.wait_ge(sA.h, saw + hc - 1)
                        for k in range(8):
                            ins = e.matmul(psh[:, hc % 2, :],
                                           rw1s[:, k, hc * 128:(hc + 1) * 128],
                                           x1T[:, k, :],
                                           start=(k == 0), stop=(k == 7))
                        sT.inc(ins)

                @block.scalar
                def _(e, g=g):
                    base_t = sT.c - 32
                    for hc in range(32):
                        e.wait_ge(sT.h, base_t + hc + 1)
                        sA.inc(e.activation(out=h1b[:, hc, :],
                                            in_=psh[:, hc % 2, :],
                                            func=AF.Relu,
                                            bias=rb1s[:, hc:hc + 1]))

                @block.tensor
                def _(e, g=g):
                    base_a = sA.c - 32
                    for k2 in range(32):
                        e.wait_ge(sA.h, base_a + k2 + 1)
                        ins = e.matmul(pslg[:], rw2s[:, k2, :], h1b[:, k2, :],
                                       start=(k2 == 0), stop=(k2 == 31))
                    sT.inc(ins)

                @block.vector
                def _(e, g=g):
                    w(e, sT)
                    sV.inc(e.tensor_scalar(out=lgct[:], in0=pslg[:],
                                           scalar1=rb2s[:, 0:1], scalar2=None,
                                           op0=ALU.add))

                @block.tensor
                def _(e, g=g):
                    w(e, sV)
                    svw = sV.c
                    for tt in range(4):
                        if tt >= 2:
                            e.wait_ge(sV.h, svw + tt - 1)
                        sT.inc(e.transpose(pstf[:, tt % 2, 0:NE],
                                           lgct[:, tt * 128:(tt + 1) * 128],
                                           idf[0:NE, 0:NE]))

                @block.vector
                def _(e, g=g):
                    base_t = sT.c - 4
                    for tt in range(4):
                        e.wait_ge(sT.h, base_t + tt + 1)
                        sV.inc(e.tensor_copy(out=lgtm[:, tt, :],
                                             in_=pstf[:, tt % 2, 0:NE]))

                @block.vector
                def _(e, g=g):
                    def b(ins):
                        sV.inc(ins)
                        e.wait_ge(sV.h, sV.c)
                    for tt in range(4):
                        lg = lgtm[:, tt, :]
                        b(e.reduce_max(out=lnc[:, 0:1], in_=lg, axis=AX))
                        b(e.tensor_scalar(out=tka[:], in0=lg,
                                          scalar1=lnc[:, 0:1],
                                          scalar2=None, op0=ALU.is_ge))
                        b(e.scalar_tensor_tensor(out=tka[:], in0=tka[:],
                                                 scalar=-1024.0, in1=io8b[:],
                                                 op0=ALU.mult, op1=ALU.add))
                        b(e.tensor_reduce(out=lnc[:, 1:2], in_=tka[:], axis=AX,
                                          op=ALU.min))
                        b(e.tensor_scalar(out=ivt[:, tt, 0:1], in0=lnc[:, 1:2],
                                          scalar1=1024.0, scalar2=None,
                                          op0=ALU.add))
                        b(e.tensor_scalar(out=tka[:], in0=io8b[:],
                                          scalar1=ivt[:, tt, 0:1], scalar2=None,
                                          op0=ALU.is_equal))
                        b(e.scalar_tensor_tensor(out=tkb[:], in0=tka[:],
                                                 scalar=-2e30, in1=lg,
                                                 op0=ALU.mult, op1=ALU.add))
                        b(e.reduce_max(out=lnc[:, 2:3], in_=tkb[:], axis=AX))
                        b(e.tensor_scalar(out=tka[:], in0=tkb[:],
                                          scalar1=lnc[:, 2:3], scalar2=None,
                                          op0=ALU.is_ge))
                        b(e.scalar_tensor_tensor(out=tka[:], in0=tka[:],
                                                 scalar=-1024.0, in1=io8b[:],
                                                 op0=ALU.mult, op1=ALU.add))
                        b(e.tensor_reduce(out=lnc[:, 3:4], in_=tka[:], axis=AX,
                                          op=ALU.min))
                        b(e.tensor_scalar(out=ivt[:, tt, 1:2], in0=lnc[:, 3:4],
                                          scalar1=1024.0, scalar2=None,
                                          op0=ALU.add))
                        b(e.tensor_sub(out=pdif[:, tt:tt + 1],
                                       in0=lnc[:, 2:3], in1=lnc[:, 0:1]))

                @block.scalar
                def _(e, g=g):
                    w(e, sV)
                    sA.inc(e.activation(out=pexp[:], in_=pdif[:], func=AF.Exp))

                @block.vector
                def _(e, g=g):
                    w(e, sA)
                    sV.inc(e.tensor_scalar(out=tka[:, 0:4], in0=pexp[:],
                                           scalar1=1.0, scalar2=None,
                                           op0=ALU.add))
                    e.wait_ge(sV.h, sV.c)
                    sV.inc(e.reciprocal(out=pdif[:], in_=tka[:, 0:4]))
                    e.wait_ge(sV.h, sV.c)
                    for tt in range(4):
                        e.tensor_copy(out=ivt[:, tt, 2:3],
                                      in_=pdif[:, tt:tt + 1])
                        ins = e.tensor_mul(out=ivt[:, tt, 3:4],
                                           in0=pexp[:, tt:tt + 1],
                                           in1=pdif[:, tt:tt + 1])
                    sV.inc(ins)

                @block.gpsimd
                def _(e, g=g):
                    w(e, sV)
                    for tt in range(4):
                        tok0 = g * 512 + tt * 128
                        qG.inc(e.dma_start(out=ivout[tok0:tok0 + 128, :],
                                           in_=ivt[:, tt, :]), 16)

            if dbg:
                @block.sync
                def _(e):
                    w(e, qG, sA, sV)
                    qL.inc(e.dma_start(out=dbg_u, in_=u_d.ap()), 16)
                    qL.inc(e.dma_start(out=dbg_z, in_=zsil_d.ap()), 16)
                    qL.inc(e.dma_start(out=dbg_y2, in_=y2_d.ap()), 16)
                    qL.inc(e.dma_start(out=dbg_mix, in_=mix_sh.ap()), 16)
                    e.wait_ge(qL.h, qL.c)

            @block.gpsimd
            def _(e):
                e.wait_ge(qG.h, qG.c)

        esR.close()
    es.close()
    return nc


def build_L2(N=32768, stop='full'):
    NK = 2 * N
    NSTRIP = 16
    SW = NK // NSTRIP
    nc = bass.Bass("TRN2", target_bir_lowering=False, debug=False,
                   detect_race_conditions=False)

    ein = lambda n, s, d=F32: nc.dram_tensor(n, s, d, kind="ExternalInput").ap()
    x1f = ein("x1f", [N, DM])
    idsf = ein("idsf", [1, NK])
    tokv = ein("tokv", [1, NK], I32)
    ew1in = ein("ew1in", [DM, HF])
    ew2in = ein("ew2in", [HF, DM])
    eb1v = ein("eb1v", [128, 32])
    eb2v = ein("eb2v", [1, DM])
    emodv = ein("emodv", [128, 1])
    spatv = ein("spatv", [128, 128])
    selv = ein("selv", [128, 16])
    identv = ein("identv", [128, 128])

    eoout = nc.dram_tensor("eoout", [CAP, DM], F32, kind="ExternalOutput").ap()

    slot_d = nc.dram_tensor("slot_d", [NK, 1], I32)
    src_d = nc.dram_tensor("src_d", [CAP + 128, 1], I32)
    disp_d = nc.dram_tensor("disp_d", [CAP, DM], BF16)
    ew1b_d = nc.dram_tensor("ew1b_d", [8, 128, HF], BF16)
    ew2b_d = nc.dram_tensor("ew2b_d", [32, 128, DM], BF16)

    es = ExitStack()
    _e = es.enter_context
    emod = _e(nc.sbuf_tensor("emod", [128, 1], F32))
    spat = _e(nc.sbuf_tensor("spat", [128, 128], F32))
    sel = _e(nc.sbuf_tensor("sel", [128, 16], F32))
    id16 = _e(nc.sbuf_tensor("id16", [128, 128], BF16))
    idf = _e(nc.sbuf_tensor("idf", [128, 128], F32))
    eb1s = _e(nc.sbuf_tensor("eb1s", [128, 32], F32))
    eb2b = _e(nc.sbuf_tensor("eb2b", [128, DM], F32))
    _qL = _e(nc.semaphore("qL")); _qG = _e(nc.semaphore("qG"))
    _sV = _e(nc.semaphore("sV")); _sA = _e(nc.semaphore("sA"))
    _sT = _e(nc.semaphore("sT"))
    block = _e(nc.Block())
    if True:
        qL, qG, sV, sA, sT = Sem(_qL), Sem(_qG), Sem(_sV), Sem(_sA), Sem(_sT)
        rbnd_box = {}

        @block.gpsimd
        def _(e):
            r1 = e.alloc_register(rn("rbndcap"))
            e.reg_mov(r1, CAP - 1)
            r2 = e.alloc_register(rn("rbndn"))
            e.reg_mov(r2, N - 1)
            rbnd_box["cap"] = r1
            rbnd_box["n"] = r2

        @block.sync
        def _(e):
            for t_, src in ((emod, emodv), (spat, spatv), (sel, selv),
                            (idf, identv), (eb1s, eb1v)):
                qL.inc(e.dma_start(out=t_[:], in_=src), 16)
            qL.inc(e.dma_start(out=eb2b[:],
                               in_=eb2v.partition_broadcast(128)), 16)

        @block.vector
        def _(e):
            w(e, qL)
            sV.inc(e.tensor_copy(out=id16[:], in_=idf[:]))

        with nc.sbuf_tensor("wt32", [128, 2, HF], F32) as wt32, \
             nc.sbuf_tensor("wt16", [128, 2, HF], BF16) as wt16:
            w1v = ew1in.rearrange("(k p) h -> p k h", p=128)
            w2v = ew2in.rearrange("(k p) m -> p k m", p=128)
            seq = [(w1v, ew1b_d, HF, k) for k in range(8)] + \
                  [(w2v, ew2b_d, DM, k) for k in range(32)]
            ldq, csv, stg = {}, {}, {}
            pl_, pv_, pg_ = qL.c, sV.c, qG.c
            for j in range(len(seq)):
                pl_ += 16; ldq[j] = pl_
                pv_ += 1; csv[j] = pv_
                pg_ += 16; stg[j] = pg_
            for j, (src, dst, cols, k) in enumerate(seq):
                @block.sync
                def _(e, j=j, src=src, cols=cols, k=k):
                    if j == 0:
                        w(e, sV, qG)
                    if j >= 2:
                        e.wait_ge(sV.h, csv[j - 2])
                    qL.inc(e.dma_start(out=wt32[:, j % 2, 0:cols],
                                       in_=src[:, k, :]), 16)
                    assert qL.c == ldq[j]

                @block.vector
                def _(e, j=j, cols=cols):
                    e.wait_ge(qL.h, ldq[j])
                    if j >= 2:
                        e.wait_ge(qG.h, stg[j - 2])
                    sV.inc(e.tensor_copy(out=wt16[:, j % 2, 0:cols],
                                         in_=wt32[:, j % 2, 0:cols]))
                    assert sV.c == csv[j]

                @block.gpsimd
                def _(e, j=j, dst=dst, cols=cols, k=k):
                    e.wait_ge(sV.h, csv[j])
                    qG.inc(e.dma_start(out=dst.ap()[k, :, :],
                                       in_=wt16[:, j % 2, 0:cols]), 16)
                    assert qG.c == stg[j]

        esr = ExitStack()
        _e2 = esr.enter_context
        idt = _e2(nc.sbuf_tensor("idt", [128, SW], F32))
        ones = _e2(nc.sbuf_tensor("ones", [128, SW], F32))
        oneh = _e2(nc.sbuf_tensor("oneh", [128, SW], F32))
        cums = _e2(nc.sbuf_tensor("cums", [128, SW], F32))
        ocol = _e2(nc.sbuf_tensor("ocol", [128, 1], F32))
        mpos = _e2(nc.sbuf_tensor("mpos", [16, SW], F32))
        mhot = _e2(nc.sbuf_tensor("mhot", [16, SW], F32))
        slotf = _e2(nc.sbuf_tensor("slotf", [16, SW], F32))
        sloti = _e2(nc.sbuf_tensor("sloti", [16, SW], I32))
        ztile = _e2(nc.sbuf_tensor("ztile", [1, CAP + 128], I32))
        pso = _e2(nc.psum_tensor("pso", [128, 1], F32))
        psel = _e2(nc.psum_tensor("psel", [16, 512], F32))
        if True:
            @block.sync
            def _(e):
                w(e, sV, qG)
                for st in range(NSTRIP):
                    qL.inc(e.dma_start(
                        out=idt[st * 8:(st + 1) * 8, :],
                        in_=idsf[:, st * SW:(st + 1) * SW]
                            .partition_broadcast(8)), 16)

            @block.vector
            def _(e):
                w(e, qL)
                sV.inc(e.memset(ones[:], 1.0))
                e.wait_ge(sV.h, sV.c)
                sV.inc(e.tensor_scalar(out=oneh[:], in0=idt[:],
                                       scalar1=emod[:, 0:1],
                                       scalar2=None, op0=ALU.is_equal))
                e.wait_ge(sV.h, sV.c)
                sV.inc(e.tensor_tensor_scan(out=cums[:], data0=ones[:],
                                            data1=oneh[:], initial=0.0,
                                            op0=ALU.mult, op1=ALU.add))

            @block.tensor
            def _(e):
                w(e, sV)
                sT.inc(e.matmul(pso[:], spat[:], cums[:, SW - 1:SW],
                                start=True, stop=True))

            @block.vector
            def _(e):
                w(e, sT)
                sV.inc(e.tensor_copy(out=ocol[:], in_=pso[:]))
                e.wait_ge(sV.h, sV.c)
                sV.inc(e.tensor_scalar(out=cums[:], in0=cums[:],
                                       scalar1=ocol[:, 0:1], scalar2=None,
                                       op0=ALU.add))
                e.wait_ge(sV.h, sV.c)
                sV.inc(e.tensor_mul(out=ones[:], in0=oneh[:], in1=cums[:]))

            @block.tensor
            def _(e):
                w(e, sV)
                for j in range(SW // 512):
                    sT.inc(e.matmul(psel[:], sel[:],
                                    ones[:, j * 512:(j + 1) * 512],
                                    start=True, stop=True))
                    e.wait_ge(sV.h, sV.c + j + 1)
                for j in range(SW // 512):
                    sT.inc(e.matmul(psel[:], sel[:],
                                    oneh[:, j * 512:(j + 1) * 512],
                                    start=True, stop=True))
                    e.wait_ge(sV.h, sV.c + SW // 512 + j + 1)

            @block.vector
            def _(e):
                base_t = sT.c - 2 * (SW // 512)
                for j in range(SW // 512):
                    e.wait_ge(sT.h, base_t + j + 1)
                    sV.inc(e.tensor_copy(out=mpos[:, j * 512:(j + 1) * 512],
                                         in_=psel[:]))
                for j in range(SW // 512):
                    e.wait_ge(sT.h, base_t + SW // 512 + j + 1)
                    sV.inc(e.tensor_copy(out=mhot[:, j * 512:(j + 1) * 512],
                                         in_=psel[:]))

            @block.vector
            def _(e):
                w(e, sV)
                def b(ins):
                    sV.inc(ins)
                    e.wait_ge(sV.h, sV.c)
                b(e.tensor_scalar(out=mpos[:], in0=mpos[:], scalar1=1.0,
                                  scalar2=None, op0=ALU.subtract))
                b(e.tensor_scalar(out=slotf[:], in0=mpos[:],
                                  scalar1=float(CAP),
                                  scalar2=None, op0=ALU.is_lt))
                b(e.tensor_mul(out=slotf[:], in0=slotf[:], in1=mhot[:]))
                b(e.tensor_mul(out=mpos[:], in0=mpos[:], in1=slotf[:]))
                b(e.tensor_scalar(out=slotf[:], in0=slotf[:], scalar1=-1.0,
                                  scalar2=1.0, op0=ALU.mult, op1=ALU.add))
                b(e.scalar_tensor_tensor(out=slotf[:], in0=slotf[:],
                                         scalar=BIGI * 1.0, in1=mpos[:],
                                         op0=ALU.mult, op1=ALU.add))
                b(e.tensor_copy(out=sloti[:], in_=slotf[:]))
                sV.inc(e.memset(ztile[:], 0))

            @block.gpsimd
            def _(e):
                w(e, sV)
                qG.inc(e.dma_start(
                    out=slot_d.ap().rearrange("(s f) b -> s (f b)", s=16),
                    in_=sloti[:]), 16)
                qG.inc(e.dma_start(out=src_d.ap().rearrange("a b -> b a"),
                                   in_=ztile[:]), 16)

        if stop == "r":
            @block.gpsimd
            def _(e):
                w(e, sA, sV, sT)
                e.wait_ge(qG.h, qG.c)
            esr.close()
            es.close()
            return nc
        esr.close()
        NG8 = NK // 1024
        ess = ExitStack()
        ic8 = ess.enter_context(nc.sbuf_tensor("ic8", [128, 2, 8], I32))
        vc8 = ess.enter_context(nc.sbuf_tensor("vc8", [128, 2, 8], I32))
        if True:
            slot_v = slot_d.ap().rearrange("(G p f) b -> p G (f b)",
                                           p=128, f=8)
            tok_v = tokv.rearrange("a (G p f) -> a p G f", p=128, f=8)
            for g8 in range(NG8):
                @block.sync
                def _(e, g8=g8):
                    if g8 == 0:
                        w(e, qG)
                    elif g8 >= 2:
                        e.wait_ge(qG.h, qG.c - 16 * 8)
                    qL.inc(e.dma_start(
                        out=ic8[:, g8 % 2, :],
                        in_=slot_v[:, g8, :]), 16)
                    qL.inc(e.dma_start(
                        out=vc8[:, g8 % 2, :],
                        in_=tok_v[0, :, g8, :]), 16)

                @block.gpsimd
                def _(e, g8=g8):
                    w(e, qL)
                    for f in range(8):
                        qG.inc(e.indirect_dma_start(
                            out=src_d.ap(),
                            out_offset=IndirectOffsetOnAxis(
                                ap=ic8[:, g8 % 2, f:f + 1], axis=0),
                            in_=vc8[:, g8 % 2, f:f + 1], in_offset=None,
                            bounds_check=rbnd_box["cap"], oob_is_err=False),
                            16)

        if stop == "sc":
            @block.gpsimd
            def _(e):
                w(e, sA, sV, sT)
                e.wait_ge(qG.h, qG.c)
            ess.close()
            es.close()
            return nc
        ess.close()
        esg = ExitStack()
        sidx = esg.enter_context(nc.sbuf_tensor("sidx", [128, 32], I32))
        xg = esg.enter_context(nc.sbuf_tensor("xg", [128, 2, DM], F32))
        xg16 = esg.enter_context(nc.sbuf_tensor("xg16", [128, 2, DM], BF16))
        if True:
            @block.sync
            def _(e):
                w(e, qG)
                for sg in range(32):
                    qL.inc(e.dma_start(
                        out=sidx[:, sg:sg + 1],
                        in_=src_d.ap()[sg * 128:(sg + 1) * 128, :]), 16)

            with nc.sbuf_tensor("eq0", [128, 32], I32) as eq0:
                @block.vector
                def _(e):
                    w(e, qL)
                    sV.inc(e.tensor_scalar(out=eq0[:], in0=sidx[:], scalar1=0,
                                           scalar2=None, op0=ALU.is_equal))
                    e.wait_ge(sV.h, sV.c)
                    sV.inc(e.scalar_tensor_tensor(out=sidx[:], in0=eq0[:],
                                                  scalar=BIGI, in1=sidx[:],
                                                  op0=ALU.mult, op1=ALU.add))
                    e.wait_ge(sV.h, sV.c)
                    sV.inc(e.tensor_scalar(out=sidx[:], in0=sidx[:], scalar1=1,
                                           scalar2=None, op0=ALU.subtract))

            for sg in range(32):
                @block.vector
                def _(e, sg=sg):
                    if sg >= 2:
                        w(e, qG)
                    sV.inc(e.memset(xg[:, sg % 2, :], 0.0))

                @block.gpsimd
                def _(e, sg=sg):
                    w(e, sV)
                    qG.inc(e.indirect_dma_start(
                        out=xg[:, sg % 2, :], out_offset=None,
                        in_=x1f,
                        in_offset=IndirectOffsetOnAxis(ap=sidx[:, sg:sg + 1],
                                                       axis=0),
                        bounds_check=rbnd_box["n"], oob_is_err=False), 16)

                @block.vector
                def _(e, sg=sg):
                    w(e, qG)
                    sV.inc(e.tensor_copy(out=xg16[:, sg % 2, :],
                                         in_=xg[:, sg % 2, :]))

                @block.gpsimd
                def _(e, sg=sg):
                    w(e, sV)
                    qG.inc(e.dma_start(
                        out=disp_d.ap()[sg * 128:(sg + 1) * 128, :],
                        in_=xg16[:, sg % 2, :]), 16)

        if stop == "g":
            @block.gpsimd
            def _(e):
                w(e, sA, sV, sT)
                e.wait_ge(qG.h, qG.c)
            esg.close()
            es.close()
            return nc
        esg.close()
        esx = ExitStack()
        _e3 = esx.enter_context
        ew1s = _e3(nc.sbuf_tensor("ew1s", [128, 8, HF], BF16))
        dtile = _e3(nc.sbuf_tensor("dtile", [128, 4, DM], BF16))
        dT = _e3(nc.sbuf_tensor("dT", [128, 8, 512], BF16))
        h1s = _e3(nc.sbuf_tensor("h1s", [128, 32, 512], BF16))
        ew2r = _e3(nc.sbuf_tensor("ew2r", [128, 32, DM], BF16))
        eot = _e3(nc.sbuf_tensor("eot", [128, 2, 512], F32))
        pst2 = _e3(nc.psum_tensor("pst2", [128, 2, 128], BF16))
        psh2 = _e3(nc.psum_tensor("psh2", [128, 2, 512], F32))
        pse = _e3(nc.psum_tensor("pse", [128, 2, 512], F32))
        if True:
            @block.sync
            def _(e):
                w(e, qG)
                for k in range(8):
                    qL.inc(e.dma_start(out=ew1s[:, k, :],
                                       in_=ew1b_d.ap()[k, :, :]), 16)
                for k2 in range(32):
                    qL.inc(e.dma_start(out=ew2r[:, k2, :],
                                       in_=ew2b_d.ap()[k2, :, :]), 16)

            for st in range(8):
                @block.sync
                def _(e, st=st):
                    w(e, sT)
                    for j in range(4):
                        qL.inc(e.dma_start(
                            out=dtile[:, j, :],
                            in_=disp_d.ap()[st * 512 + j * 128:
                                            st * 512 + (j + 1) * 128, :]), 16)

                @block.sync
                def _(e, st=st):
                    w(e, sT, qL)
                    for j in range(32):
                        tt, mc = j // 8, j % 8
                        qL.inc(e.dma_start(
                            out=dT[:, mc, tt * 128:(tt + 1) * 128],
                            in_=dtile[:, tt, mc * 128:(mc + 1) * 128],
                            transpose=True), 16)

                @block.tensor
                def _(e, st=st):
                    w(e, sV, qL)
                    saw = sA.c
                    for hc in range(32):
                        if hc >= 2:
                            e.wait_ge(sA.h, saw + hc - 1)
                        for k in range(8):
                            ins = e.matmul(psh2[:, hc % 2, :],
                                           ew1s[:, k, hc * 128:(hc + 1) * 128],
                                           dT[:, k, :],
                                           start=(k == 0), stop=(k == 7))
                        sT.inc(ins)

                @block.scalar
                def _(e, st=st):
                    base_t = sT.c - 32
                    for hc in range(32):
                        e.wait_ge(sT.h, base_t + hc + 1)
                        sA.inc(e.activation(out=h1s[:, hc, :],
                                            in_=psh2[:, hc % 2, :],
                                            func=AF.Relu,
                                            bias=eb1s[:, hc:hc + 1]))

                for mh in range(2):
                    @block.tensor
                    def _(e, st=st, mh=mh):
                        w(e, qL, sA)
                        svw = sV.c
                        for ss in range(4):
                            if ss >= 2:
                                e.wait_ge(sV.h, svw + ss - 1)
                            for k2 in range(32):
                                ins = e.matmul(
                                    pse[:, ss % 2, :],
                                    h1s[:, k2, ss * 128:(ss + 1) * 128],
                                    ew2r[:, k2, mh * 512:(mh + 1) * 512],
                                    start=(k2 == 0), stop=(k2 == 31))
                            sT.inc(ins)

                    @block.vector
                    def _(e, st=st, mh=mh):
                        base_t = sT.c - 4
                        for ss in range(4):
                            e.wait_ge(sT.h, base_t + ss + 1)
                            if ss >= 2:
                                w(e, qG)
                            sV.inc(e.tensor_add(
                                out=eot[:, ss % 2, :], in0=pse[:, ss % 2, :],
                                in1=eb2b[:, mh * 512:(mh + 1) * 512]))

                    @block.gpsimd
                    def _(e, st=st, mh=mh):
                        base_v = sV.c - 4
                        for ss in range(4):
                            e.wait_ge(sV.h, base_v + ss + 1)
                            qG.inc(e.dma_start(
                                out=eoout[st * 512 + ss * 128:
                                          st * 512 + (ss + 1) * 128,
                                          mh * 512:(mh + 1) * 512],
                                in_=eot[:, ss % 2, :]), 16)

            @block.gpsimd
            def _(e):
                w(e, sA, sV, sT, qL)
                e.wait_ge(qG.h, qG.c)

        esx.close()
    es.close()
    return nc


def build_L3(N=32768, NTOK=4096):
    NK = 2 * N
    NSTRIP = 16
    SW = NK // NSTRIP
    nc = bass.Bass("TRN2", target_bir_lowering=False, debug=False,
                   detect_race_conditions=False)

    ein = lambda n, s, d=F32: nc.dram_tensor(n, s, d, kind="ExternalInput").ap()
    eof = ein("eof", [NE * CAP, DM])
    x1s = ein("x1s", [NTOK, DM])
    idsf = ein("idsf", [1, NK])
    p0s = ein("p0s", [16, N // 16])
    p1s = ein("p1s", [16, N // 16])
    emodv = ein("emodv", [128, 1])
    spatv = ein("spatv", [128, 128])
    sumpv = ein("sumpv", [128, 16])
    selpv = ein("selpv", [128, 16])
    stripev = ein("stripev", [1, 1], I32)
    iotacv = ein("iotacv", [128, 1], I32)
    lncv3 = ein("lncv3", [128, 2])

    out = nc.dram_tensor("out", [NTOK, DM], F32, kind="ExternalOutput").ap()

    idx0_d = nc.dram_tensor("idx0_d", [N, 1], I32)
    idx1_d = nc.dram_tensor("idx1_d", [N, 1], I32)
    pk0_d = nc.dram_tensor("pk0_d", [N, 1], F32)
    pk1_d = nc.dram_tensor("pk1_d", [N, 1], F32)

    es = ExitStack()
    _e = es.enter_context
    emod = _e(nc.sbuf_tensor("emod", [128, 1], F32))
    spat = _e(nc.sbuf_tensor("spat", [128, 128], F32))
    sump = _e(nc.sbuf_tensor("sump", [128, 16], F32))
    selp = _e(nc.sbuf_tensor("selp", [128, 16], F32))
    lncst3 = _e(nc.sbuf_tensor("lncst3", [128, 2], F32))
    _qL = _e(nc.semaphore("qL")); _qG = _e(nc.semaphore("qG"))
    _sV = _e(nc.semaphore("sV")); _sA = _e(nc.semaphore("sA"))
    _sT = _e(nc.semaphore("sT"))
    block = _e(nc.Block())
    if True:
        qL, qG, sV, sA, sT = Sem(_qL), Sem(_qG), Sem(_sV), Sem(_sA), Sem(_sT)

        @block.sync
        def _(e):
            for t_, src in ((emod, emodv), (spat, spatv), (sump, sumpv),
                            (selp, selpv), (lncst3, lncv3)):
                qL.inc(e.dma_start(out=t_[:], in_=src), 16)

        esr = ExitStack()
        _e2 = esr.enter_context
        idt = _e2(nc.sbuf_tensor("idt", [128, SW], F32))
        ones = _e2(nc.sbuf_tensor("ones", [128, SW], F32))
        oneh = _e2(nc.sbuf_tensor("oneh", [128, SW], F32))
        cums = _e2(nc.sbuf_tensor("cums", [128, SW], F32))
        ocol = _e2(nc.sbuf_tensor("ocol", [128, 1], F32))
        post = _e2(nc.sbuf_tensor("post", [16, SW], F32))
        ids16 = _e2(nc.sbuf_tensor("ids16", [16, SW], F32))
        keep = _e2(nc.sbuf_tensor("keep", [16, SW], F32))
        dei = _e2(nc.sbuf_tensor("dei", [16, SW], F32))
        deii = _e2(nc.sbuf_tensor("deii", [16, SW], I32))
        pso = _e2(nc.psum_tensor("pso", [128, 1], F32))
        psel = _e2(nc.psum_tensor("psel", [16, 512], F32))
        if True:
            @block.sync
            def _(e):
                w(e, qL)
                for st in range(NSTRIP):
                    qL.inc(e.dma_start(
                        out=idt[st * 8:(st + 1) * 8, :],
                        in_=idsf[:, st * SW:(st + 1) * SW]
                            .partition_broadcast(8)), 16)

            @block.vector
            def _(e):
                w(e, qL)
                sV.inc(e.memset(ones[:], 1.0))
                e.wait_ge(sV.h, sV.c)
                sV.inc(e.tensor_scalar(out=oneh[:], in0=idt[:],
                                       scalar1=emod[:, 0:1],
                                       scalar2=None, op0=ALU.is_equal))
                e.wait_ge(sV.h, sV.c)
                sV.inc(e.tensor_tensor_scan(out=cums[:], data0=ones[:],
                                            data1=oneh[:], initial=0.0,
                                            op0=ALU.mult, op1=ALU.add))

            @block.tensor
            def _(e):
                w(e, sV)
                sT.inc(e.matmul(pso[:], spat[:], cums[:, SW - 1:SW],
                                start=True, stop=True))

            @block.vector
            def _(e):
                w(e, sT)
                sV.inc(e.tensor_copy(out=ocol[:], in_=pso[:]))
                e.wait_ge(sV.h, sV.c)
                sV.inc(e.tensor_scalar(out=cums[:], in0=cums[:],
                                       scalar1=ocol[:, 0:1],
                                       scalar2=None, op0=ALU.add))
                e.wait_ge(sV.h, sV.c)
                sV.inc(e.tensor_mul(out=ones[:], in0=oneh[:], in1=cums[:]))

            @block.tensor
            def _(e):
                w(e, sV)
                for j in range(SW // 512):
                    sT.inc(e.matmul(psel[:], sump[:],
                                    ones[:, j * 512:(j + 1) * 512],
                                    start=True, stop=True))
                    e.wait_ge(sV.h, sV.c + j + 1)
                for j in range(SW // 512):
                    sT.inc(e.matmul(psel[:], selp[:],
                                    idt[:, j * 512:(j + 1) * 512],
                                    start=True, stop=True))
                    e.wait_ge(sV.h, sV.c + SW // 512 + j + 1)

            @block.vector
            def _(e):
                base_t = sT.c - 2 * (SW // 512)
                for j in range(SW // 512):
                    e.wait_ge(sT.h, base_t + j + 1)
                    sV.inc(e.tensor_copy(out=post[:, j * 512:(j + 1) * 512],
                                         in_=psel[:]))
                for j in range(SW // 512):
                    e.wait_ge(sT.h, base_t + SW // 512 + j + 1)
                    sV.inc(e.tensor_copy(out=ids16[:, j * 512:(j + 1) * 512],
                                         in_=psel[:]))

            @block.vector
            def _(e):
                w(e, sV)
                def b(ins):
                    sV.inc(ins)
                    e.wait_ge(sV.h, sV.c)
                b(e.tensor_scalar(out=post[:], in0=post[:], scalar1=1.0,
                                  scalar2=None, op0=ALU.subtract))
                b(e.tensor_scalar(out=keep[:], in0=post[:], scalar1=float(CAP),
                                  scalar2=None, op0=ALU.is_lt))
                b(e.tensor_scalar(out=post[:], in0=post[:],
                                  scalar1=float(CAP - 1),
                                  scalar2=None, op0=ALU.min))
                b(e.scalar_tensor_tensor(out=post[:], in0=ids16[:],
                                         scalar=float(CAP), in1=post[:],
                                         op0=ALU.mult, op1=ALU.add))

            @block.vector
            def _(e):
                sV.inc(e.tensor_copy(out=dei[:, 0:SW // 2],
                                     in_=post[:].rearrange("p (f k) -> p k f",
                                                           k=2)[:, 0, :]))
                e.wait_ge(sV.h, sV.c)
                sV.inc(e.tensor_copy(out=deii[:, 0:SW // 2],
                                     in_=dei[:, 0:SW // 2]))

            @block.gpsimd
            def _(e):
                w(e, sV)
                qG.inc(e.dma_start(
                    out=idx0_d.ap().rearrange("(s f) b -> s (f b)", s=16),
                    in_=deii[:, 0:SW // 2]), 16)

            @block.vector
            def _(e):
                w(e, qG)
                sV.inc(e.tensor_copy(out=dei[:, 0:SW // 2],
                                     in_=post[:].rearrange("p (f k) -> p k f",
                                                           k=2)[:, 1, :]))
                e.wait_ge(sV.h, sV.c)
                sV.inc(e.tensor_copy(out=deii[:, 0:SW // 2],
                                     in_=dei[:, 0:SW // 2]))

            @block.gpsimd
            def _(e):
                w(e, sV, qG)
                qG.inc(e.dma_start(
                    out=idx1_d.ap().rearrange("(s f) b -> s (f b)", s=16),
                    in_=deii[:, 0:SW // 2]), 16)

            pin = esr.enter_context(nc.sbuf_tensor("pin", [16, SW // 2], F32))
            if True:
                @block.sync
                def _(e):
                    w(e, sV)
                    qL.inc(e.dma_start(out=pin[:], in_=p0s), 16)

                @block.vector
                def _(e):
                    w(e, qL, qG)
                    sV.inc(e.tensor_copy(out=dei[:, 0:SW // 2],
                                         in_=keep[:].rearrange(
                                             "p (f k) -> p k f", k=2)[:, 0, :]))
                    e.wait_ge(sV.h, sV.c)
                    sV.inc(e.tensor_mul(out=dei[:, 0:SW // 2],
                                        in0=dei[:, 0:SW // 2], in1=pin[:]))

                @block.gpsimd
                def _(e):
                    w(e, sV)
                    qG.inc(e.dma_start(
                        out=pk0_d.ap().rearrange("(s f) b -> s (f b)", s=16),
                        in_=dei[:, 0:SW // 2]), 16)

                @block.sync
                def _(e):
                    w(e, sV, qG)
                    qL.inc(e.dma_start(out=pin[:], in_=p1s), 16)

                @block.vector
                def _(e):
                    w(e, qL, qG)
                    sV.inc(e.tensor_copy(out=dei[:, 0:SW // 2],
                                         in_=keep[:].rearrange(
                                             "p (f k) -> p k f", k=2)[:, 1, :]))
                    e.wait_ge(sV.h, sV.c)
                    sV.inc(e.tensor_mul(out=dei[:, 0:SW // 2],
                                        in0=dei[:, 0:SW // 2], in1=pin[:]))

                @block.gpsimd
                def _(e):
                    w(e, sV)
                    qG.inc(e.dma_start(
                        out=pk1_d.ap().rearrange("(s f) b -> s (f b)", s=16),
                        in_=dei[:, 0:SW // 2]), 16)

        esr.close()
        esa = ExitStack()
        _e3 = esa.enter_context
        g0 = _e3(nc.sbuf_tensor("g0", [128, DM], F32))
        g1 = _e3(nc.sbuf_tensor("g1", [128, DM], F32))
        yv = _e3(nc.sbuf_tensor("yv", [128, DM], F32))
        x1t = _e3(nc.sbuf_tensor("x1t", [128, DM], F32))
        ic0 = _e3(nc.sbuf_tensor("ic0", [128, 1], I32))
        ic1 = _e3(nc.sbuf_tensor("ic1", [128, 1], I32))
        pc0 = _e3(nc.sbuf_tensor("pc0", [128, 1], F32))
        pc1 = _e3(nc.sbuf_tensor("pc1", [128, 1], F32))
        lncc = _e3(nc.sbuf_tensor("lncc", [128, 8], F32))
        sstr = _e3(nc.sbuf_tensor("sstr", [1, 1], I32))
        iocol = _e3(nc.sbuf_tensor("iocol", [128, 1], I32))
        strc = _e3(nc.sbuf_tensor("strc", [128, 1], I32))
        bcol = _e3(nc.sbuf_tensor("bcol", [128, 1], I32))
        tcol = _e3(nc.sbuf_tensor("tcol", [128, 2, 1], I32))
        if True:
            @block.sync
            def _(e):
                qL.inc(e.dma_start(out=sstr[:], in_=stripev), 16)
                qL.inc(e.dma_start(out=iocol[:], in_=iotacv), 16)
                qL.inc(e.dma_start(out=strc[:],
                                   in_=stripev.partition_broadcast(128)), 16)

            @block.vector
            def _(e):
                w(e, qL)
                sV.inc(e.tensor_scalar(out=bcol[:], in0=strc[:], scalar1=NTOK,
                                       scalar2=None, op0=ALU.mult))
                e.wait_ge(sV.h, sV.c)
                sV.inc(e.tensor_tensor(out=bcol[:], in0=bcol[:], in1=iocol[:],
                                       op=ALU.add))

            for tt in range(NTOK // 128):
                @block.sync
                def _(e, tt=tt):
                    w(e, sV, qG)
                    qL.inc(e.dma_start(out=x1t[:],
                                       in_=x1s[tt * 128:(tt + 1) * 128, :]), 16)

                @block.vector
                def _(e, tt=tt):
                    w(e, qG)
                    sV.inc(e.tensor_scalar(out=tcol[:, tt % 2, :], in0=bcol[:],
                                           scalar1=tt * 128, scalar2=None,
                                           op0=ALU.add))

                @block.gpsimd
                def _(e, tt=tt):
                    w(e, qL, sV)
                    tc = tcol[:, tt % 2, :]
                    qG.inc(e.indirect_dma_start(
                        out=ic0[:], out_offset=None, in_=idx0_d.ap(),
                        in_offset=IndirectOffsetOnAxis(ap=tc, axis=0)), 16)
                    qG.inc(e.indirect_dma_start(
                        out=ic1[:], out_offset=None, in_=idx1_d.ap(),
                        in_offset=IndirectOffsetOnAxis(ap=tc, axis=0)), 16)
                    qG.inc(e.indirect_dma_start(
                        out=pc0[:], out_offset=None, in_=pk0_d.ap(),
                        in_offset=IndirectOffsetOnAxis(ap=tc, axis=0)), 16)
                    qG.inc(e.indirect_dma_start(
                        out=pc1[:], out_offset=None, in_=pk1_d.ap(),
                        in_offset=IndirectOffsetOnAxis(ap=tc, axis=0)), 16)
                    e.wait_ge(qG.h, qG.c)
                    qG.inc(e.indirect_dma_start(
                        out=g0[:], out_offset=None, in_=eof,
                        in_offset=IndirectOffsetOnAxis(ap=ic0[:], axis=0)), 16)
                    qG.inc(e.indirect_dma_start(
                        out=g1[:], out_offset=None, in_=eof,
                        in_offset=IndirectOffsetOnAxis(ap=ic1[:], axis=0)), 16)

                @block.vector
                def _(e, tt=tt):
                    w(e, qG)
                    sV.inc(e.tensor_scalar(out=g0[:], in0=g0[:],
                                            scalar1=pc0[:, 0:1],
                                            scalar2=None, op0=ALU.mult))
                    e.wait_ge(sV.h, sV.c)
                    sV.inc(e.scalar_tensor_tensor(out=yv[:], in0=g1[:],
                                                  scalar=pc1[:, 0:1],
                                                  in1=g0[:],
                                                  op0=ALU.mult, op1=ALU.add))
                    e.wait_ge(sV.h, sV.c)
                    sV.inc(e.tensor_add(out=yv[:], in0=yv[:], in1=x1t[:]))
                    e.wait_ge(sV.h, sV.c)
                    sV.inc(e.reduce_sum(out=lncc[:, 0:1], in_=yv[:], axis=AX))
                    e.wait_ge(sV.h, sV.c)
                    sV.inc(e.tensor_scalar(out=lncc[:, 1:2], in0=lncc[:, 0:1],
                                           scalar1=1.0 / DM, scalar2=None,
                                           op0=ALU.mult))
                    e.wait_ge(sV.h, sV.c)
                    sV.inc(e.tensor_scalar(out=g0[:], in0=yv[:],
                                           scalar1=lncc[:, 1:2], scalar2=None,
                                           op0=ALU.subtract))

                @block.scalar
                def _(e, tt=tt):
                    w(e, sV)
                    sA.inc(e.activation(out=g1[:], in_=g0[:], func=AF.Square))

                @block.vector
                def _(e, tt=tt):
                    w(e, sA)
                    sV.inc(e.reduce_sum(out=lncc[:, 2:3], in_=g1[:], axis=AX))
                    e.wait_ge(sV.h, sV.c)
                    sV.inc(e.tensor_scalar(out=lncc[:, 3:4], in0=lncc[:, 2:3],
                                           scalar1=1.0 / DM, scalar2=None,
                                           op0=ALU.mult))

                @block.scalar
                def _(e, tt=tt):
                    w(e, sV)
                    sA.inc(e.activation(out=lncc[:, 4:5], in_=lncc[:, 3:4],
                                        func=AF.Ln, bias=lncst3[:, 0:1]))
                    e.wait_ge(sA.h, sA.c)
                    sA.inc(e.activation(out=lncc[:, 5:6], in_=lncc[:, 4:5],
                                        func=AF.Exp, scale=lncst3[:, 1:2]))

                @block.vector
                def _(e, tt=tt):
                    w(e, sA)
                    sV.inc(e.tensor_scalar(out=yv[:], in0=g0[:],
                                           scalar1=lncc[:, 5:6], scalar2=None,
                                           op0=ALU.mult))

                @block.gpsimd
                def _(e, tt=tt):
                    w(e, sV)
                    qG.inc(e.dma_start(out=out[tt * 128:(tt + 1) * 128, :],
                                       in_=yv[:]), 16)

            @block.gpsimd
            def _(e):
                e.wait_ge(qG.h, qG.c)

        esa.close()
    es.close()
    return nc


_cache = {}


def _get(key, builder):
    if key not in _cache:
        _cache[key] = builder()
    return _cache[key]


def _rs(a, nch=NCH, inner=None):
    a = np.asarray(a)
    k = a.shape[1] if a.ndim == 2 else 1
    a = a.reshape(nch, 128, -1).transpose(1, 0, 2).reshape(128, -1)
    return np.ascontiguousarray(a)


def kernel(L0_attn_x, L0_attn_W_in, L0_attn_conv_w, L0_attn_conv_b,
           L0_attn_W_x, L0_attn_W_dt, L0_attn_b_dt, L0_attn_A_log,
           L0_attn_D_param, L0_attn_W_out, L0_ffn_rw1, L0_ffn_rb1,
           L0_ffn_rw2, L0_ffn_rb2, L0_ffn_ew1, L0_ffn_eb1, L0_ffn_ew2,
           L0_ffn_eb2):
    x = np.asarray(L0_attn_x, np.float32)
    B, T, _ = x.shape
    TH = T // 2
    N = B * T
    NK = 2 * N

    ident = np.eye(128, dtype=np.float32)
    iota8 = np.arange(NE, dtype=np.float32)[None, :]
    nonce = np.array([[np.random.randint(1, 1 << 30)]], np.int32)

    nc1 = _get(("L1", T), lambda: build_L1(T=T))
    in1 = []
    for c in range(NCORES):
        b, dh = c >> 1, c & 1
        sl = slice(dh * DH, (dh + 1) * DH)
        m = dict(
            xT=np.ascontiguousarray(x[b].T),
            x_tm=np.ascontiguousarray(x[b, dh * TH:(dh + 1) * TH]),
            winT=np.ascontiguousarray(
                np.concatenate([L0_attn_W_in[sl], L0_attn_W_in[2048:][sl]],
                               0).T.astype(np.float32)),
            cwv=_rs(np.asarray(L0_attn_conv_w)[sl, 0, :]),
            cbv=_rs(np.asarray(L0_attn_conv_b)[sl, None]),
            wxT=np.ascontiguousarray(np.asarray(L0_attn_W_x)[:, sl].T),
            wdtv=_rs(np.asarray(L0_attn_W_dt)[sl]),
            bdtv=_rs(np.asarray(L0_attn_b_dt)[sl, None]),
            alogv=_rs(np.asarray(L0_attn_A_log)[sl]),
            dpv=_rs(np.asarray(L0_attn_D_param)[sl, None]),
            woutT=np.ascontiguousarray(np.asarray(L0_attn_W_out)[:, sl].T),
            rw1=np.asarray(L0_ffn_rw1, np.float32),
            rb1v=np.ascontiguousarray(
                np.asarray(L0_ffn_rb1).reshape(32, 128).T),
            rw2=np.asarray(L0_ffn_rw2, np.float32),
            rb2v=np.asarray(L0_ffn_rb2, np.float32).reshape(NE, 1),
            identv=ident, iotav=iota8, noncev=nonce,
            slotv=np.array([[dh]], np.int32),
            lncv=np.tile(np.array([[1e-5, -0.5]], np.float32), (128, 1)),
        )
        in1.append({k: np.ascontiguousarray(v) for k, v in m.items()})
    r1 = run_bass_kernel_spmd(nc1, in1, core_ids=list(range(NCORES)))
    x1f = np.concatenate([r1.results[c]["x1out"] for c in range(NCORES)], 0)
    iv = np.concatenate([r1.results[c]["ivout"] for c in range(NCORES)], 0)

    idsf = np.ascontiguousarray(
        np.stack([iv[:, 0], iv[:, 1]], 1).reshape(1, NK).astype(np.float32))
    tokv = np.ascontiguousarray(
        (np.arange(NK, dtype=np.int64) // 2 + 1).astype(np.int32)[None, :])
    emodv = (np.arange(128, dtype=np.float32) % 8)[:, None]
    spatv = np.zeros((128, 128), np.float32)
    for rp in range(128):
        for r in range(128):
            if rp % 8 == r % 8 and rp // 8 < r // 8:
                spatv[rp, r] = 1.0

    nc2 = _get(("L2", N), lambda: build_L2(N=N))
    in2 = []
    for e_ in range(NCORES):
        selv = np.zeros((128, 16), np.float32)
        for q in range(16):
            selv[q * 8 + e_, q] = 1.0
        m = dict(
            x1f=x1f, idsf=idsf, tokv=tokv,
            ew1in=np.asarray(L0_ffn_ew1)[e_].astype(np.float32),
            ew2in=np.asarray(L0_ffn_ew2)[e_].astype(np.float32),
            eb1v=np.ascontiguousarray(
                np.asarray(L0_ffn_eb1)[e_].reshape(32, 128).T),
            eb2v=np.asarray(L0_ffn_eb2)[e_].astype(np.float32)[None, :],
            emodv=emodv, spatv=spatv, selv=selv, identv=ident,
        )
        in2.append({k: np.ascontiguousarray(v) for k, v in m.items()})
    r2 = run_bass_kernel_spmd(nc2, in2, core_ids=list(range(NCORES)))
    eof = np.concatenate([r2.results[e_]["eoout"] for e_ in range(NCORES)], 0)

    sumpv = np.zeros((128, 16), np.float32)
    selpv = np.zeros((128, 16), np.float32)
    for q in range(16):
        for e_ in range(8):
            sumpv[q * 8 + e_, q] = 1.0
        selpv[q * 8, q] = 1.0
    p0s = np.ascontiguousarray(iv[:, 2].reshape(16, N // 16))
    p1s = np.ascontiguousarray(iv[:, 3].reshape(16, N // 16))

    nc3 = _get(("L3", N), lambda: build_L3(N=N, NTOK=N // NCORES))
    NTOK = N // NCORES
    in3 = []
    for c in range(NCORES):
        m = dict(
            eof=eof, x1s=x1f[c * NTOK:(c + 1) * NTOK], idsf=idsf,
            p0s=p0s, p1s=p1s, emodv=emodv, spatv=spatv,
            sumpv=sumpv, selpv=selpv,
            stripev=np.array([[c]], np.int32),
            iotacv=np.arange(128, dtype=np.int32)[:, None],
            lncv3=np.tile(np.array([[1e-5, -0.5]], np.float32), (128, 1)),
        )
        in3.append({k: np.ascontiguousarray(v) for k, v in m.items()})
    r3 = run_bass_kernel_spmd(nc3, in3, core_ids=list(range(NCORES)))
    y = np.concatenate([r3.results[c]["out"] for c in range(NCORES)], 0)
    return y.reshape(B, T, DM)


# version 08
# speedup vs baseline: 1.1039x; 1.0145x over previous
import numpy as np
from contextlib import ExitStack
import concourse.bass as bass
import concourse.mybir as mybir
from concourse.bass import IndirectOffsetOnAxis
from concourse.bass_utils import run_bass_kernel_spmd

F32 = mybir.dt.float32
BF16 = mybir.dt.bfloat16
I32 = mybir.dt.int32
AF = mybir.ActivationFunctionType
ALU = mybir.AluOpType
AX = mybir.AxisListType.X

DM = 1024
DH = 1024
NCH = 8
DS = 16
HF = 4096
NE = 8
CAP = 4096
NCORES = 8
BIGI = 1 << 20



_regc = [0]


def rn(p):
    _regc[0] += 1
    return f"{p}{_regc[0]}"

class Sem:
    def __init__(self, h):
        self.h = h
        self.c = 0

    def inc(self, instr, n=1):
        instr.then_inc(self.h, n)
        self.c += n
        return self.c


def w(eng, *sems):
    for s in sems:
        if s.c:
            eng.wait_ge(s.h, s.c)


def slot_branch(e, sslot, arm0, arm1, regname="rs"):
    with e.register(rn(regname)) as rs_:
        e.reg_load(rs_, sslot[0:1, 0:1])
        with e.If_eq(rs_, 0):
            arm0()
        with e.Else():
            arm1()


def poll_partner(e, rpoll, rtmp, flag_ap_pt, sflag, snonce_sb, pp_h):
    with e.register(rn("rf")) as rf, e.register(rn("rc")) as rc, \
         e.register(rn("rnn")) as rnn:
        e.reg_load(rnn, snonce_sb[0:1, 0:1])
        e.reg_mov(rc, 1)
        with e.While(rc):
            e.reg_add(rpoll, rpoll, 16)
            e.dma_start(out=sflag[:], in_=flag_ap_pt).then_inc(pp_h, 16)
            e.wait_ge(pp_h, e.snap(rpoll))
            e.reg_load(rf, sflag[0:1, 0:1])
            e.reg_alu(rc, rf, e.snap(rnn), ALU.not_equal)


def build_L1(T=8192, MT=512, ST=1024, WST=None, dbg=False, stop='full'):
    TH = T // 2
    NMT = T // MT
    NST = T // ST
    WST = WST or ST
    NWS = T // WST
    NRG = TH // 512
    nc = bass.Bass("TRN2", target_bir_lowering=False, debug=False,
                   detect_race_conditions=False)

    ein = lambda n, s, d=F32: nc.dram_tensor(n, s, d, kind="ExternalInput").ap()
    xT = ein("xT", [DM, T])
    x_tm = ein("x_tm", [TH, DM])
    winT = ein("winT", [DM, 2 * DH])
    cwv = ein("cwv", [128, NCH * 4])
    cbv = ein("cbv", [128, NCH])
    wxT = ein("wxT", [DH, 33])
    wdtv = ein("wdtv", [128, NCH])
    bdtv = ein("bdtv", [128, NCH])
    alogv = ein("alogv", [128, NCH * DS])
    dpv = ein("dpv", [128, NCH])
    woutT = ein("woutT", [DH, DM])
    rw1 = ein("rw1", [DM, HF])
    rb1v = ein("rb1v", [128, HF // 128])
    rw2 = ein("rw2", [HF, NE])
    rb2v = ein("rb2v", [NE, 1])
    identv = ein("identv", [128, 128])
    iotav = ein("iotav", [1, NE])
    noncev = ein("noncev", [1, 1], I32)
    slotv = ein("slotv", [1, 1], I32)
    lncv = ein("lncv", [128, 2])

    x1out = nc.dram_tensor("x1out", [TH, DM], F32, kind="ExternalOutput").ap()
    ivout = nc.dram_tensor("ivout", [TH, 4], F32, kind="ExternalOutput").ap()

    ssm_sh = nc.dram_tensor("ssm_sh", [2, 33, T], F32, addr_space="Shared")
    mix_sh = nc.dram_tensor("mix_sh", [2 * T, DM], F32, addr_space="Shared")
    flag_sh = nc.dram_tensor("flag_sh", [2, 2, 1], I32, addr_space="Shared")

    if dbg:
        dbg_u = nc.dram_tensor("dbg_u", [NCH, 128, T], F32, kind="ExternalOutput").ap()
        dbg_z = nc.dram_tensor("dbg_z", [NCH, 128, T], F32, kind="ExternalOutput").ap()
        dbg_y2 = nc.dram_tensor("dbg_y2", [NCH, 128, T], F32, kind="ExternalOutput").ap()
        dbg_ssm = nc.dram_tensor("dbg_ssm", [33, T], F32, kind="ExternalOutput").ap()
        dbg_mix = nc.dram_tensor("dbg_mix", [2 * T, DM], F32, kind="ExternalOutput").ap()
        dbg_ln = nc.dram_tensor("dbg_ln", [128, 8 + 2 * DM], F32, kind="ExternalOutput").ap()
    zsil_d = nc.dram_tensor("zsil_d", [NCH, 128, T], F32)
    u_d = nc.dram_tensor("u_d", [NCH, 128, T], F32)
    y2_d = nc.dram_tensor("y2_d", [NCH, 128, T], F32)
    ssmb_d = nc.dram_tensor("ssmb_d", [33, T], BF16)
    ssm0_d = nc.dram_tensor("ssm0_d", [1, T], F32)

    es = ExitStack()
    sb = lambda n, s, d=F32: es.enter_context(nc.sbuf_tensor(n, s, d))
    ps = lambda n, s: es.enter_context(nc.psum_tensor(n, s, F32))
    cw = sb("cw", [128, NCH, 4]); cb = sb("cb", [128, NCH])
    wdtc = sb("wdtc", [128, NCH]); bdtc = sb("bdtc", [128, NCH])
    ac = sb("ac", [128, NCH * DS]); dpc = sb("dpc", [128, NCH])
    id16 = sb("id16", [128, 128], BF16); idf = sb("idf", [128, 128])
    io8b = sb("io8b", [128, NE]); snonce = sb("snonce", [1, 1], I32)
    sflag = sb("sflag", [1, 1], I32); sslot = sb("sslot", [1, 1], I32)
    hcarry = sb("hcarry", [128, NCH * DS])
    lncst = sb("lncst", [128, 2])
    _qL = es.enter_context(nc.semaphore("qL"))
    _qG = es.enter_context(nc.semaphore("qG"))
    _sV = es.enter_context(nc.semaphore("sV"))
    _sA = es.enter_context(nc.semaphore("sA"))
    _sT = es.enter_context(nc.semaphore("sT"))
    _pp = es.enter_context(nc.semaphore("pp"))
    _pm = es.enter_context(nc.semaphore("pm"))
    block = es.enter_context(nc.Block())
    if True:
        qL, qG, sV, sA, sT = Sem(_qL), Sem(_qG), Sem(_sV), Sem(_sA), Sem(_sT)
        pm = Sem(_pm)

        rpoll_box = {}

        @block.gpsimd
        def _(e):
            rpoll_box["r"] = e.alloc_register(rn("rpoll"))
            e.reg_mov(rpoll_box["r"], 0)

        @block.sync
        def _(e):
            for t_, src in ((cw, cwv), (cb, cbv), (wdtc, wdtv), (bdtc, bdtv),
                            (ac, alogv), (dpc, dpv), (idf, identv),
                            (snonce, noncev), (sslot, slotv), (lncst, lncv)):
                qL.inc(e.dma_start(out=t_[:], in_=src), 16)
            qL.inc(e.dma_start(out=io8b[:], in_=iotav.partition_broadcast(128)), 16)

        @block.vector
        def _(e):
            w(e, qL)
            sV.inc(e.tensor_copy(out=id16[:], in_=idf[:]))
            sV.inc(e.memset(hcarry[:], 0.0))

        @block.scalar
        def _(e):
            w(e, qL)
            sA.inc(e.activation(out=ac[:], in_=ac[:], func=AF.Exp))

        @block.vector
        def _(e):
            w(e, sA)
            sV.inc(e.tensor_scalar(out=ac[:], in0=ac[:], scalar1=-1.0,
                                   scalar2=None, op0=ALU.mult))

        esM = ExitStack()
        w16 = esM.enter_context(nc.sbuf_tensor("w16", [128, 8, 2 * DH], BF16))
        wx16 = esM.enter_context(nc.sbuf_tensor("wx16", [128, NCH, 33], BF16))
        ssma = esM.enter_context(nc.sbuf_tensor("ssma", [33, T], F32))
        if True:
            with nc.sbuf_tensor("wtmp", [128, 8, 2 * DH], F32) as wtmp:
                @block.sync
                def _(e):
                    w(e, sV)
                    qL.inc(e.dma_start(
                        out=wtmp[:],
                        in_=winT.rearrange("(k p) n -> p k n", p=128)), 16)

                @block.vector
                def _(e):
                    w(e, qL)
                    sV.inc(e.tensor_copy(out=w16[:], in_=wtmp[:]))

                @block.sync
                def _(e):
                    w(e, sV)
                    qL.inc(e.dma_start(
                        out=wtmp[:, 0, 0:NCH * 33].rearrange(
                            "p (k n) -> p k n", k=NCH),
                        in_=wxT.rearrange("(k p) n -> p k n", p=128)), 16)

                @block.vector
                def _(e):
                    w(e, qL)
                    sV.inc(e.tensor_copy(
                        out=wx16[:],
                        in_=wtmp[:, 0, 0:NCH * 33].rearrange(
                            "p (k n) -> p k n", k=NCH)))

            esMt = ExitStack()
            xt32 = esMt.enter_context(nc.sbuf_tensor("xt32", [128, 8, MT], F32))
            xt16 = esMt.enter_context(nc.sbuf_tensor("xt16", [128, 8, MT], BF16))
            xdh = esMt.enter_context(nc.sbuf_tensor("xdh", [128, NCH, MT + 3], F32))
            cacc = esMt.enter_context(nc.sbuf_tensor("cacc", [128, 2, MT], F32))
            ut = esMt.enter_context(nc.sbuf_tensor("ut", [128, NCH, MT], F32))
            ut16 = esMt.enter_context(nc.sbuf_tensor("ut16", [128, NCH, MT], BF16))
            zst = esMt.enter_context(nc.sbuf_tensor("zst", [128, NCH, MT], F32))
            psa = esMt.enter_context(nc.psum_tensor("psa", [128, 2, 512], F32))
            psm = esMt.enter_context(nc.psum_tensor("psm", [33, 512], F32))
            if True:
                NSUB = MT // 512

                @block.vector
                def _(e):
                    sV.inc(e.memset(xdh[:], 0.0))

                for i in range(NMT):
                    @block.sync
                    def _(e, i=i):
                        w(e, sV, sT)
                        qL.inc(e.dma_start(
                            out=xt32[:],
                            in_=xT[:, i * MT:(i + 1) * MT]
                                .rearrange("(k p) t -> p k t", p=128)), 16)

                    @block.vector
                    def _(e, i=i):
                        w(e, qL)
                        sV.inc(e.tensor_copy(out=xt16[:], in_=xt32[:]))

                    mk = {}

                    @block.tensor
                    def _(e, i=i, mk=mk):
                        w(e, sV, sA)
                        saw = sA.c
                        planned = {}
                        for j in range(16):
                            if j < 8:
                                planned[j] = saw + 2 * j + 2
                            else:
                                planned[j] = saw + 16 + (j - 8) + 1
                        for o in range(16):
                            if o >= 2:
                                e.wait_ge(sA.h, planned[o - 2])
                            for k in range(8):
                                ins = e.matmul(psa[:, o % 2, 0:MT],
                                               w16[:, k, o * 128:(o + 1) * 128],
                                               xt16[:, k, :],
                                               start=(k == 0), stop=(k == 7))
                            sT.inc(ins)
                            mk[("xz", o)] = sT.c

                    @block.scalar
                    def _(e, i=i, mk=mk):
                        w(e, qG)
                        for c in range(NCH):
                            e.wait_ge(sT.h, mk[("xz", c)])
                            sA.inc(e.copy(out=xdh[:, c, 0:3],
                                          in_=xdh[:, c, MT:MT + 3]))
                            e.wait_ge(sA.h, sA.c)
                            sA.inc(e.copy(out=xdh[:, c, 3:MT + 3],
                                          in_=psa[:, c % 2, 0:MT]))
                            mk[("evac", c)] = sA.c
                        for o in range(8, 16):
                            e.wait_ge(sT.h, mk[("xz", o)])
                            sA.inc(e.activation(out=zst[:, o - 8, :],
                                                in_=psa[:, o % 2, 0:MT],
                                                func=AF.Silu))
                            mk[("zev", o)] = sA.c

                    @block.vector
                    def _(e, i=i, mk=mk):
                        saw = sA.c
                        for c in range(NCH):
                            e.wait_ge(sA.h, mk[("evac", c)])
                            if c >= 1:
                                e.wait_ge(sA.h, saw + c)
                            sV.inc(e.tensor_scalar(out=cacc[:, 0, :],
                                            in0=xdh[:, c, 0:MT],
                                            scalar1=cw[:, c, 0:1], scalar2=None,
                                            op0=ALU.mult))
                            for j in (1, 2, 3):
                                e.wait_ge(sV.h, sV.c)
                                sV.inc(e.scalar_tensor_tensor(
                                    out=cacc[:, j % 2, :],
                                    in0=xdh[:, c, j:j + MT],
                                    scalar=cw[:, c, j:j + 1],
                                    in1=cacc[:, (j + 1) % 2, :],
                                    op0=ALU.mult, op1=ALU.add))
                            mk[("conv", c)] = sV.c

                    @block.scalar
                    def _(e, i=i, mk=mk):
                        for c in range(NCH):
                            e.wait_ge(sV.h, mk[("conv", c)])
                            sA.inc(e.activation(out=ut[:, c, :],
                                                in_=cacc[:, 1, :], func=AF.Silu,
                                                bias=cb[:, c:c + 1], scale=1.0))
                            mk[("usil", c)] = sA.c

                    @block.vector
                    def _(e, i=i):
                        w(e, sA, sT)
                        sV.inc(e.tensor_copy(out=ut16[:], in_=ut[:]))

                    @block.tensor
                    def _(e, i=i):
                        w(e, sV, sA)
                        for sub in range(MT // 512):
                            for c in range(NCH):
                                ins = e.matmul(
                                    psm[:, :], wx16[:, c, :],
                                    ut16[:, c, sub * 512:(sub + 1) * 512],
                                    start=(c == 0), stop=(c == NCH - 1))
                            sT.inc(ins)

                    @block.scalar
                    def _(e, i=i):
                        base_t = sT.c - MT // 512
                        for sub in range(MT // 512):
                            e.wait_ge(sT.h, base_t + sub + 1)
                            sA.inc(e.copy(
                                out=ssma[:, i * MT + sub * 512:
                                         i * MT + (sub + 1) * 512],
                                in_=psm[:, :]))

                    @block.gpsimd
                    def _(e, i=i):
                        w(e, sA, sV)
                        for c in range(NCH):
                            qG.inc(e.dma_start(
                                out=zsil_d.ap()[c, :, i * MT:(i + 1) * MT],
                                in_=zst[:, c, :]), 16)
                            qG.inc(e.dma_start(
                                out=u_d.ap()[c, :, i * MT:(i + 1) * MT],
                                in_=ut[:, c, :]), 16)

            esMt.close()
            with (
                nc.sbuf_tensor("stmp", [33, T], F32) as stmp,
                nc.sbuf_tensor("ssmb16", [33, T], BF16) as ssmb16,
            ):
                @block.gpsimd
                def _(e):
                    w(e, sA, qG)
                    rpoll = rpoll_box["r"]

                    def arm(me):
                        pt = 1 - me
                        e.dma_start(out=ssm_sh.ap()[me, :, :],
                                    in_=ssma[:]).then_inc(_qG, 16)
                        e.wait_ge(_qG, qG.c + 16)
                        e.dma_start(out=flag_sh.ap()[me, 0, :],
                                    in_=snonce[:]).then_inc(_qG, 16)
                        e.wait_ge(_qG, qG.c + 32)
                        poll_partner(e, rpoll, None,
                                     flag_sh.ap()[pt, 0, :],
                                     sflag, snonce, _pp)
                        e.reg_add(rpoll, rpoll, 16)
                        e.dma_start(out=stmp[:],
                                    in_=ssm_sh.ap()[pt, :, :]).then_inc(_pp, 16)
                        e.wait_ge(_pp, e.snap(rpoll))

                    slot_branch(e, sslot, lambda: arm(0), lambda: arm(1))
                    qG.c += 32
                    pm.inc(e.nop())

                @block.vector
                def _(e):
                    e.wait_ge(pm.h, pm.c)
                    sV.inc(e.tensor_add(out=ssma[:], in0=ssma[:], in1=stmp[:]))
                    e.wait_ge(sV.h, sV.c)
                    sV.inc(e.tensor_copy(out=ssmb16[:], in_=ssma[:]))

                @block.scalar
                def _(e):
                    w(e, sV)
                    sA.inc(e.dma_start(out=ssmb_d.ap(), in_=ssmb16[:]), 16)
                    sA.inc(e.dma_start(out=ssm0_d.ap(), in_=ssma[0:1, :]), 16)
                    if dbg:
                        sA.inc(e.dma_start(out=dbg_ssm, in_=ssma[:]), 16)

        esM.close()
        if stop == "x":
            @block.gpsimd
            def _(e):
                w(e, sA, sV)
                e.wait_ge(qG.h, qG.c)
            es.close()
            return nc
        esS = ExitStack()
        _e = esS.enter_context
        bcB = _e(nc.sbuf_tensor("bcB", [128, DS, ST], BF16))
        bcC = _e(nc.sbuf_tensor("bcC", [128, DS, ST], BF16))
        bcdt = _e(nc.sbuf_tensor("bcdt", [128, ST], F32))
        ustr = _e(nc.sbuf_tensor("ustr", [128, NCH, ST], F32))
        delta = _e(nc.sbuf_tensor("delta", [128, ST], F32))
        du = _e(nc.sbuf_tensor("du", [128, ST], F32))
        spr = _e(nc.sbuf_tensor("spr", [128, ST], F32))
        spl = _e(nc.sbuf_tensor("spl", [128, ST], F32))
        spl2 = _e(nc.sbuf_tensor("spl2", [128, ST], F32))
        dAb = _e(nc.sbuf_tensor("dAb", [128, 2, ST], F32))
        dBub = _e(nc.sbuf_tensor("dBub", [128, 2, ST], F32))
        hb = _e(nc.sbuf_tensor("hb", [128, 2, ST], F32))
        pr16 = _e(nc.sbuf_tensor("pr16", [128, 2, ST], BF16))
        y2s = _e(nc.sbuf_tensor("y2s", [128, 2, ST], F32))
        psy = _e(nc.psum_tensor("psy", [128, ST // 512, 512], F32))
        if True:
            import os as _osC
            CL = int(_osC.environ.get("CLIM", "0")) or NCH
            NYB = ST // 512
            sv_scan, sv_prod, sv_du, sv_y2 = {}, {}, {}, {}
            sa_dA, sa_carry, sa_mark = {}, {}, {}
            st_ymm, qg_y2, ql_strip = {}, {}, {}

            pv, pa, pt_, pg, pl = sV.c, sA.c, sT.c, qG.c, qL.c
            for si in range(NST):
                pl += 16 * (1 + 2 * DS + NCH)
                ql_strip[si] = pl
                for c in range(CL):
                    pa += 1
                    sa_mark[(si, c)] = pa
                    pv += 2
                    sv_du[(si, c)] = pv
                    for s in range(DS):
                        pa += 1
                        sa_dA[(si, c, s)] = pa
                        pv += 1
                        pv += 1
                        sv_scan[(si, c, s)] = pv
                        pv += 1
                        sv_prod[(si, c, s)] = pv
                        pa += 1
                        sa_carry[(si, c, s)] = pa
                        pt_ += NYB
                        st_ymm[(si, c, s)] = pt_
                    pv += 1
                    sv_y2[(si, c)] = pv
                    pg += 16
                    qg_y2[(si, c)] = pg

            for si in range(NST):
                t0 = si * ST

                @block.sync
                def _(e, si=si, t0=t0):
                    if si > 0:
                        e.wait_ge(sV.h, sv_y2[(si - 1, CL - 1)])
                        e.wait_ge(sA.h, sa_carry[(si - 1, CL - 1, DS - 1)])
                    else:
                        w(e, sA, sV)
                    qL.inc(e.dma_start(
                        out=bcdt[:],
                        in_=ssm0_d.ap()[:, t0:t0 + ST]
                            .partition_broadcast(128)), 16)
                    for s in range(DS):
                        qL.inc(e.dma_start(
                            out=bcB[:, s, :],
                            in_=ssmb_d.ap()[1 + s:2 + s, t0:t0 + ST]
                                .partition_broadcast(128)), 16)
                        qL.inc(e.dma_start(
                            out=bcC[:, s, :],
                            in_=ssmb_d.ap()[17 + s:18 + s, t0:t0 + ST]
                                .partition_broadcast(128)), 16)
                    for c in range(NCH):
                        qL.inc(e.dma_start(
                            out=ustr[:, c, :],
                            in_=u_d.ap()[c, :, t0:t0 + ST]), 16)
                    assert qL.c == ql_strip[si]

                @block.scalar
                def _(e, si=si):
                    e.wait_ge(qL.h, ql_strip[si])
                    for c in range(CL):
                        if c >= 1:
                            e.wait_ge(sV.h, sv_du[(si, c - 1)])
                        elif si >= 1:
                            e.wait_ge(sV.h, sv_du[(si - 1, CL - 1)])
                        e.activation(out=spr[:], in_=bcdt[:], func=AF.Relu,
                                     scale=wdtc[:, c:c + 1],
                                     bias=bdtc[:, c:c + 1])
                        e.activation(out=spl[:], in_=bcdt[:], func=AF.Abs,
                                     scale=wdtc[:, c:c + 1],
                                     bias=bdtc[:, c:c + 1])
                        e.activation(out=spl2[:], in_=spl[:], func=AF.Exp,
                                     scale=-1.0)
                        sA.inc(e.activation(out=spl[:], in_=spl2[:], func=AF.Ln,
                                            bias=1.0))
                        assert sA.c == sa_mark[(si, c)]
                        for s in range(DS):
                            if s >= 2:
                                e.wait_ge(sV.h, sv_scan[(si, c, s - 2)])
                            elif c >= 1:
                                e.wait_ge(sV.h, sv_scan[(si, c - 1, DS - 2 + s)])
                            elif si >= 1:
                                e.wait_ge(sV.h,
                                          sv_scan[(si - 1, CL - 1, DS - 2 + s)])
                            e.wait_ge(sV.h, sv_du[(si, c)])
                            sA.inc(e.activation(
                                out=dAb[:, s % 2, :], in_=delta[:], func=AF.Exp,
                                scale=ac[:, c * DS + s:c * DS + s + 1]))
                            assert sA.c == sa_dA[(si, c, s)]
                            e.wait_ge(sV.h, sv_scan[(si, c, s)])
                            sA.inc(e.copy(
                                out=hcarry[:, c * DS + s:c * DS + s + 1],
                                in_=hb[:, s % 2, ST - 1:ST]))
                            assert sA.c == sa_carry[(si, c, s)]

                @block.vector
                def _(e, si=si):
                    for c in range(CL):
                        e.wait_ge(sA.h, sa_mark[(si, c)])
                        sV.inc(e.tensor_add(out=delta[:], in0=spr[:],
                                            in1=spl[:]))
                        sV.inc(e.tensor_mul(out=du[:], in0=delta[:],
                                            in1=ustr[:, c, :]))
                        assert sV.c == sv_du[(si, c)]
                        for s in range(DS):
                            col = c * DS + s
                            sV.inc(e.tensor_tensor(out=dBub[:, s % 2, :],
                                                   in0=du[:],
                                                   in1=bcB[:, s, :],
                                                   op=ALU.mult))
                            e.wait_ge(sA.h, sa_dA[(si, c, s)])
                            if si == 0:
                                init = 0.0
                            else:
                                e.wait_ge(sA.h, sa_carry[(si - 1, c, s)])
                                init = hcarry[:, col:col + 1]
                            if s >= 2:
                                e.wait_ge(sT.h, st_ymm[(si, c, s - 2)])
                            elif c >= 1:
                                e.wait_ge(sT.h, st_ymm[(si, c - 1, DS - 2 + s)])
                            elif si >= 1:
                                e.wait_ge(sT.h,
                                          st_ymm[(si - 1, CL - 1, DS - 2 + s)])
                            sV.inc(e.tensor_tensor_scan(
                                out=hb[:, s % 2, :], data0=dAb[:, s % 2, :],
                                data1=dBub[:, s % 2, :], initial=init,
                                op0=ALU.mult, op1=ALU.add))
                            assert sV.c == sv_scan[(si, c, s)]
                            sV.inc(e.tensor_tensor(
                                out=pr16[:, s % 2, :], in0=hb[:, s % 2, :],
                                in1=bcC[:, s, :], op=ALU.mult))
                            assert sV.c == sv_prod[(si, c, s)]
                        e.wait_ge(sT.h, st_ymm[(si, c, DS - 1)])
                        if c >= 2:
                            e.wait_ge(qG.h, qg_y2[(si, c - 2)])
                        elif si >= 1 and CL - 2 + c >= 0:
                            e.wait_ge(qG.h, qg_y2[(si - 1, CL - 2 + c)])
                        sV.inc(e.scalar_tensor_tensor(
                            out=y2s[:, c % 2, :], in0=ustr[:, c, :],
                            scalar=dpc[:, c:c + 1],
                            in1=psy[:].rearrange("p a b -> p (a b)"),
                            op0=ALU.mult, op1=ALU.add))
                        assert sV.c == sv_y2[(si, c)]

                @block.tensor
                def _(e, si=si):
                    for c in range(CL):
                        if c >= 1:
                            e.wait_ge(sV.h, sv_y2[(si, c - 1)])
                        elif si >= 1:
                            e.wait_ge(sV.h, sv_y2[(si - 1, CL - 1)])
                        for s in range(DS):
                            e.wait_ge(sV.h, sv_prod[(si, c, s)])
                            for yb in range(NYB):
                                ins = e.matmul(
                                    psy[:, yb, :], id16[:],
                                    pr16[:, s % 2, yb * 512:(yb + 1) * 512],
                                    start=(s == 0), stop=(s == DS - 1))
                            sT.inc(ins, NYB)
                            assert sT.c == st_ymm[(si, c, s)]

                @block.gpsimd
                def _(e, si=si, t0=t0):
                    for c in range(CL):
                        e.wait_ge(sV.h, sv_y2[(si, c)])
                        qG.inc(e.dma_start(
                            out=y2_d.ap()[c, :, t0:t0 + ST],
                            in_=y2s[:, c % 2, :]), 16)
                        assert qG.c == qg_y2[(si, c)]

        if stop == "s":
            @block.gpsimd
            def _(e):
                w(e, sA, sV, sT)
                e.wait_ge(qG.h, qG.c)
            esS.close()
            es.close()
            return nc
        esS.close()
        esW = ExitStack()
        _e = esW.enter_context
        wo16 = _e(nc.sbuf_tensor("wo16", [128, NCH, DM], BF16))
        ywt = _e(nc.sbuf_tensor("ywt", [128, NCH, WST], F32))
        zwt = _e(nc.sbuf_tensor("zwt", [128, NCH, WST], F32))
        yg16 = _e(nc.sbuf_tensor("yg16", [128, NCH, WST], BF16))
        mixt = _e(nc.sbuf_tensor("mixt", [128, 2, 512], F32))
        psw = _e(nc.psum_tensor("psw", [128, 2, 512], F32))
        if True:
            with nc.sbuf_tensor("wotmp", [128, NCH, DM], F32) as wotmp:
                @block.sync
                def _(e):
                    w(e, sV, sA, sT)
                    qL.inc(e.dma_start(
                        out=wotmp[:],
                        in_=woutT.rearrange("(k p) n -> p k n", p=128)), 16)

                @block.vector
                def _(e):
                    w(e, qL)
                    sV.inc(e.tensor_copy(out=wo16[:], in_=wotmp[:]))

            for si in range(NWS):
                t0 = si * WST
                NW = 2 * (WST // 128)

                @block.sync
                def _(e, si=si, t0=t0):
                    w(e, sV, sT)
                    for c in range(NCH):
                        qL.inc(e.dma_start(out=ywt[:, c, :],
                                           in_=y2_d.ap()[c, :, t0:t0 + WST]),
                               16)
                        qL.inc(e.dma_start(out=zwt[:, c, :],
                                           in_=zsil_d.ap()[c, :, t0:t0 + WST]),
                               16)

                @block.vector
                def _(e, si=si):
                    w(e, qL)
                    for c in range(NCH):
                        ins = e.tensor_mul(out=yg16[:, c, :], in0=ywt[:, c, :],
                                           in1=zwt[:, c, :])
                    sV.inc(ins)

                @block.tensor
                def _(e, si=si):
                    w(e, sV)
                    saw = sA.c
                    for j in range(NW):
                        tsub, mh = j // 2, j % 2
                        if j >= 2:
                            e.wait_ge(sA.h, saw + j - 1)
                        for c in range(NCH):
                            ins = e.matmul(
                                psw[:, j % 2, :],
                                yg16[:, c, tsub * 128:(tsub + 1) * 128],
                                wo16[:, c, mh * 512:(mh + 1) * 512],
                                start=(c == 0), stop=(c == NCH - 1))
                        sT.inc(ins)

                @block.scalar
                def _(e, si=si):
                    base_t = sT.c - NW
                    qgw = qG.c
                    for j in range(NW):
                        e.wait_ge(sT.h, base_t + j + 1)
                        if j >= 2:
                            e.wait_ge(qG.h, qgw + 16 * (j - 1))
                        sA.inc(e.copy(out=mixt[:, j % 2, :],
                                      in_=psw[:, j % 2, :]))

                @block.gpsimd
                def _(e, si=si, t0=t0):
                    base_a = sA.c - NW

                    def arm(me):
                        for j in range(NW):
                            tsub, mh = j // 2, j % 2
                            e.wait_ge(sA.h, base_a + j + 1)
                            row = me * T + t0 + tsub * 128
                            e.dma_start(
                                out=mix_sh.ap()[row:row + 128,
                                                mh * 512:(mh + 1) * 512],
                                in_=mixt[:, j % 2, :]).then_inc(_qG, 16)

                    slot_branch(e, sslot, lambda: arm(0), lambda: arm(1))
                    qG.c += 16 * NW

            @block.gpsimd
            def _(e):
                w(e, qG)
                rpoll = rpoll_box["r"]

                def arm(me):
                    pt = 1 - me
                    e.dma_start(out=flag_sh.ap()[me, 1, :],
                                in_=snonce[:]).then_inc(_qG, 16)
                    e.wait_ge(_qG, qG.c + 16)
                    poll_partner(e, rpoll, None,
                                 flag_sh.ap()[pt, 1, :],
                                 sflag, snonce, _pp)

                slot_branch(e, sslot, lambda: arm(0), lambda: arm(1))
                qG.c += 16
                pm.inc(e.nop())

        if stop == "w":
            @block.gpsimd
            def _(e):
                w(e, sA, sV, sT)
                e.wait_ge(qG.h, qG.c)
                e.wait_ge(_pm, pm.c)
            esW.close()
            es.close()
            return nc
        esW.close()
        esR = ExitStack()
        _e = esR.enter_context
        rw1s = _e(nc.sbuf_tensor("rw1s", [128, 8, HF], BF16))
        rb1s = _e(nc.sbuf_tensor("rb1s", [128, HF // 128], F32))
        rw2s = _e(nc.sbuf_tensor("rw2s", [128, 32, NE], BF16))
        rb2s = _e(nc.sbuf_tensor("rb2s", [NE, 1], F32))
        lnA = _e(nc.sbuf_tensor("lnA", [128, 2, DM], F32))
        lnB = _e(nc.sbuf_tensor("lnB", [128, 2, DM], F32))
        lnc = _e(nc.sbuf_tensor("lnc", [128, 8], F32))
        pdif = _e(nc.sbuf_tensor("pdif", [128, 4], F32))
        pexp = _e(nc.sbuf_tensor("pexp", [128, 4], F32))
        x1b = _e(nc.sbuf_tensor("x1b", [128, 4, DM], BF16))
        lnL = _e(nc.sbuf_tensor("lnL", [128, 2, 3, DM], F32))
        x1T = _e(nc.sbuf_tensor("x1T", [128, 8, 512], BF16))
        h1b = _e(nc.sbuf_tensor("h1b", [128, 32, 512], BF16))
        lgct = _e(nc.sbuf_tensor("lgct", [NE, 512], F32))
        lgtm = _e(nc.sbuf_tensor("lgtm", [128, 4, NE], F32))
        tka = _e(nc.sbuf_tensor("tka", [128, NE], F32))
        tkb = _e(nc.sbuf_tensor("tkb", [128, NE], F32))
        ivt = _e(nc.sbuf_tensor("ivt", [128, 4, 4], F32))
        pst = _e(nc.psum_tensor("pst", [128, 2, 1024], BF16))
        psh = _e(nc.psum_tensor("psh", [128, 2, 512], F32))
        pslg = _e(nc.psum_tensor("pslg", [NE, 512], F32))
        pstf = _e(nc.psum_tensor("pstf", [128, 2, 512], F32))
        if True:
            with nc.sbuf_tensor("rwtmp", [128, HF], F32) as rwtmp:
                for k in range(8):
                    @block.sync
                    def _(e, k=k):
                        if k == 0:
                            e.wait_ge(pm.h, pm.c)
                            w(e, sV, sA, sT)
                        else:
                            w(e, sV)
                        qL.inc(e.dma_start(
                            out=rwtmp[:],
                            in_=rw1.rearrange("(k p) h -> p k h",
                                              p=128)[:, k, :]), 16)

                    @block.vector
                    def _(e, k=k):
                        w(e, qL)
                        sV.inc(e.tensor_copy(out=rw1s[:, k, :], in_=rwtmp[:]))

                @block.sync
                def _(e):
                    w(e, sV)
                    qL.inc(e.dma_start(
                        out=rwtmp[:, 0:32 * NE].rearrange(
                            "p (k n) -> p k n", k=32),
                        in_=rw2.rearrange("(k p) n -> p k n", p=128)), 16)
                    qL.inc(e.dma_start(out=rb1s[:], in_=rb1v), 16)
                    qL.inc(e.dma_start(out=rb2s[:], in_=rb2v), 16)

                @block.vector
                def _(e):
                    w(e, qL)
                    sV.inc(e.tensor_copy(
                        out=rw2s[:],
                        in_=rwtmp[:, 0:32 * NE]
                            .rearrange("p (k n) -> p k n", k=32)))

            ql_ln, sv_add, qg_st, st_grp = {}, {}, {}, {}

            def ln_loads(e, i):
                tok0 = i * 128

                def arm(me):
                    pt = 1 - me
                    r0 = me * T + me * TH + tok0
                    r1 = pt * T + me * TH + tok0
                    e.dma_start(out=lnL[:, i % 2, 0, :],
                                in_=mix_sh.ap()[r0:r0 + 128, :]
                                ).then_inc(_qL, 16)
                    e.dma_start(out=lnL[:, i % 2, 1, :],
                                in_=mix_sh.ap()[r1:r1 + 128, :]
                                ).then_inc(_qL, 16)

                slot_branch(e, sslot, lambda: arm(0), lambda: arm(1))
                qL.c += 32
                qL.inc(e.dma_start(
                    out=lnL[:, i % 2, 2, :],
                    in_=x_tm[tok0:tok0 + 128, :]), 16)
                ql_ln[i] = qL.c

            @block.sync
            def _(e):
                w(e, sV, qG, sT)
                ln_loads(e, 0)

            for g in range(NRG):
                for tt in range(4):
                    ti = g * 4 + tt
                    tok0 = ti * 128

                    @block.sync
                    def _(e, ti=ti):
                        if ti + 1 < 4 * NRG:
                            if ti >= 1:
                                e.wait_ge(sV.h, sv_add[ti - 1])
                            ln_loads(e, ti + 1)

                    @block.vector
                    def _(e, ti=ti, g=g):
                        e.wait_ge(qL.h, ql_ln[ti])
                        if ti >= 1:
                            e.wait_ge(qG.h, qg_st[ti - 1])
                        if g >= 1:
                            e.wait_ge(sT.h, st_grp[g - 1])
                        sV.inc(e.tensor_add(out=lnA[:, 0, :],
                                            in0=lnL[:, ti % 2, 0, :],
                                            in1=lnL[:, ti % 2, 1, :]))
                        sV.inc(e.tensor_add(out=lnA[:, 0, :],
                                            in0=lnA[:, 0, :],
                                            in1=lnL[:, ti % 2, 2, :]))
                        sv_add[ti] = sV.c
                        sV.inc(e.reduce_sum(out=lnc[:, 0:1], in_=lnA[:, 0, :],
                                            axis=AX))
                        e.wait_ge(sV.h, sV.c)
                        sV.inc(e.tensor_scalar(out=lnc[:, 1:2], in0=lnc[:, 0:1],
                                               scalar1=1.0 / DM, scalar2=None,
                                               op0=ALU.mult))
                        e.wait_ge(sV.h, sV.c)
                        sV.inc(e.tensor_scalar(out=lnB[:, 1, :],
                                               in0=lnA[:, 0, :],
                                               scalar1=lnc[:, 1:2],
                                               scalar2=None,
                                               op0=ALU.subtract))
                        sV.inc(e.tensor_mul(out=lnA[:, 1, :],
                                            in0=lnB[:, 1, :],
                                            in1=lnB[:, 1, :]))
                        sV.inc(e.reduce_sum(out=lnc[:, 2:3], in_=lnA[:, 1, :],
                                            axis=AX))
                        e.wait_ge(sV.h, sV.c)
                        sV.inc(e.tensor_scalar(out=lnc[:, 3:4], in0=lnc[:, 2:3],
                                               scalar1=1.0 / DM, scalar2=None,
                                               op0=ALU.mult))

                    @block.scalar
                    def _(e):
                        w(e, sV)
                        sA.inc(e.activation(out=lnc[:, 4:5], in_=lnc[:, 3:4],
                                            func=AF.Ln, bias=lncst[:, 0:1]))
                        e.wait_ge(sA.h, sA.c)
                        sA.inc(e.activation(out=lnc[:, 5:6], in_=lnc[:, 4:5],
                                            func=AF.Exp, scale=lncst[:, 1:2]))

                    @block.vector
                    def _(e, tt=tt):
                        w(e, sA)
                        sV.inc(e.tensor_scalar(out=lnA[:, 0, :],
                                               in0=lnB[:, 1, :],
                                               scalar1=lnc[:, 5:6],
                                               scalar2=None,
                                               op0=ALU.mult))
                        e.wait_ge(sV.h, sV.c)
                        sV.inc(e.tensor_copy(out=x1b[:, tt, :],
                                             in_=lnA[:, 0, :]))

                    @block.gpsimd
                    def _(e, tok0=tok0, g=g, tt=tt):
                        w(e, sV)
                        qG.inc(e.dma_start(out=x1out[tok0:tok0 + 128, :],
                                           in_=lnA[:, 0, :]), 16)
                        qg_st[ti] = qG.c
                        if dbg and g == 0 and tt == 0:
                            qG.inc(e.dma_start(out=dbg_ln[:, 0:8],
                                               in_=lnc[:]), 16)
                            qG.inc(e.dma_start(out=dbg_ln[:, 8:8 + DM],
                                               in_=lnB[:, 1, :]), 16)
                            qG.inc(e.dma_start(out=dbg_ln[:, 8 + DM:],
                                               in_=lnA[:, 1, :]), 16)

                @block.tensor
                def _(e, g=g):
                    w(e, sV)
                    svw = sV.c
                    for j in range(32):
                        tt, mc = j // 8, j % 8
                        if j >= 2:
                            e.wait_ge(sV.h, svw + j - 1)
                        sT.inc(e.transpose(pst[:, j % 2, 0:128],
                                           x1b[:, tt, mc * 128:(mc + 1) * 128],
                                           id16[:]))
                st_grp[g] = sT.c

                @block.vector
                def _(e, g=g):
                    base_t = sT.c - 32
                    for j in range(32):
                        tt, mc = j // 8, j % 8
                        e.wait_ge(sT.h, base_t + j + 1)
                        sV.inc(e.tensor_copy(
                            out=x1T[:, mc, tt * 128:(tt + 1) * 128],
                            in_=pst[:, j % 2, 0:128]))

                @block.tensor
                def _(e, g=g):
                    w(e, sV)
                    saw = sA.c
                    for hc in range(32):
                        if hc >= 2:
                            e.wait_ge(sA.h, saw + hc - 1)
                        for k in range(8):
                            ins = e.matmul(psh[:, hc % 2, :],
                                           rw1s[:, k, hc * 128:(hc + 1) * 128],
                                           x1T[:, k, :],
                                           start=(k == 0), stop=(k == 7))
                        sT.inc(ins)

                @block.scalar
                def _(e, g=g):
                    base_t = sT.c - 32
                    for hc in range(32):
                        e.wait_ge(sT.h, base_t + hc + 1)
                        sA.inc(e.activation(out=h1b[:, hc, :],
                                            in_=psh[:, hc % 2, :],
                                            func=AF.Relu,
                                            bias=rb1s[:, hc:hc + 1]))

                @block.tensor
                def _(e, g=g):
                    base_a = sA.c - 32
                    for k2 in range(32):
                        e.wait_ge(sA.h, base_a + k2 + 1)
                        ins = e.matmul(pslg[:], rw2s[:, k2, :], h1b[:, k2, :],
                                       start=(k2 == 0), stop=(k2 == 31))
                    sT.inc(ins)

                @block.vector
                def _(e, g=g):
                    w(e, sT)
                    sV.inc(e.tensor_scalar(out=lgct[:], in0=pslg[:],
                                           scalar1=rb2s[:, 0:1], scalar2=None,
                                           op0=ALU.add))

                @block.tensor
                def _(e, g=g):
                    w(e, sV)
                    svw = sV.c
                    for tt in range(4):
                        if tt >= 2:
                            e.wait_ge(sV.h, svw + tt - 1)
                        sT.inc(e.transpose(pstf[:, tt % 2, 0:NE],
                                           lgct[:, tt * 128:(tt + 1) * 128],
                                           idf[0:NE, 0:NE]))

                @block.vector
                def _(e, g=g):
                    base_t = sT.c - 4
                    for tt in range(4):
                        e.wait_ge(sT.h, base_t + tt + 1)
                        sV.inc(e.tensor_copy(out=lgtm[:, tt, :],
                                             in_=pstf[:, tt % 2, 0:NE]))

                @block.vector
                def _(e, g=g):
                    def b(ins):
                        sV.inc(ins)
                        e.wait_ge(sV.h, sV.c)
                    for tt in range(4):
                        lg = lgtm[:, tt, :]
                        b(e.reduce_max(out=lnc[:, 0:1], in_=lg, axis=AX))
                        b(e.tensor_scalar(out=tka[:], in0=lg,
                                          scalar1=lnc[:, 0:1],
                                          scalar2=None, op0=ALU.is_ge))
                        b(e.scalar_tensor_tensor(out=tka[:], in0=tka[:],
                                                 scalar=-1024.0, in1=io8b[:],
                                                 op0=ALU.mult, op1=ALU.add))
                        b(e.tensor_reduce(out=lnc[:, 1:2], in_=tka[:], axis=AX,
                                          op=ALU.min))
                        b(e.tensor_scalar(out=ivt[:, tt, 0:1], in0=lnc[:, 1:2],
                                          scalar1=1024.0, scalar2=None,
                                          op0=ALU.add))
                        b(e.tensor_scalar(out=tka[:], in0=io8b[:],
                                          scalar1=ivt[:, tt, 0:1], scalar2=None,
                                          op0=ALU.is_equal))
                        b(e.scalar_tensor_tensor(out=tkb[:], in0=tka[:],
                                                 scalar=-2e30, in1=lg,
                                                 op0=ALU.mult, op1=ALU.add))
                        b(e.reduce_max(out=lnc[:, 2:3], in_=tkb[:], axis=AX))
                        b(e.tensor_scalar(out=tka[:], in0=tkb[:],
                                          scalar1=lnc[:, 2:3], scalar2=None,
                                          op0=ALU.is_ge))
                        b(e.scalar_tensor_tensor(out=tka[:], in0=tka[:],
                                                 scalar=-1024.0, in1=io8b[:],
                                                 op0=ALU.mult, op1=ALU.add))
                        b(e.tensor_reduce(out=lnc[:, 3:4], in_=tka[:], axis=AX,
                                          op=ALU.min))
                        b(e.tensor_scalar(out=ivt[:, tt, 1:2], in0=lnc[:, 3:4],
                                          scalar1=1024.0, scalar2=None,
                                          op0=ALU.add))
                        b(e.tensor_sub(out=pdif[:, tt:tt + 1],
                                       in0=lnc[:, 2:3], in1=lnc[:, 0:1]))

                @block.scalar
                def _(e, g=g):
                    w(e, sV)
                    sA.inc(e.activation(out=pexp[:], in_=pdif[:], func=AF.Exp))

                @block.vector
                def _(e, g=g):
                    w(e, sA)
                    sV.inc(e.tensor_scalar(out=tka[:, 0:4], in0=pexp[:],
                                           scalar1=1.0, scalar2=None,
                                           op0=ALU.add))
                    e.wait_ge(sV.h, sV.c)
                    sV.inc(e.reciprocal(out=pdif[:], in_=tka[:, 0:4]))
                    e.wait_ge(sV.h, sV.c)
                    for tt in range(4):
                        e.tensor_copy(out=ivt[:, tt, 2:3],
                                      in_=pdif[:, tt:tt + 1])
                        ins = e.tensor_mul(out=ivt[:, tt, 3:4],
                                           in0=pexp[:, tt:tt + 1],
                                           in1=pdif[:, tt:tt + 1])
                    sV.inc(ins)

                @block.gpsimd
                def _(e, g=g):
                    w(e, sV)
                    for tt in range(4):
                        tok0 = g * 512 + tt * 128
                        qG.inc(e.dma_start(out=ivout[tok0:tok0 + 128, :],
                                           in_=ivt[:, tt, :]), 16)

            if dbg:
                @block.sync
                def _(e):
                    w(e, qG, sA, sV)
                    qL.inc(e.dma_start(out=dbg_u, in_=u_d.ap()), 16)
                    qL.inc(e.dma_start(out=dbg_z, in_=zsil_d.ap()), 16)
                    qL.inc(e.dma_start(out=dbg_y2, in_=y2_d.ap()), 16)
                    qL.inc(e.dma_start(out=dbg_mix, in_=mix_sh.ap()), 16)
                    e.wait_ge(qL.h, qL.c)

            @block.gpsimd
            def _(e):
                e.wait_ge(qG.h, qG.c)

        esR.close()
    es.close()
    return nc


def build_L2(N=32768, stop='full'):
    NK = 2 * N
    NSTRIP = 16
    SW = NK // NSTRIP
    nc = bass.Bass("TRN2", target_bir_lowering=False, debug=False,
                   detect_race_conditions=False)

    ein = lambda n, s, d=F32: nc.dram_tensor(n, s, d, kind="ExternalInput").ap()
    x1f = ein("x1f", [N, DM])
    idsf = ein("idsf", [1, NK])
    tokv = ein("tokv", [1, NK], I32)
    ew1in = ein("ew1in", [DM, HF])
    ew2in = ein("ew2in", [HF, DM])
    eb1v = ein("eb1v", [128, 32])
    eb2v = ein("eb2v", [1, DM])
    emodv = ein("emodv", [128, 1])
    spatv = ein("spatv", [128, 128])
    selv = ein("selv", [128, 16])
    identv = ein("identv", [128, 128])

    eoout = nc.dram_tensor("eoout", [CAP, DM], F32, kind="ExternalOutput").ap()

    slot_d = nc.dram_tensor("slot_d", [NK, 1], I32)
    src_d = nc.dram_tensor("src_d", [CAP + 128, 1], I32)
    disp_d = nc.dram_tensor("disp_d", [CAP, DM], BF16)
    ew1b_d = nc.dram_tensor("ew1b_d", [8, 128, HF], BF16)
    ew2b_d = nc.dram_tensor("ew2b_d", [32, 128, DM], BF16)

    es = ExitStack()
    _e = es.enter_context
    emod = _e(nc.sbuf_tensor("emod", [128, 1], F32))
    spat = _e(nc.sbuf_tensor("spat", [128, 128], F32))
    sel = _e(nc.sbuf_tensor("sel", [128, 16], F32))
    id16 = _e(nc.sbuf_tensor("id16", [128, 128], BF16))
    idf = _e(nc.sbuf_tensor("idf", [128, 128], F32))
    eb1s = _e(nc.sbuf_tensor("eb1s", [128, 32], F32))
    eb2b = _e(nc.sbuf_tensor("eb2b", [128, DM], F32))
    _qL = _e(nc.semaphore("qL")); _qG = _e(nc.semaphore("qG"))
    _sV = _e(nc.semaphore("sV")); _sA = _e(nc.semaphore("sA"))
    _sT = _e(nc.semaphore("sT"))
    block = _e(nc.Block())
    if True:
        qL, qG, sV, sA, sT = Sem(_qL), Sem(_qG), Sem(_sV), Sem(_sA), Sem(_sT)
        rbnd_box = {}

        @block.gpsimd
        def _(e):
            r1 = e.alloc_register(rn("rbndcap"))
            e.reg_mov(r1, CAP - 1)
            r2 = e.alloc_register(rn("rbndn"))
            e.reg_mov(r2, N - 1)
            rbnd_box["cap"] = r1
            rbnd_box["n"] = r2

        @block.sync
        def _(e):
            for t_, src in ((emod, emodv), (spat, spatv), (sel, selv),
                            (idf, identv), (eb1s, eb1v)):
                qL.inc(e.dma_start(out=t_[:], in_=src), 16)
            qL.inc(e.dma_start(out=eb2b[:],
                               in_=eb2v.partition_broadcast(128)), 16)

        @block.vector
        def _(e):
            w(e, qL)
            sV.inc(e.tensor_copy(out=id16[:], in_=idf[:]))

        with nc.sbuf_tensor("wt32", [128, 2, HF], F32) as wt32, \
             nc.sbuf_tensor("wt16", [128, 2, HF], BF16) as wt16:
            w1v = ew1in.rearrange("(k p) h -> p k h", p=128)
            w2v = ew2in.rearrange("(k p) m -> p k m", p=128)
            seq = [(w1v, ew1b_d, HF, k) for k in range(8)] + \
                  [(w2v, ew2b_d, DM, k) for k in range(32)]
            ldq, csv, stg = {}, {}, {}
            pl_, pv_, pg_ = qL.c, sV.c, qG.c
            for j in range(len(seq)):
                pl_ += 16; ldq[j] = pl_
                pv_ += 1; csv[j] = pv_
                pg_ += 16; stg[j] = pg_
            for j, (src, dst, cols, k) in enumerate(seq):
                @block.sync
                def _(e, j=j, src=src, cols=cols, k=k):
                    if j == 0:
                        w(e, sV, qG)
                    if j >= 2:
                        e.wait_ge(sV.h, csv[j - 2])
                    qL.inc(e.dma_start(out=wt32[:, j % 2, 0:cols],
                                       in_=src[:, k, :]), 16)
                    assert qL.c == ldq[j]

                @block.vector
                def _(e, j=j, cols=cols):
                    e.wait_ge(qL.h, ldq[j])
                    if j >= 2:
                        e.wait_ge(qG.h, stg[j - 2])
                    sV.inc(e.tensor_copy(out=wt16[:, j % 2, 0:cols],
                                         in_=wt32[:, j % 2, 0:cols]))
                    assert sV.c == csv[j]

                @block.gpsimd
                def _(e, j=j, dst=dst, cols=cols, k=k):
                    e.wait_ge(sV.h, csv[j])
                    qG.inc(e.dma_start(out=dst.ap()[k, :, :],
                                       in_=wt16[:, j % 2, 0:cols]), 16)
                    assert qG.c == stg[j]

        esr = ExitStack()
        _e2 = esr.enter_context
        idt = _e2(nc.sbuf_tensor("idt", [128, SW], F32))
        ones = _e2(nc.sbuf_tensor("ones", [128, SW], F32))
        oneh = _e2(nc.sbuf_tensor("oneh", [128, SW], F32))
        cums = _e2(nc.sbuf_tensor("cums", [128, SW], F32))
        ocol = _e2(nc.sbuf_tensor("ocol", [128, 1], F32))
        mpos = _e2(nc.sbuf_tensor("mpos", [16, SW], F32))
        mhot = _e2(nc.sbuf_tensor("mhot", [16, SW], F32))
        slotf = _e2(nc.sbuf_tensor("slotf", [16, SW], F32))
        sloti = _e2(nc.sbuf_tensor("sloti", [16, SW], I32))
        ztile = _e2(nc.sbuf_tensor("ztile", [1, CAP + 128], I32))
        pso = _e2(nc.psum_tensor("pso", [128, 1], F32))
        psel = _e2(nc.psum_tensor("psel", [16, 512], F32))
        if True:
            @block.sync
            def _(e):
                w(e, sV, qG)
                for st in range(NSTRIP):
                    qL.inc(e.dma_start(
                        out=idt[st * 8:(st + 1) * 8, :],
                        in_=idsf[:, st * SW:(st + 1) * SW]
                            .partition_broadcast(8)), 16)

            @block.vector
            def _(e):
                w(e, qL)
                sV.inc(e.memset(ones[:], 1.0))
                e.wait_ge(sV.h, sV.c)
                sV.inc(e.tensor_scalar(out=oneh[:], in0=idt[:],
                                       scalar1=emod[:, 0:1],
                                       scalar2=None, op0=ALU.is_equal))
                e.wait_ge(sV.h, sV.c)
                sV.inc(e.tensor_tensor_scan(out=cums[:], data0=ones[:],
                                            data1=oneh[:], initial=0.0,
                                            op0=ALU.mult, op1=ALU.add))

            @block.tensor
            def _(e):
                w(e, sV)
                sT.inc(e.matmul(pso[:], spat[:], cums[:, SW - 1:SW],
                                start=True, stop=True))

            @block.vector
            def _(e):
                w(e, sT)
                sV.inc(e.tensor_copy(out=ocol[:], in_=pso[:]))
                e.wait_ge(sV.h, sV.c)
                sV.inc(e.tensor_scalar(out=cums[:], in0=cums[:],
                                       scalar1=ocol[:, 0:1], scalar2=None,
                                       op0=ALU.add))
                e.wait_ge(sV.h, sV.c)
                sV.inc(e.tensor_mul(out=ones[:], in0=oneh[:], in1=cums[:]))

            @block.tensor
            def _(e):
                w(e, sV)
                for j in range(SW // 512):
                    sT.inc(e.matmul(psel[:], sel[:],
                                    ones[:, j * 512:(j + 1) * 512],
                                    start=True, stop=True))
                    e.wait_ge(sV.h, sV.c + j + 1)
                for j in range(SW // 512):
                    sT.inc(e.matmul(psel[:], sel[:],
                                    oneh[:, j * 512:(j + 1) * 512],
                                    start=True, stop=True))
                    e.wait_ge(sV.h, sV.c + SW // 512 + j + 1)

            @block.vector
            def _(e):
                base_t = sT.c - 2 * (SW // 512)
                for j in range(SW // 512):
                    e.wait_ge(sT.h, base_t + j + 1)
                    sV.inc(e.tensor_copy(out=mpos[:, j * 512:(j + 1) * 512],
                                         in_=psel[:]))
                for j in range(SW // 512):
                    e.wait_ge(sT.h, base_t + SW // 512 + j + 1)
                    sV.inc(e.tensor_copy(out=mhot[:, j * 512:(j + 1) * 512],
                                         in_=psel[:]))

            @block.vector
            def _(e):
                w(e, sV)
                def b(ins):
                    sV.inc(ins)
                    e.wait_ge(sV.h, sV.c)
                b(e.tensor_scalar(out=mpos[:], in0=mpos[:], scalar1=1.0,
                                  scalar2=None, op0=ALU.subtract))
                b(e.tensor_scalar(out=slotf[:], in0=mpos[:],
                                  scalar1=float(CAP),
                                  scalar2=None, op0=ALU.is_lt))
                b(e.tensor_mul(out=slotf[:], in0=slotf[:], in1=mhot[:]))
                b(e.tensor_mul(out=mpos[:], in0=mpos[:], in1=slotf[:]))
                b(e.tensor_scalar(out=slotf[:], in0=slotf[:], scalar1=-1.0,
                                  scalar2=1.0, op0=ALU.mult, op1=ALU.add))
                b(e.scalar_tensor_tensor(out=slotf[:], in0=slotf[:],
                                         scalar=BIGI * 1.0, in1=mpos[:],
                                         op0=ALU.mult, op1=ALU.add))
                b(e.tensor_copy(out=sloti[:], in_=slotf[:]))
                sV.inc(e.memset(ztile[:], 0))

            @block.gpsimd
            def _(e):
                w(e, sV)
                qG.inc(e.dma_start(
                    out=slot_d.ap().rearrange("(s f) b -> s (f b)", s=16),
                    in_=sloti[:]), 16)
                qG.inc(e.dma_start(out=src_d.ap().rearrange("a b -> b a"),
                                   in_=ztile[:]), 16)

        if stop == "r":
            @block.gpsimd
            def _(e):
                w(e, sA, sV, sT)
                e.wait_ge(qG.h, qG.c)
            esr.close()
            es.close()
            return nc
        esr.close()
        NG8 = NK // 1024
        ess = ExitStack()
        ic8 = ess.enter_context(nc.sbuf_tensor("ic8", [128, 2, 8], I32))
        vc8 = ess.enter_context(nc.sbuf_tensor("vc8", [128, 2, 8], I32))
        if True:
            slot_v = slot_d.ap().rearrange("(G p f) b -> p G (f b)",
                                           p=128, f=8)
            tok_v = tokv.rearrange("a (G p f) -> a p G f", p=128, f=8)
            for g8 in range(NG8):
                @block.sync
                def _(e, g8=g8):
                    if g8 == 0:
                        w(e, qG)
                    elif g8 >= 2:
                        e.wait_ge(qG.h, qG.c - 16 * 8)
                    qL.inc(e.dma_start(
                        out=ic8[:, g8 % 2, :],
                        in_=slot_v[:, g8, :]), 16)
                    qL.inc(e.dma_start(
                        out=vc8[:, g8 % 2, :],
                        in_=tok_v[0, :, g8, :]), 16)

                @block.gpsimd
                def _(e, g8=g8):
                    w(e, qL)
                    for f in range(8):
                        qG.inc(e.indirect_dma_start(
                            out=src_d.ap(),
                            out_offset=IndirectOffsetOnAxis(
                                ap=ic8[:, g8 % 2, f:f + 1], axis=0),
                            in_=vc8[:, g8 % 2, f:f + 1], in_offset=None,
                            bounds_check=rbnd_box["cap"], oob_is_err=False),
                            16)

        if stop == "sc":
            @block.gpsimd
            def _(e):
                w(e, sA, sV, sT)
                e.wait_ge(qG.h, qG.c)
            ess.close()
            es.close()
            return nc
        ess.close()
        esg = ExitStack()
        sidx = esg.enter_context(nc.sbuf_tensor("sidx", [128, 32], I32))
        xg = esg.enter_context(nc.sbuf_tensor("xg", [128, 2, DM], F32))
        xg16 = esg.enter_context(nc.sbuf_tensor("xg16", [128, 2, DM], BF16))
        if True:
            @block.sync
            def _(e):
                w(e, qG)
                for sg in range(32):
                    qL.inc(e.dma_start(
                        out=sidx[:, sg:sg + 1],
                        in_=src_d.ap()[sg * 128:(sg + 1) * 128, :]), 16)

            with nc.sbuf_tensor("eq0", [128, 32], I32) as eq0:
                @block.vector
                def _(e):
                    w(e, qL)
                    sV.inc(e.tensor_scalar(out=eq0[:], in0=sidx[:], scalar1=0,
                                           scalar2=None, op0=ALU.is_equal))
                    e.wait_ge(sV.h, sV.c)
                    sV.inc(e.scalar_tensor_tensor(out=sidx[:], in0=eq0[:],
                                                  scalar=BIGI, in1=sidx[:],
                                                  op0=ALU.mult, op1=ALU.add))
                    e.wait_ge(sV.h, sV.c)
                    sV.inc(e.tensor_scalar(out=sidx[:], in0=sidx[:], scalar1=1,
                                           scalar2=None, op0=ALU.subtract))

            for sg in range(32):
                @block.vector
                def _(e, sg=sg):
                    if sg >= 2:
                        w(e, qG)
                    sV.inc(e.memset(xg[:, sg % 2, :], 0.0))

                @block.gpsimd
                def _(e, sg=sg):
                    w(e, sV)
                    qG.inc(e.indirect_dma_start(
                        out=xg[:, sg % 2, :], out_offset=None,
                        in_=x1f,
                        in_offset=IndirectOffsetOnAxis(ap=sidx[:, sg:sg + 1],
                                                       axis=0),
                        bounds_check=rbnd_box["n"], oob_is_err=False), 16)

                @block.vector
                def _(e, sg=sg):
                    w(e, qG)
                    sV.inc(e.tensor_copy(out=xg16[:, sg % 2, :],
                                         in_=xg[:, sg % 2, :]))

                @block.gpsimd
                def _(e, sg=sg):
                    w(e, sV)
                    qG.inc(e.dma_start(
                        out=disp_d.ap()[sg * 128:(sg + 1) * 128, :],
                        in_=xg16[:, sg % 2, :]), 16)

        if stop == "g":
            @block.gpsimd
            def _(e):
                w(e, sA, sV, sT)
                e.wait_ge(qG.h, qG.c)
            esg.close()
            es.close()
            return nc
        esg.close()
        esx = ExitStack()
        _e3 = esx.enter_context
        ew1s = _e3(nc.sbuf_tensor("ew1s", [128, 8, HF], BF16))
        dtile = _e3(nc.sbuf_tensor("dtile", [128, 4, DM], BF16))
        dT = _e3(nc.sbuf_tensor("dT", [128, 8, 512], BF16))
        h1s = _e3(nc.sbuf_tensor("h1s", [128, 32, 512], BF16))
        ew2r = _e3(nc.sbuf_tensor("ew2r", [128, 32, DM], BF16))
        eot = _e3(nc.sbuf_tensor("eot", [128, 2, 512], F32))
        pst2 = _e3(nc.psum_tensor("pst2", [128, 2, 128], BF16))
        psh2 = _e3(nc.psum_tensor("psh2", [128, 2, 512], F32))
        pse = _e3(nc.psum_tensor("pse", [128, 2, 512], F32))
        if True:
            @block.sync
            def _(e):
                w(e, qG)
                for k in range(8):
                    qL.inc(e.dma_start(out=ew1s[:, k, :],
                                       in_=ew1b_d.ap()[k, :, :]), 16)
                for k2 in range(32):
                    qL.inc(e.dma_start(out=ew2r[:, k2, :],
                                       in_=ew2b_d.ap()[k2, :, :]), 16)

            for st in range(8):
                @block.sync
                def _(e, st=st):
                    w(e, sT)
                    for j in range(4):
                        qL.inc(e.dma_start(
                            out=dtile[:, j, :],
                            in_=disp_d.ap()[st * 512 + j * 128:
                                            st * 512 + (j + 1) * 128, :]), 16)

                @block.sync
                def _(e, st=st):
                    w(e, sT, qL)
                    for j in range(32):
                        tt, mc = j // 8, j % 8
                        qL.inc(e.dma_start(
                            out=dT[:, mc, tt * 128:(tt + 1) * 128],
                            in_=dtile[:, tt, mc * 128:(mc + 1) * 128],
                            transpose=True), 16)

                @block.tensor
                def _(e, st=st):
                    w(e, sV, qL)
                    saw = sA.c
                    for hc in range(32):
                        if hc >= 2:
                            e.wait_ge(sA.h, saw + hc - 1)
                        for k in range(8):
                            ins = e.matmul(psh2[:, hc % 2, :],
                                           ew1s[:, k, hc * 128:(hc + 1) * 128],
                                           dT[:, k, :],
                                           start=(k == 0), stop=(k == 7))
                        sT.inc(ins)

                @block.scalar
                def _(e, st=st):
                    base_t = sT.c - 32
                    for hc in range(32):
                        e.wait_ge(sT.h, base_t + hc + 1)
                        sA.inc(e.activation(out=h1s[:, hc, :],
                                            in_=psh2[:, hc % 2, :],
                                            func=AF.Relu,
                                            bias=eb1s[:, hc:hc + 1]))

                for mh in range(2):
                    @block.tensor
                    def _(e, st=st, mh=mh):
                        w(e, qL, sA)
                        svw = sV.c
                        for ss in range(4):
                            if ss >= 2:
                                e.wait_ge(sV.h, svw + ss - 1)
                            for k2 in range(32):
                                ins = e.matmul(
                                    pse[:, ss % 2, :],
                                    h1s[:, k2, ss * 128:(ss + 1) * 128],
                                    ew2r[:, k2, mh * 512:(mh + 1) * 512],
                                    start=(k2 == 0), stop=(k2 == 31))
                            sT.inc(ins)

                    @block.vector
                    def _(e, st=st, mh=mh):
                        base_t = sT.c - 4
                        for ss in range(4):
                            e.wait_ge(sT.h, base_t + ss + 1)
                            if ss >= 2:
                                w(e, qG)
                            sV.inc(e.tensor_add(
                                out=eot[:, ss % 2, :], in0=pse[:, ss % 2, :],
                                in1=eb2b[:, mh * 512:(mh + 1) * 512]))

                    @block.gpsimd
                    def _(e, st=st, mh=mh):
                        base_v = sV.c - 4
                        for ss in range(4):
                            e.wait_ge(sV.h, base_v + ss + 1)
                            qG.inc(e.dma_start(
                                out=eoout[st * 512 + ss * 128:
                                          st * 512 + (ss + 1) * 128,
                                          mh * 512:(mh + 1) * 512],
                                in_=eot[:, ss % 2, :]), 16)

            @block.gpsimd
            def _(e):
                w(e, sA, sV, sT, qL)
                e.wait_ge(qG.h, qG.c)

        esx.close()
    es.close()
    return nc


def build_L3(N=32768, NTOK=4096):
    NK = 2 * N
    NSTRIP = 16
    SW = NK // NSTRIP
    nc = bass.Bass("TRN2", target_bir_lowering=False, debug=False,
                   detect_race_conditions=False)

    ein = lambda n, s, d=F32: nc.dram_tensor(n, s, d, kind="ExternalInput").ap()
    eof = ein("eof", [NE * CAP, DM])
    x1s = ein("x1s", [NTOK, DM])
    idsf = ein("idsf", [1, NK])
    p0s = ein("p0s", [16, N // 16])
    p1s = ein("p1s", [16, N // 16])
    emodv = ein("emodv", [128, 1])
    spatv = ein("spatv", [128, 128])
    sumpv = ein("sumpv", [128, 16])
    selpv = ein("selpv", [128, 16])
    stripev = ein("stripev", [1, 1], I32)
    iotacv = ein("iotacv", [128, 1], I32)
    lncv3 = ein("lncv3", [128, 2])

    out = nc.dram_tensor("out", [NTOK, DM], F32, kind="ExternalOutput").ap()

    idx0_d = nc.dram_tensor("idx0_d", [N, 1], I32)
    idx1_d = nc.dram_tensor("idx1_d", [N, 1], I32)
    pk0_d = nc.dram_tensor("pk0_d", [N, 1], F32)
    pk1_d = nc.dram_tensor("pk1_d", [N, 1], F32)

    es = ExitStack()
    _e = es.enter_context
    emod = _e(nc.sbuf_tensor("emod", [128, 1], F32))
    spat = _e(nc.sbuf_tensor("spat", [128, 128], F32))
    sump = _e(nc.sbuf_tensor("sump", [128, 16], F32))
    selp = _e(nc.sbuf_tensor("selp", [128, 16], F32))
    lncst3 = _e(nc.sbuf_tensor("lncst3", [128, 2], F32))
    _qL = _e(nc.semaphore("qL")); _qG = _e(nc.semaphore("qG"))
    _sV = _e(nc.semaphore("sV")); _sA = _e(nc.semaphore("sA"))
    _sT = _e(nc.semaphore("sT"))
    block = _e(nc.Block())
    if True:
        qL, qG, sV, sA, sT = Sem(_qL), Sem(_qG), Sem(_sV), Sem(_sA), Sem(_sT)

        @block.sync
        def _(e):
            for t_, src in ((emod, emodv), (spat, spatv), (sump, sumpv),
                            (selp, selpv), (lncst3, lncv3)):
                qL.inc(e.dma_start(out=t_[:], in_=src), 16)

        esr = ExitStack()
        _e2 = esr.enter_context
        idt = _e2(nc.sbuf_tensor("idt", [128, SW], F32))
        ones = _e2(nc.sbuf_tensor("ones", [128, SW], F32))
        oneh = _e2(nc.sbuf_tensor("oneh", [128, SW], F32))
        cums = _e2(nc.sbuf_tensor("cums", [128, SW], F32))
        ocol = _e2(nc.sbuf_tensor("ocol", [128, 1], F32))
        post = _e2(nc.sbuf_tensor("post", [16, SW], F32))
        ids16 = _e2(nc.sbuf_tensor("ids16", [16, SW], F32))
        keep = _e2(nc.sbuf_tensor("keep", [16, SW], F32))
        dei = _e2(nc.sbuf_tensor("dei", [16, SW], F32))
        deii = _e2(nc.sbuf_tensor("deii", [16, SW], I32))
        pso = _e2(nc.psum_tensor("pso", [128, 1], F32))
        psel = _e2(nc.psum_tensor("psel", [16, 512], F32))
        if True:
            @block.sync
            def _(e):
                w(e, qL)
                for st in range(NSTRIP):
                    qL.inc(e.dma_start(
                        out=idt[st * 8:(st + 1) * 8, :],
                        in_=idsf[:, st * SW:(st + 1) * SW]
                            .partition_broadcast(8)), 16)

            @block.vector
            def _(e):
                w(e, qL)
                sV.inc(e.memset(ones[:], 1.0))
                e.wait_ge(sV.h, sV.c)
                sV.inc(e.tensor_scalar(out=oneh[:], in0=idt[:],
                                       scalar1=emod[:, 0:1],
                                       scalar2=None, op0=ALU.is_equal))
                e.wait_ge(sV.h, sV.c)
                sV.inc(e.tensor_tensor_scan(out=cums[:], data0=ones[:],
                                            data1=oneh[:], initial=0.0,
                                            op0=ALU.mult, op1=ALU.add))

            @block.tensor
            def _(e):
                w(e, sV)
                sT.inc(e.matmul(pso[:], spat[:], cums[:, SW - 1:SW],
                                start=True, stop=True))

            @block.vector
            def _(e):
                w(e, sT)
                sV.inc(e.tensor_copy(out=ocol[:], in_=pso[:]))
                e.wait_ge(sV.h, sV.c)
                sV.inc(e.tensor_scalar(out=cums[:], in0=cums[:],
                                       scalar1=ocol[:, 0:1],
                                       scalar2=None, op0=ALU.add))
                e.wait_ge(sV.h, sV.c)
                sV.inc(e.tensor_mul(out=ones[:], in0=oneh[:], in1=cums[:]))

            @block.tensor
            def _(e):
                w(e, sV)
                for j in range(SW // 512):
                    sT.inc(e.matmul(psel[:], sump[:],
                                    ones[:, j * 512:(j + 1) * 512],
                                    start=True, stop=True))
                    e.wait_ge(sV.h, sV.c + j + 1)
                for j in range(SW // 512):
                    sT.inc(e.matmul(psel[:], selp[:],
                                    idt[:, j * 512:(j + 1) * 512],
                                    start=True, stop=True))
                    e.wait_ge(sV.h, sV.c + SW // 512 + j + 1)

            @block.vector
            def _(e):
                base_t = sT.c - 2 * (SW // 512)
                for j in range(SW // 512):
                    e.wait_ge(sT.h, base_t + j + 1)
                    sV.inc(e.tensor_copy(out=post[:, j * 512:(j + 1) * 512],
                                         in_=psel[:]))
                for j in range(SW // 512):
                    e.wait_ge(sT.h, base_t + SW // 512 + j + 1)
                    sV.inc(e.tensor_copy(out=ids16[:, j * 512:(j + 1) * 512],
                                         in_=psel[:]))

            @block.vector
            def _(e):
                w(e, sV)
                def b(ins):
                    sV.inc(ins)
                    e.wait_ge(sV.h, sV.c)
                b(e.tensor_scalar(out=post[:], in0=post[:], scalar1=1.0,
                                  scalar2=None, op0=ALU.subtract))
                b(e.tensor_scalar(out=keep[:], in0=post[:], scalar1=float(CAP),
                                  scalar2=None, op0=ALU.is_lt))
                b(e.tensor_scalar(out=post[:], in0=post[:],
                                  scalar1=float(CAP - 1),
                                  scalar2=None, op0=ALU.min))
                b(e.scalar_tensor_tensor(out=post[:], in0=ids16[:],
                                         scalar=float(CAP), in1=post[:],
                                         op0=ALU.mult, op1=ALU.add))

            @block.vector
            def _(e):
                sV.inc(e.tensor_copy(out=dei[:, 0:SW // 2],
                                     in_=post[:].rearrange("p (f k) -> p k f",
                                                           k=2)[:, 0, :]))
                e.wait_ge(sV.h, sV.c)
                sV.inc(e.tensor_copy(out=deii[:, 0:SW // 2],
                                     in_=dei[:, 0:SW // 2]))

            @block.gpsimd
            def _(e):
                w(e, sV)
                qG.inc(e.dma_start(
                    out=idx0_d.ap().rearrange("(s f) b -> s (f b)", s=16),
                    in_=deii[:, 0:SW // 2]), 16)

            @block.vector
            def _(e):
                w(e, qG)
                sV.inc(e.tensor_copy(out=dei[:, 0:SW // 2],
                                     in_=post[:].rearrange("p (f k) -> p k f",
                                                           k=2)[:, 1, :]))
                e.wait_ge(sV.h, sV.c)
                sV.inc(e.tensor_copy(out=deii[:, 0:SW // 2],
                                     in_=dei[:, 0:SW // 2]))

            @block.gpsimd
            def _(e):
                w(e, sV, qG)
                qG.inc(e.dma_start(
                    out=idx1_d.ap().rearrange("(s f) b -> s (f b)", s=16),
                    in_=deii[:, 0:SW // 2]), 16)

            pin = esr.enter_context(nc.sbuf_tensor("pin", [16, SW // 2], F32))
            if True:
                @block.sync
                def _(e):
                    w(e, sV)
                    qL.inc(e.dma_start(out=pin[:], in_=p0s), 16)

                @block.vector
                def _(e):
                    w(e, qL, qG)
                    sV.inc(e.tensor_copy(out=dei[:, 0:SW // 2],
                                         in_=keep[:].rearrange(
                                             "p (f k) -> p k f", k=2)[:, 0, :]))
                    e.wait_ge(sV.h, sV.c)
                    sV.inc(e.tensor_mul(out=dei[:, 0:SW // 2],
                                        in0=dei[:, 0:SW // 2], in1=pin[:]))

                @block.gpsimd
                def _(e):
                    w(e, sV)
                    qG.inc(e.dma_start(
                        out=pk0_d.ap().rearrange("(s f) b -> s (f b)", s=16),
                        in_=dei[:, 0:SW // 2]), 16)

                @block.sync
                def _(e):
                    w(e, sV, qG)
                    qL.inc(e.dma_start(out=pin[:], in_=p1s), 16)

                @block.vector
                def _(e):
                    w(e, qL, qG)
                    sV.inc(e.tensor_copy(out=dei[:, 0:SW // 2],
                                         in_=keep[:].rearrange(
                                             "p (f k) -> p k f", k=2)[:, 1, :]))
                    e.wait_ge(sV.h, sV.c)
                    sV.inc(e.tensor_mul(out=dei[:, 0:SW // 2],
                                        in0=dei[:, 0:SW // 2], in1=pin[:]))

                @block.gpsimd
                def _(e):
                    w(e, sV)
                    qG.inc(e.dma_start(
                        out=pk1_d.ap().rearrange("(s f) b -> s (f b)", s=16),
                        in_=dei[:, 0:SW // 2]), 16)

        esr.close()
        esa = ExitStack()
        _e3 = esa.enter_context
        g0 = _e3(nc.sbuf_tensor("g0", [128, DM], F32))
        g1 = _e3(nc.sbuf_tensor("g1", [128, DM], F32))
        yv = _e3(nc.sbuf_tensor("yv", [128, DM], F32))
        x1t = _e3(nc.sbuf_tensor("x1t", [128, DM], F32))
        ic0 = _e3(nc.sbuf_tensor("ic0", [128, 1], I32))
        ic1 = _e3(nc.sbuf_tensor("ic1", [128, 1], I32))
        pc0 = _e3(nc.sbuf_tensor("pc0", [128, 1], F32))
        pc1 = _e3(nc.sbuf_tensor("pc1", [128, 1], F32))
        lncc = _e3(nc.sbuf_tensor("lncc", [128, 8], F32))
        sstr = _e3(nc.sbuf_tensor("sstr", [1, 1], I32))
        iocol = _e3(nc.sbuf_tensor("iocol", [128, 1], I32))
        strc = _e3(nc.sbuf_tensor("strc", [128, 1], I32))
        bcol = _e3(nc.sbuf_tensor("bcol", [128, 1], I32))
        tcol = _e3(nc.sbuf_tensor("tcol", [128, 2, 1], I32))
        if True:
            @block.sync
            def _(e):
                qL.inc(e.dma_start(out=sstr[:], in_=stripev), 16)
                qL.inc(e.dma_start(out=iocol[:], in_=iotacv), 16)
                qL.inc(e.dma_start(out=strc[:],
                                   in_=stripev.partition_broadcast(128)), 16)

            @block.vector
            def _(e):
                w(e, qL)
                sV.inc(e.tensor_scalar(out=bcol[:], in0=strc[:], scalar1=NTOK,
                                       scalar2=None, op0=ALU.mult))
                e.wait_ge(sV.h, sV.c)
                sV.inc(e.tensor_tensor(out=bcol[:], in0=bcol[:], in1=iocol[:],
                                       op=ALU.add))

            for tt in range(NTOK // 128):
                @block.sync
                def _(e, tt=tt):
                    w(e, sV, qG)
                    qL.inc(e.dma_start(out=x1t[:],
                                       in_=x1s[tt * 128:(tt + 1) * 128, :]), 16)

                @block.vector
                def _(e, tt=tt):
                    w(e, qG)
                    sV.inc(e.tensor_scalar(out=tcol[:, tt % 2, :], in0=bcol[:],
                                           scalar1=tt * 128, scalar2=None,
                                           op0=ALU.add))

                @block.gpsimd
                def _(e, tt=tt):
                    w(e, qL, sV)
                    tc = tcol[:, tt % 2, :]
                    qG.inc(e.indirect_dma_start(
                        out=ic0[:], out_offset=None, in_=idx0_d.ap(),
                        in_offset=IndirectOffsetOnAxis(ap=tc, axis=0)), 16)
                    qG.inc(e.indirect_dma_start(
                        out=ic1[:], out_offset=None, in_=idx1_d.ap(),
                        in_offset=IndirectOffsetOnAxis(ap=tc, axis=0)), 16)
                    qG.inc(e.indirect_dma_start(
                        out=pc0[:], out_offset=None, in_=pk0_d.ap(),
                        in_offset=IndirectOffsetOnAxis(ap=tc, axis=0)), 16)
                    qG.inc(e.indirect_dma_start(
                        out=pc1[:], out_offset=None, in_=pk1_d.ap(),
                        in_offset=IndirectOffsetOnAxis(ap=tc, axis=0)), 16)
                    e.wait_ge(qG.h, qG.c)
                    qG.inc(e.indirect_dma_start(
                        out=g0[:], out_offset=None, in_=eof,
                        in_offset=IndirectOffsetOnAxis(ap=ic0[:], axis=0)), 16)
                    qG.inc(e.indirect_dma_start(
                        out=g1[:], out_offset=None, in_=eof,
                        in_offset=IndirectOffsetOnAxis(ap=ic1[:], axis=0)), 16)

                @block.vector
                def _(e, tt=tt):
                    w(e, qG)
                    sV.inc(e.tensor_scalar(out=g0[:], in0=g0[:],
                                            scalar1=pc0[:, 0:1],
                                            scalar2=None, op0=ALU.mult))
                    e.wait_ge(sV.h, sV.c)
                    sV.inc(e.scalar_tensor_tensor(out=yv[:], in0=g1[:],
                                                  scalar=pc1[:, 0:1],
                                                  in1=g0[:],
                                                  op0=ALU.mult, op1=ALU.add))
                    e.wait_ge(sV.h, sV.c)
                    sV.inc(e.tensor_add(out=yv[:], in0=yv[:], in1=x1t[:]))
                    e.wait_ge(sV.h, sV.c)
                    sV.inc(e.reduce_sum(out=lncc[:, 0:1], in_=yv[:], axis=AX))
                    e.wait_ge(sV.h, sV.c)
                    sV.inc(e.tensor_scalar(out=lncc[:, 1:2], in0=lncc[:, 0:1],
                                           scalar1=1.0 / DM, scalar2=None,
                                           op0=ALU.mult))
                    e.wait_ge(sV.h, sV.c)
                    sV.inc(e.tensor_scalar(out=g0[:], in0=yv[:],
                                           scalar1=lncc[:, 1:2], scalar2=None,
                                           op0=ALU.subtract))

                @block.scalar
                def _(e, tt=tt):
                    w(e, sV)
                    sA.inc(e.activation(out=g1[:], in_=g0[:], func=AF.Square))

                @block.vector
                def _(e, tt=tt):
                    w(e, sA)
                    sV.inc(e.reduce_sum(out=lncc[:, 2:3], in_=g1[:], axis=AX))
                    e.wait_ge(sV.h, sV.c)
                    sV.inc(e.tensor_scalar(out=lncc[:, 3:4], in0=lncc[:, 2:3],
                                           scalar1=1.0 / DM, scalar2=None,
                                           op0=ALU.mult))

                @block.scalar
                def _(e, tt=tt):
                    w(e, sV)
                    sA.inc(e.activation(out=lncc[:, 4:5], in_=lncc[:, 3:4],
                                        func=AF.Ln, bias=lncst3[:, 0:1]))
                    e.wait_ge(sA.h, sA.c)
                    sA.inc(e.activation(out=lncc[:, 5:6], in_=lncc[:, 4:5],
                                        func=AF.Exp, scale=lncst3[:, 1:2]))

                @block.vector
                def _(e, tt=tt):
                    w(e, sA)
                    sV.inc(e.tensor_scalar(out=yv[:], in0=g0[:],
                                           scalar1=lncc[:, 5:6], scalar2=None,
                                           op0=ALU.mult))

                @block.gpsimd
                def _(e, tt=tt):
                    w(e, sV)
                    qG.inc(e.dma_start(out=out[tt * 128:(tt + 1) * 128, :],
                                       in_=yv[:]), 16)

            @block.gpsimd
            def _(e):
                e.wait_ge(qG.h, qG.c)

        esa.close()
    es.close()
    return nc


_cache = {}


def _get(key, builder):
    if key not in _cache:
        _cache[key] = builder()
    return _cache[key]


def _rs(a, nch=NCH, inner=None):
    a = np.asarray(a)
    k = a.shape[1] if a.ndim == 2 else 1
    a = a.reshape(nch, 128, -1).transpose(1, 0, 2).reshape(128, -1)
    return np.ascontiguousarray(a)


def kernel(L0_attn_x, L0_attn_W_in, L0_attn_conv_w, L0_attn_conv_b,
           L0_attn_W_x, L0_attn_W_dt, L0_attn_b_dt, L0_attn_A_log,
           L0_attn_D_param, L0_attn_W_out, L0_ffn_rw1, L0_ffn_rb1,
           L0_ffn_rw2, L0_ffn_rb2, L0_ffn_ew1, L0_ffn_eb1, L0_ffn_ew2,
           L0_ffn_eb2):
    x = np.asarray(L0_attn_x, np.float32)
    B, T, _ = x.shape
    TH = T // 2
    N = B * T
    NK = 2 * N

    ident = np.eye(128, dtype=np.float32)
    iota8 = np.arange(NE, dtype=np.float32)[None, :]
    nonce = np.array([[np.random.randint(1, 1 << 30)]], np.int32)

    nc1 = _get(("L1", T), lambda: build_L1(T=T))
    in1 = []
    for c in range(NCORES):
        b, dh = c >> 1, c & 1
        sl = slice(dh * DH, (dh + 1) * DH)
        m = dict(
            xT=np.ascontiguousarray(x[b].T),
            x_tm=np.ascontiguousarray(x[b, dh * TH:(dh + 1) * TH]),
            winT=np.ascontiguousarray(
                np.concatenate([L0_attn_W_in[sl], L0_attn_W_in[2048:][sl]],
                               0).T.astype(np.float32)),
            cwv=_rs(np.asarray(L0_attn_conv_w)[sl, 0, :]),
            cbv=_rs(np.asarray(L0_attn_conv_b)[sl, None]),
            wxT=np.ascontiguousarray(np.asarray(L0_attn_W_x)[:, sl].T),
            wdtv=_rs(np.asarray(L0_attn_W_dt)[sl]),
            bdtv=_rs(np.asarray(L0_attn_b_dt)[sl, None]),
            alogv=_rs(np.asarray(L0_attn_A_log)[sl]),
            dpv=_rs(np.asarray(L0_attn_D_param)[sl, None]),
            woutT=np.ascontiguousarray(np.asarray(L0_attn_W_out)[:, sl].T),
            rw1=np.asarray(L0_ffn_rw1, np.float32),
            rb1v=np.ascontiguousarray(
                np.asarray(L0_ffn_rb1).reshape(32, 128).T),
            rw2=np.asarray(L0_ffn_rw2, np.float32),
            rb2v=np.asarray(L0_ffn_rb2, np.float32).reshape(NE, 1),
            identv=ident, iotav=iota8, noncev=nonce,
            slotv=np.array([[dh]], np.int32),
            lncv=np.tile(np.array([[1e-5, -0.5]], np.float32), (128, 1)),
        )
        in1.append({k: np.ascontiguousarray(v) for k, v in m.items()})
    r1 = run_bass_kernel_spmd(nc1, in1, core_ids=list(range(NCORES)))
    x1f = np.concatenate([r1.results[c]["x1out"] for c in range(NCORES)], 0)
    iv = np.concatenate([r1.results[c]["ivout"] for c in range(NCORES)], 0)

    idsf = np.ascontiguousarray(
        np.stack([iv[:, 0], iv[:, 1]], 1).reshape(1, NK).astype(np.float32))
    tokv = np.ascontiguousarray(
        (np.arange(NK, dtype=np.int64) // 2 + 1).astype(np.int32)[None, :])
    emodv = (np.arange(128, dtype=np.float32) % 8)[:, None]
    spatv = np.zeros((128, 128), np.float32)
    for rp in range(128):
        for r in range(128):
            if rp % 8 == r % 8 and rp // 8 < r // 8:
                spatv[rp, r] = 1.0

    nc2 = _get(("L2", N), lambda: build_L2(N=N))
    in2 = []
    for e_ in range(NCORES):
        selv = np.zeros((128, 16), np.float32)
        for q in range(16):
            selv[q * 8 + e_, q] = 1.0
        m = dict(
            x1f=x1f, idsf=idsf, tokv=tokv,
            ew1in=np.asarray(L0_ffn_ew1)[e_].astype(np.float32),
            ew2in=np.asarray(L0_ffn_ew2)[e_].astype(np.float32),
            eb1v=np.ascontiguousarray(
                np.asarray(L0_ffn_eb1)[e_].reshape(32, 128).T),
            eb2v=np.asarray(L0_ffn_eb2)[e_].astype(np.float32)[None, :],
            emodv=emodv, spatv=spatv, selv=selv, identv=ident,
        )
        in2.append({k: np.ascontiguousarray(v) for k, v in m.items()})
    r2 = run_bass_kernel_spmd(nc2, in2, core_ids=list(range(NCORES)))
    eof = np.concatenate([r2.results[e_]["eoout"] for e_ in range(NCORES)], 0)

    sumpv = np.zeros((128, 16), np.float32)
    selpv = np.zeros((128, 16), np.float32)
    for q in range(16):
        for e_ in range(8):
            sumpv[q * 8 + e_, q] = 1.0
        selpv[q * 8, q] = 1.0
    p0s = np.ascontiguousarray(iv[:, 2].reshape(16, N // 16))
    p1s = np.ascontiguousarray(iv[:, 3].reshape(16, N // 16))

    nc3 = _get(("L3", N), lambda: build_L3(N=N, NTOK=N // NCORES))
    NTOK = N // NCORES
    in3 = []
    for c in range(NCORES):
        m = dict(
            eof=eof, x1s=x1f[c * NTOK:(c + 1) * NTOK], idsf=idsf,
            p0s=p0s, p1s=p1s, emodv=emodv, spatv=spatv,
            sumpv=sumpv, selpv=selpv,
            stripev=np.array([[c]], np.int32),
            iotacv=np.arange(128, dtype=np.int32)[:, None],
            lncv3=np.tile(np.array([[1e-5, -0.5]], np.float32), (128, 1)),
        )
        in3.append({k: np.ascontiguousarray(v) for k, v in m.items()})
    r3 = run_bass_kernel_spmd(nc3, in3, core_ids=list(range(NCORES)))
    y = np.concatenate([r3.results[c]["out"] for c in range(NCORES)], 0)
    return y.reshape(B, T, DM)


# version 09 (final)
# speedup vs baseline: 1.1616x; 1.0522x over previous
import numpy as np
from contextlib import ExitStack
import concourse.bass as bass
import concourse.mybir as mybir
from concourse.bass import IndirectOffsetOnAxis
from concourse.bass_utils import run_bass_kernel_spmd

F32 = mybir.dt.float32
BF16 = mybir.dt.bfloat16
I32 = mybir.dt.int32
AF = mybir.ActivationFunctionType
ALU = mybir.AluOpType
AX = mybir.AxisListType.X

DM = 1024
DH = 1024
NCH = 8
DS = 16
HF = 4096
NE = 8
CAP = 4096
NCORES = 8
BIGI = 1 << 20



_regc = [0]


def rn(p):
    _regc[0] += 1
    return f"{p}{_regc[0]}"

class Sem:
    def __init__(self, h):
        self.h = h
        self.c = 0

    def inc(self, instr, n=1):
        instr.then_inc(self.h, n)
        self.c += n
        return self.c


def w(eng, *sems):
    for s in sems:
        if s.c:
            eng.wait_ge(s.h, s.c)


def slot_branch(e, sslot, arm0, arm1, regname="rs"):
    with e.register(rn(regname)) as rs_:
        e.reg_load(rs_, sslot[0:1, 0:1])
        with e.If_eq(rs_, 0):
            arm0()
        with e.Else():
            arm1()


def poll_partner(e, rpoll, rtmp, flag_ap_pt, sflag, snonce_sb, pp_h):
    with e.register(rn("rf")) as rf, e.register(rn("rc")) as rc, \
         e.register(rn("rnn")) as rnn:
        e.reg_load(rnn, snonce_sb[0:1, 0:1])
        e.reg_mov(rc, 1)
        with e.While(rc):
            e.reg_add(rpoll, rpoll, 16)
            e.dma_start(out=sflag[:], in_=flag_ap_pt).then_inc(pp_h, 16)
            e.wait_ge(pp_h, e.snap(rpoll))
            e.reg_load(rf, sflag[0:1, 0:1])
            e.reg_alu(rc, rf, e.snap(rnn), ALU.not_equal)


def build_L1(T=8192, MT=512, ST=1024, WST=None, dbg=False, stop='full'):
    TH = T // 2
    NMT = T // MT
    NST = T // ST
    WST = WST or ST
    NWS = T // WST
    NRG = TH // 512
    nc = bass.Bass("TRN2", target_bir_lowering=False, debug=False,
                   detect_race_conditions=False)

    ein = lambda n, s, d=F32: nc.dram_tensor(n, s, d, kind="ExternalInput").ap()
    xT = ein("xT", [DM, T])
    x_tm = ein("x_tm", [TH, DM])
    winT = ein("winT", [DM, 2 * DH])
    cwv = ein("cwv", [128, NCH * 4])
    cbv = ein("cbv", [128, NCH])
    wxT = ein("wxT", [DH, 33])
    wdtv = ein("wdtv", [128, NCH])
    bdtv = ein("bdtv", [128, NCH])
    alogv = ein("alogv", [128, NCH * DS])
    dpv = ein("dpv", [128, NCH])
    woutT = ein("woutT", [DH, DM])
    rw1 = ein("rw1", [DM, HF])
    rb1v = ein("rb1v", [128, HF // 128])
    rw2 = ein("rw2", [HF, NE])
    rb2v = ein("rb2v", [NE, 1])
    identv = ein("identv", [128, 128])
    iotav = ein("iotav", [1, NE])
    noncev = ein("noncev", [1, 1], I32)
    slotv = ein("slotv", [1, 1], I32)
    lncv = ein("lncv", [128, 2])

    x1out = nc.dram_tensor("x1out", [TH, DM], F32, kind="ExternalOutput").ap()
    ivout = nc.dram_tensor("ivout", [TH, 4], F32, kind="ExternalOutput").ap()

    ssm_sh = nc.dram_tensor("ssm_sh", [2, 33, T], F32, addr_space="Shared")
    mix_sh = nc.dram_tensor("mix_sh", [2 * T, DM], F32, addr_space="Shared")
    flag_sh = nc.dram_tensor("flag_sh", [2, 2, 1], I32, addr_space="Shared")

    if dbg:
        dbg_u = nc.dram_tensor("dbg_u", [NCH, 128, T], F32, kind="ExternalOutput").ap()
        dbg_z = nc.dram_tensor("dbg_z", [NCH, 128, T], F32, kind="ExternalOutput").ap()
        dbg_y2 = nc.dram_tensor("dbg_y2", [NCH, 128, T], F32, kind="ExternalOutput").ap()
        dbg_ssm = nc.dram_tensor("dbg_ssm", [33, T], F32, kind="ExternalOutput").ap()
        dbg_mix = nc.dram_tensor("dbg_mix", [2 * T, DM], F32, kind="ExternalOutput").ap()
        dbg_ln = nc.dram_tensor("dbg_ln", [128, 8 + 2 * DM], F32, kind="ExternalOutput").ap()
    zsil_d = nc.dram_tensor("zsil_d", [NCH, 128, T], F32)
    u_d = nc.dram_tensor("u_d", [NCH, 128, T], F32)
    y2_d = nc.dram_tensor("y2_d", [NCH, 128, T], F32)
    ssmb_d = nc.dram_tensor("ssmb_d", [33, T], BF16)
    ssm0_d = nc.dram_tensor("ssm0_d", [1, T], F32)

    es = ExitStack()
    sb = lambda n, s, d=F32: es.enter_context(nc.sbuf_tensor(n, s, d))
    ps = lambda n, s: es.enter_context(nc.psum_tensor(n, s, F32))
    cw = sb("cw", [128, NCH, 4]); cb = sb("cb", [128, NCH])
    wdtc = sb("wdtc", [128, NCH]); bdtc = sb("bdtc", [128, NCH])
    ac = sb("ac", [128, NCH * DS]); dpc = sb("dpc", [128, NCH])
    id16 = sb("id16", [128, 128], BF16); idf = sb("idf", [128, 128])
    io8b = sb("io8b", [128, NE]); snonce = sb("snonce", [1, 1], I32)
    sflag = sb("sflag", [1, 1], I32); sslot = sb("sslot", [1, 1], I32)
    hcarry = sb("hcarry", [128, NCH * DS])
    lncst = sb("lncst", [128, 2])
    _qL = es.enter_context(nc.semaphore("qL"))
    _qG = es.enter_context(nc.semaphore("qG"))
    _sV = es.enter_context(nc.semaphore("sV"))
    _sA = es.enter_context(nc.semaphore("sA"))
    _sT = es.enter_context(nc.semaphore("sT"))
    _pp = es.enter_context(nc.semaphore("pp"))
    _pm = es.enter_context(nc.semaphore("pm"))
    block = es.enter_context(nc.Block())
    if True:
        qL, qG, sV, sA, sT = Sem(_qL), Sem(_qG), Sem(_sV), Sem(_sA), Sem(_sT)
        pm = Sem(_pm)

        rpoll_box = {}

        @block.gpsimd
        def _(e):
            rpoll_box["r"] = e.alloc_register(rn("rpoll"))
            e.reg_mov(rpoll_box["r"], 0)

        @block.sync
        def _(e):
            for t_, src in ((cw, cwv), (cb, cbv), (wdtc, wdtv), (bdtc, bdtv),
                            (ac, alogv), (dpc, dpv), (idf, identv),
                            (snonce, noncev), (sslot, slotv), (lncst, lncv)):
                qL.inc(e.dma_start(out=t_[:], in_=src), 16)
            qL.inc(e.dma_start(out=io8b[:], in_=iotav.partition_broadcast(128)), 16)

        @block.vector
        def _(e):
            w(e, qL)
            sV.inc(e.tensor_copy(out=id16[:], in_=idf[:]))
            sV.inc(e.memset(hcarry[:], 0.0))

        @block.scalar
        def _(e):
            w(e, qL)
            sA.inc(e.activation(out=ac[:], in_=ac[:], func=AF.Exp))

        @block.vector
        def _(e):
            w(e, sA)
            sV.inc(e.tensor_scalar(out=ac[:], in0=ac[:], scalar1=-1.0,
                                   scalar2=None, op0=ALU.mult))

        esM = ExitStack()
        w16 = esM.enter_context(nc.sbuf_tensor("w16", [128, 8, 2 * DH], BF16))
        wx16 = esM.enter_context(nc.sbuf_tensor("wx16", [128, NCH, 33], BF16))
        ssma = esM.enter_context(nc.sbuf_tensor("ssma", [33, T], F32))
        if True:
            with nc.sbuf_tensor("wtmp", [128, 8, 2 * DH], F32) as wtmp:
                @block.sync
                def _(e):
                    w(e, sV)
                    qL.inc(e.dma_start(
                        out=wtmp[:],
                        in_=winT.rearrange("(k p) n -> p k n", p=128)), 16)

                @block.vector
                def _(e):
                    w(e, qL)
                    sV.inc(e.tensor_copy(out=w16[:], in_=wtmp[:]))

                @block.sync
                def _(e):
                    w(e, sV)
                    qL.inc(e.dma_start(
                        out=wtmp[:, 0, 0:NCH * 33].rearrange(
                            "p (k n) -> p k n", k=NCH),
                        in_=wxT.rearrange("(k p) n -> p k n", p=128)), 16)

                @block.vector
                def _(e):
                    w(e, qL)
                    sV.inc(e.tensor_copy(
                        out=wx16[:],
                        in_=wtmp[:, 0, 0:NCH * 33].rearrange(
                            "p (k n) -> p k n", k=NCH)))

            esMt = ExitStack()
            xt32 = esMt.enter_context(nc.sbuf_tensor("xt32", [128, 8, MT], F32))
            xt16 = esMt.enter_context(nc.sbuf_tensor("xt16", [128, 8, MT], BF16))
            xdh = esMt.enter_context(nc.sbuf_tensor("xdh", [128, NCH, MT + 3], F32))
            cacc = esMt.enter_context(nc.sbuf_tensor("cacc", [128, 2, MT], F32))
            ut = esMt.enter_context(nc.sbuf_tensor("ut", [128, NCH, MT], F32))
            ut16 = esMt.enter_context(nc.sbuf_tensor("ut16", [128, NCH, MT], BF16))
            zst = esMt.enter_context(nc.sbuf_tensor("zst", [128, NCH, MT], F32))
            psa = esMt.enter_context(nc.psum_tensor("psa", [128, 2, 512], F32))
            psm = esMt.enter_context(nc.psum_tensor("psm", [33, 512], F32))
            if True:
                NSUB = MT // 512

                @block.vector
                def _(e):
                    sV.inc(e.memset(xdh[:], 0.0))

                for i in range(NMT):
                    @block.sync
                    def _(e, i=i):
                        w(e, sV, sT)
                        qL.inc(e.dma_start(
                            out=xt32[:],
                            in_=xT[:, i * MT:(i + 1) * MT]
                                .rearrange("(k p) t -> p k t", p=128)), 16)

                    @block.vector
                    def _(e, i=i):
                        w(e, qL)
                        sV.inc(e.tensor_copy(out=xt16[:], in_=xt32[:]))

                    mk = {}

                    @block.tensor
                    def _(e, i=i, mk=mk):
                        w(e, sV, sA)
                        saw = sA.c
                        planned = {}
                        for j in range(16):
                            if j < 8:
                                planned[j] = saw + 2 * j + 2
                            else:
                                planned[j] = saw + 16 + (j - 8) + 1
                        for o in range(16):
                            if o >= 2:
                                e.wait_ge(sA.h, planned[o - 2])
                            for k in range(8):
                                ins = e.matmul(psa[:, o % 2, 0:MT],
                                               w16[:, k, o * 128:(o + 1) * 128],
                                               xt16[:, k, :],
                                               start=(k == 0), stop=(k == 7))
                            sT.inc(ins)
                            mk[("xz", o)] = sT.c

                    @block.scalar
                    def _(e, i=i, mk=mk):
                        w(e, qG)
                        for c in range(NCH):
                            e.wait_ge(sT.h, mk[("xz", c)])
                            sA.inc(e.copy(out=xdh[:, c, 0:3],
                                          in_=xdh[:, c, MT:MT + 3]))
                            e.wait_ge(sA.h, sA.c)
                            sA.inc(e.copy(out=xdh[:, c, 3:MT + 3],
                                          in_=psa[:, c % 2, 0:MT]))
                            mk[("evac", c)] = sA.c
                        for o in range(8, 16):
                            e.wait_ge(sT.h, mk[("xz", o)])
                            sA.inc(e.activation(out=zst[:, o - 8, :],
                                                in_=psa[:, o % 2, 0:MT],
                                                func=AF.Silu))
                            mk[("zev", o)] = sA.c

                    @block.vector
                    def _(e, i=i, mk=mk):
                        saw = sA.c
                        for c in range(NCH):
                            e.wait_ge(sA.h, mk[("evac", c)])
                            if c >= 1:
                                e.wait_ge(sA.h, saw + c)
                            sV.inc(e.tensor_scalar(out=cacc[:, 0, :],
                                            in0=xdh[:, c, 0:MT],
                                            scalar1=cw[:, c, 0:1], scalar2=None,
                                            op0=ALU.mult))
                            for j in (1, 2, 3):
                                sV.inc(e.scalar_tensor_tensor(
                                    out=cacc[:, j % 2, :],
                                    in0=xdh[:, c, j:j + MT],
                                    scalar=cw[:, c, j:j + 1],
                                    in1=cacc[:, (j + 1) % 2, :],
                                    op0=ALU.mult, op1=ALU.add))
                            mk[("conv", c)] = sV.c

                    @block.scalar
                    def _(e, i=i, mk=mk):
                        for c in range(NCH):
                            e.wait_ge(sV.h, mk[("conv", c)])
                            sA.inc(e.activation(out=ut[:, c, :],
                                                in_=cacc[:, 1, :], func=AF.Silu,
                                                bias=cb[:, c:c + 1], scale=1.0))
                            mk[("usil", c)] = sA.c

                    @block.vector
                    def _(e, i=i):
                        w(e, sA, sT)
                        sV.inc(e.tensor_copy(out=ut16[:], in_=ut[:]))

                    @block.tensor
                    def _(e, i=i):
                        w(e, sV, sA)
                        for sub in range(MT // 512):
                            for c in range(NCH):
                                ins = e.matmul(
                                    psm[:, :], wx16[:, c, :],
                                    ut16[:, c, sub * 512:(sub + 1) * 512],
                                    start=(c == 0), stop=(c == NCH - 1))
                            sT.inc(ins)

                    @block.scalar
                    def _(e, i=i):
                        base_t = sT.c - MT // 512
                        for sub in range(MT // 512):
                            e.wait_ge(sT.h, base_t + sub + 1)
                            sA.inc(e.copy(
                                out=ssma[:, i * MT + sub * 512:
                                         i * MT + (sub + 1) * 512],
                                in_=psm[:, :]))

                    @block.gpsimd
                    def _(e, i=i):
                        w(e, sA, sV)
                        for c in range(NCH):
                            qG.inc(e.dma_start(
                                out=zsil_d.ap()[c, :, i * MT:(i + 1) * MT],
                                in_=zst[:, c, :]), 16)
                            qG.inc(e.dma_start(
                                out=u_d.ap()[c, :, i * MT:(i + 1) * MT],
                                in_=ut[:, c, :]), 16)

            esMt.close()
            with (
                nc.sbuf_tensor("stmp", [33, T], F32) as stmp,
                nc.sbuf_tensor("ssmb16", [33, T], BF16) as ssmb16,
            ):
                @block.gpsimd
                def _(e):
                    w(e, sA, qG)
                    rpoll = rpoll_box["r"]

                    def arm(me):
                        pt = 1 - me
                        e.dma_start(out=ssm_sh.ap()[me, :, :],
                                    in_=ssma[:]).then_inc(_qG, 16)
                        e.wait_ge(_qG, qG.c + 16)
                        e.dma_start(out=flag_sh.ap()[me, 0, :],
                                    in_=snonce[:]).then_inc(_qG, 16)
                        e.wait_ge(_qG, qG.c + 32)
                        poll_partner(e, rpoll, None,
                                     flag_sh.ap()[pt, 0, :],
                                     sflag, snonce, _pp)
                        e.reg_add(rpoll, rpoll, 16)
                        e.dma_start(out=stmp[:],
                                    in_=ssm_sh.ap()[pt, :, :]).then_inc(_pp, 16)
                        e.wait_ge(_pp, e.snap(rpoll))

                    slot_branch(e, sslot, lambda: arm(0), lambda: arm(1))
                    qG.c += 32
                    pm.inc(e.nop())

                @block.vector
                def _(e):
                    e.wait_ge(pm.h, pm.c)
                    sV.inc(e.tensor_add(out=ssma[:], in0=ssma[:], in1=stmp[:]))
                    e.wait_ge(sV.h, sV.c)
                    sV.inc(e.tensor_copy(out=ssmb16[:], in_=ssma[:]))

                @block.scalar
                def _(e):
                    w(e, sV)
                    sA.inc(e.dma_start(out=ssmb_d.ap(), in_=ssmb16[:]), 16)
                    sA.inc(e.dma_start(out=ssm0_d.ap(), in_=ssma[0:1, :]), 16)
                    if dbg:
                        sA.inc(e.dma_start(out=dbg_ssm, in_=ssma[:]), 16)

        esM.close()
        if stop == "x":
            @block.gpsimd
            def _(e):
                w(e, sA, sV)
                e.wait_ge(qG.h, qG.c)
            es.close()
            return nc
        esS = ExitStack()
        _e = esS.enter_context
        bcB = _e(nc.sbuf_tensor("bcB", [128, DS, ST], BF16))
        bcC = _e(nc.sbuf_tensor("bcC", [128, DS, ST], BF16))
        bcdt = _e(nc.sbuf_tensor("bcdt", [128, ST], F32))
        ustr = _e(nc.sbuf_tensor("ustr", [128, NCH, ST], F32))
        delta = _e(nc.sbuf_tensor("delta", [128, ST], F32))
        du = _e(nc.sbuf_tensor("du", [128, ST], F32))
        spr = _e(nc.sbuf_tensor("spr", [128, ST], F32))
        spl = _e(nc.sbuf_tensor("spl", [128, ST], F32))
        spl2 = _e(nc.sbuf_tensor("spl2", [128, ST], F32))
        dAb = _e(nc.sbuf_tensor("dAb", [128, 2, ST], F32))
        dBub = _e(nc.sbuf_tensor("dBub", [128, 2, ST], F32))
        hb = _e(nc.sbuf_tensor("hb", [128, 2, ST], F32))
        pr16 = _e(nc.sbuf_tensor("pr16", [128, 2, ST], BF16))
        y2s = _e(nc.sbuf_tensor("y2s", [128, 2, ST], F32))
        psy = _e(nc.psum_tensor("psy", [128, ST // 512, 512], F32))
        if True:
            import os as _osC
            CL = int(_osC.environ.get("CLIM", "0")) or NCH
            NYB = ST // 512
            sv_scan, sv_prod, sv_du, sv_y2 = {}, {}, {}, {}
            sa_dA, sa_carry, sa_mark = {}, {}, {}
            st_ymm, qg_y2, ql_strip = {}, {}, {}

            pv, pa, pt_, pg, pl = sV.c, sA.c, sT.c, qG.c, qL.c
            for si in range(NST):
                pl += 16 * (1 + 2 * DS + NCH)
                ql_strip[si] = pl
                for c in range(CL):
                    pa += 1
                    sa_mark[(si, c)] = pa
                    pv += 2
                    sv_du[(si, c)] = pv
                    for s in range(DS):
                        pa += 1
                        sa_dA[(si, c, s)] = pa
                        pv += 1
                        pv += 1
                        sv_scan[(si, c, s)] = pv
                        pv += 1
                        sv_prod[(si, c, s)] = pv
                        pa += 1
                        sa_carry[(si, c, s)] = pa
                        pt_ += NYB
                        st_ymm[(si, c, s)] = pt_
                    pv += 1
                    sv_y2[(si, c)] = pv
                    pg += 16
                    qg_y2[(si, c)] = pg

            for si in range(NST):
                t0 = si * ST

                @block.sync
                def _(e, si=si, t0=t0):
                    if si > 0:
                        e.wait_ge(sV.h, sv_y2[(si - 1, CL - 1)])
                        e.wait_ge(sA.h, sa_carry[(si - 1, CL - 1, DS - 1)])
                    else:
                        w(e, sA, sV)
                    qL.inc(e.dma_start(
                        out=bcdt[:],
                        in_=ssm0_d.ap()[:, t0:t0 + ST]
                            .partition_broadcast(128)), 16)
                    for s in range(DS):
                        qL.inc(e.dma_start(
                            out=bcB[:, s, :],
                            in_=ssmb_d.ap()[1 + s:2 + s, t0:t0 + ST]
                                .partition_broadcast(128)), 16)
                        qL.inc(e.dma_start(
                            out=bcC[:, s, :],
                            in_=ssmb_d.ap()[17 + s:18 + s, t0:t0 + ST]
                                .partition_broadcast(128)), 16)
                    for c in range(NCH):
                        qL.inc(e.dma_start(
                            out=ustr[:, c, :],
                            in_=u_d.ap()[c, :, t0:t0 + ST]), 16)
                    assert qL.c == ql_strip[si]

                @block.scalar
                def _(e, si=si):
                    e.wait_ge(qL.h, ql_strip[si])
                    for c in range(CL):
                        if c >= 1:
                            e.wait_ge(sV.h, sv_du[(si, c - 1)])
                        elif si >= 1:
                            e.wait_ge(sV.h, sv_du[(si - 1, CL - 1)])
                        e.activation(out=spr[:], in_=bcdt[:], func=AF.Relu,
                                     scale=wdtc[:, c:c + 1],
                                     bias=bdtc[:, c:c + 1])
                        e.activation(out=spl[:], in_=bcdt[:], func=AF.Abs,
                                     scale=wdtc[:, c:c + 1],
                                     bias=bdtc[:, c:c + 1])
                        e.activation(out=spl2[:], in_=spl[:], func=AF.Exp,
                                     scale=-1.0)
                        sA.inc(e.activation(out=spl[:], in_=spl2[:], func=AF.Ln,
                                            bias=1.0))
                        assert sA.c == sa_mark[(si, c)]
                        for s in range(DS):
                            if s >= 2:
                                e.wait_ge(sV.h, sv_scan[(si, c, s - 2)])
                            elif c >= 1:
                                e.wait_ge(sV.h, sv_scan[(si, c - 1, DS - 2 + s)])
                            elif si >= 1:
                                e.wait_ge(sV.h,
                                          sv_scan[(si - 1, CL - 1, DS - 2 + s)])
                            e.wait_ge(sV.h, sv_du[(si, c)])
                            sA.inc(e.activation(
                                out=dAb[:, s % 2, :], in_=delta[:], func=AF.Exp,
                                scale=ac[:, c * DS + s:c * DS + s + 1]))
                            assert sA.c == sa_dA[(si, c, s)]
                            e.wait_ge(sV.h, sv_scan[(si, c, s)])
                            sA.inc(e.copy(
                                out=hcarry[:, c * DS + s:c * DS + s + 1],
                                in_=hb[:, s % 2, ST - 1:ST]))
                            assert sA.c == sa_carry[(si, c, s)]

                @block.vector
                def _(e, si=si):
                    for c in range(CL):
                        e.wait_ge(sA.h, sa_mark[(si, c)])
                        sV.inc(e.tensor_add(out=delta[:], in0=spr[:],
                                            in1=spl[:]))
                        sV.inc(e.tensor_mul(out=du[:], in0=delta[:],
                                            in1=ustr[:, c, :]))
                        assert sV.c == sv_du[(si, c)]
                        for s in range(DS):
                            col = c * DS + s
                            sV.inc(e.tensor_tensor(out=dBub[:, s % 2, :],
                                                   in0=du[:],
                                                   in1=bcB[:, s, :],
                                                   op=ALU.mult))
                            e.wait_ge(sA.h, sa_dA[(si, c, s)])
                            if si == 0:
                                init = 0.0
                            else:
                                e.wait_ge(sA.h, sa_carry[(si - 1, c, s)])
                                init = hcarry[:, col:col + 1]
                            if s >= 2:
                                e.wait_ge(sT.h, st_ymm[(si, c, s - 2)])
                            elif c >= 1:
                                e.wait_ge(sT.h, st_ymm[(si, c - 1, DS - 2 + s)])
                            elif si >= 1:
                                e.wait_ge(sT.h,
                                          st_ymm[(si - 1, CL - 1, DS - 2 + s)])
                            sV.inc(e.tensor_tensor_scan(
                                out=hb[:, s % 2, :], data0=dAb[:, s % 2, :],
                                data1=dBub[:, s % 2, :], initial=init,
                                op0=ALU.mult, op1=ALU.add))
                            assert sV.c == sv_scan[(si, c, s)]
                            sV.inc(e.tensor_tensor(
                                out=pr16[:, s % 2, :], in0=hb[:, s % 2, :],
                                in1=bcC[:, s, :], op=ALU.mult))
                            assert sV.c == sv_prod[(si, c, s)]
                        e.wait_ge(sT.h, st_ymm[(si, c, DS - 1)])
                        if c >= 2:
                            e.wait_ge(qG.h, qg_y2[(si, c - 2)])
                        elif si >= 1 and CL - 2 + c >= 0:
                            e.wait_ge(qG.h, qg_y2[(si - 1, CL - 2 + c)])
                        sV.inc(e.scalar_tensor_tensor(
                            out=y2s[:, c % 2, :], in0=ustr[:, c, :],
                            scalar=dpc[:, c:c + 1],
                            in1=psy[:].rearrange("p a b -> p (a b)"),
                            op0=ALU.mult, op1=ALU.add))
                        assert sV.c == sv_y2[(si, c)]

                @block.tensor
                def _(e, si=si):
                    for c in range(CL):
                        if c >= 1:
                            e.wait_ge(sV.h, sv_y2[(si, c - 1)])
                        elif si >= 1:
                            e.wait_ge(sV.h, sv_y2[(si - 1, CL - 1)])
                        for s in range(DS):
                            e.wait_ge(sV.h, sv_prod[(si, c, s)])
                            for yb in range(NYB):
                                ins = e.matmul(
                                    psy[:, yb, :], id16[:],
                                    pr16[:, s % 2, yb * 512:(yb + 1) * 512],
                                    start=(s == 0), stop=(s == DS - 1))
                            sT.inc(ins, NYB)
                            assert sT.c == st_ymm[(si, c, s)]

                @block.gpsimd
                def _(e, si=si, t0=t0):
                    for c in range(CL):
                        e.wait_ge(sV.h, sv_y2[(si, c)])
                        qG.inc(e.dma_start(
                            out=y2_d.ap()[c, :, t0:t0 + ST],
                            in_=y2s[:, c % 2, :]), 16)
                        assert qG.c == qg_y2[(si, c)]

        if stop == "s":
            @block.gpsimd
            def _(e):
                w(e, sA, sV, sT)
                e.wait_ge(qG.h, qG.c)
            esS.close()
            es.close()
            return nc
        esS.close()
        esW = ExitStack()
        _e = esW.enter_context
        wo16 = _e(nc.sbuf_tensor("wo16", [128, NCH, DM], BF16))
        ywt = _e(nc.sbuf_tensor("ywt", [128, NCH, WST], F32))
        zwt = _e(nc.sbuf_tensor("zwt", [128, NCH, WST], F32))
        yg16 = _e(nc.sbuf_tensor("yg16", [128, NCH, WST], BF16))
        mixt = _e(nc.sbuf_tensor("mixt", [128, 2, 512], F32))
        psw = _e(nc.psum_tensor("psw", [128, 2, 512], F32))
        if True:
            with nc.sbuf_tensor("wotmp", [128, NCH, DM], F32) as wotmp:
                @block.sync
                def _(e):
                    w(e, sV, sA, sT)
                    qL.inc(e.dma_start(
                        out=wotmp[:],
                        in_=woutT.rearrange("(k p) n -> p k n", p=128)), 16)

                @block.vector
                def _(e):
                    w(e, qL)
                    sV.inc(e.tensor_copy(out=wo16[:], in_=wotmp[:]))

            for si in range(NWS):
                t0 = si * WST
                NW = 2 * (WST // 128)

                @block.sync
                def _(e, si=si, t0=t0):
                    w(e, sV, sT)
                    for c in range(NCH):
                        qL.inc(e.dma_start(out=ywt[:, c, :],
                                           in_=y2_d.ap()[c, :, t0:t0 + WST]),
                               16)
                        qL.inc(e.dma_start(out=zwt[:, c, :],
                                           in_=zsil_d.ap()[c, :, t0:t0 + WST]),
                               16)

                @block.vector
                def _(e, si=si):
                    w(e, qL)
                    for c in range(NCH):
                        ins = e.tensor_mul(out=yg16[:, c, :], in0=ywt[:, c, :],
                                           in1=zwt[:, c, :])
                    sV.inc(ins)

                @block.tensor
                def _(e, si=si):
                    w(e, sV)
                    saw = sA.c
                    for j in range(NW):
                        tsub, mh = j // 2, j % 2
                        if j >= 2:
                            e.wait_ge(sA.h, saw + j - 1)
                        for c in range(NCH):
                            ins = e.matmul(
                                psw[:, j % 2, :],
                                yg16[:, c, tsub * 128:(tsub + 1) * 128],
                                wo16[:, c, mh * 512:(mh + 1) * 512],
                                start=(c == 0), stop=(c == NCH - 1))
                        sT.inc(ins)

                @block.scalar
                def _(e, si=si):
                    base_t = sT.c - NW
                    qgw = qG.c
                    for j in range(NW):
                        e.wait_ge(sT.h, base_t + j + 1)
                        if j >= 2:
                            e.wait_ge(qG.h, qgw + 16 * (j - 1))
                        sA.inc(e.copy(out=mixt[:, j % 2, :],
                                      in_=psw[:, j % 2, :]))

                @block.gpsimd
                def _(e, si=si, t0=t0):
                    base_a = sA.c - NW

                    def arm(me):
                        for j in range(NW):
                            tsub, mh = j // 2, j % 2
                            e.wait_ge(sA.h, base_a + j + 1)
                            row = me * T + t0 + tsub * 128
                            e.dma_start(
                                out=mix_sh.ap()[row:row + 128,
                                                mh * 512:(mh + 1) * 512],
                                in_=mixt[:, j % 2, :]).then_inc(_qG, 16)

                    slot_branch(e, sslot, lambda: arm(0), lambda: arm(1))
                    qG.c += 16 * NW

            @block.gpsimd
            def _(e):
                w(e, qG)
                rpoll = rpoll_box["r"]

                def arm(me):
                    pt = 1 - me
                    e.dma_start(out=flag_sh.ap()[me, 1, :],
                                in_=snonce[:]).then_inc(_qG, 16)
                    e.wait_ge(_qG, qG.c + 16)
                    poll_partner(e, rpoll, None,
                                 flag_sh.ap()[pt, 1, :],
                                 sflag, snonce, _pp)

                slot_branch(e, sslot, lambda: arm(0), lambda: arm(1))
                qG.c += 16
                pm.inc(e.nop())

        if stop == "w":
            @block.gpsimd
            def _(e):
                w(e, sA, sV, sT)
                e.wait_ge(qG.h, qG.c)
                e.wait_ge(_pm, pm.c)
            esW.close()
            es.close()
            return nc
        esW.close()
        esR = ExitStack()
        _e = esR.enter_context
        rw1s = _e(nc.sbuf_tensor("rw1s", [128, 8, HF], BF16))
        rb1s = _e(nc.sbuf_tensor("rb1s", [128, HF // 128], F32))
        rw2s = _e(nc.sbuf_tensor("rw2s", [128, 32, NE], BF16))
        rb2s = _e(nc.sbuf_tensor("rb2s", [NE, 1], F32))
        lnA = _e(nc.sbuf_tensor("lnA", [128, 2, DM], F32))
        lnB = _e(nc.sbuf_tensor("lnB", [128, 2, DM], F32))
        lnc = _e(nc.sbuf_tensor("lnc", [128, 8], F32))
        pdif = _e(nc.sbuf_tensor("pdif", [128, 4], F32))
        pexp = _e(nc.sbuf_tensor("pexp", [128, 4], F32))
        x1b = _e(nc.sbuf_tensor("x1b", [128, 4, DM], BF16))
        lnL = _e(nc.sbuf_tensor("lnL", [128, 2, 3, DM], F32))
        x1T = _e(nc.sbuf_tensor("x1T", [128, 8, 512], BF16))
        h1b = _e(nc.sbuf_tensor("h1b", [128, 32, 512], BF16))
        lgct = _e(nc.sbuf_tensor("lgct", [NE, 512], F32))
        lgtm = _e(nc.sbuf_tensor("lgtm", [128, 4, NE], F32))
        tka = _e(nc.sbuf_tensor("tka", [128, NE], F32))
        tkb = _e(nc.sbuf_tensor("tkb", [128, NE], F32))
        ivt = _e(nc.sbuf_tensor("ivt", [128, 4, 4], F32))
        pst = _e(nc.psum_tensor("pst", [128, 2, 1024], BF16))
        psh = _e(nc.psum_tensor("psh", [128, 2, 512], F32))
        pslg = _e(nc.psum_tensor("pslg", [NE, 512], F32))
        pstf = _e(nc.psum_tensor("pstf", [128, 2, 512], F32))
        if True:
            with nc.sbuf_tensor("rwtmp", [128, HF], F32) as rwtmp:
                for k in range(8):
                    @block.sync
                    def _(e, k=k):
                        if k == 0:
                            e.wait_ge(pm.h, pm.c)
                            w(e, sV, sA, sT)
                        else:
                            w(e, sV)
                        qL.inc(e.dma_start(
                            out=rwtmp[:],
                            in_=rw1.rearrange("(k p) h -> p k h",
                                              p=128)[:, k, :]), 16)

                    @block.vector
                    def _(e, k=k):
                        w(e, qL)
                        sV.inc(e.tensor_copy(out=rw1s[:, k, :], in_=rwtmp[:]))

                @block.sync
                def _(e):
                    w(e, sV)
                    qL.inc(e.dma_start(
                        out=rwtmp[:, 0:32 * NE].rearrange(
                            "p (k n) -> p k n", k=32),
                        in_=rw2.rearrange("(k p) n -> p k n", p=128)), 16)
                    qL.inc(e.dma_start(out=rb1s[:], in_=rb1v), 16)
                    qL.inc(e.dma_start(out=rb2s[:], in_=rb2v), 16)

                @block.vector
                def _(e):
                    w(e, qL)
                    sV.inc(e.tensor_copy(
                        out=rw2s[:],
                        in_=rwtmp[:, 0:32 * NE]
                            .rearrange("p (k n) -> p k n", k=32)))

            ql_ln, sv_add, qg_st, st_grp = {}, {}, {}, {}

            def ln_loads(e, i):
                tok0 = i * 128

                def arm(me):
                    pt = 1 - me
                    r0 = me * T + me * TH + tok0
                    r1 = pt * T + me * TH + tok0
                    e.dma_start(out=lnL[:, i % 2, 0, :],
                                in_=mix_sh.ap()[r0:r0 + 128, :]
                                ).then_inc(_qL, 16)
                    e.dma_start(out=lnL[:, i % 2, 1, :],
                                in_=mix_sh.ap()[r1:r1 + 128, :]
                                ).then_inc(_qL, 16)

                slot_branch(e, sslot, lambda: arm(0), lambda: arm(1))
                qL.c += 32
                qL.inc(e.dma_start(
                    out=lnL[:, i % 2, 2, :],
                    in_=x_tm[tok0:tok0 + 128, :]), 16)
                ql_ln[i] = qL.c

            @block.sync
            def _(e):
                w(e, sV, qG, sT)
                ln_loads(e, 0)

            for g in range(NRG):
                for tt in range(4):
                    ti = g * 4 + tt
                    tok0 = ti * 128

                    @block.sync
                    def _(e, ti=ti):
                        if ti + 1 < 4 * NRG:
                            if ti >= 1:
                                e.wait_ge(sV.h, sv_add[ti - 1])
                            ln_loads(e, ti + 1)

                    @block.vector
                    def _(e, ti=ti, g=g):
                        e.wait_ge(qL.h, ql_ln[ti])
                        if ti >= 1:
                            e.wait_ge(qG.h, qg_st[ti - 1])
                        if g >= 1:
                            e.wait_ge(sT.h, st_grp[g - 1])
                        sV.inc(e.tensor_add(out=lnA[:, 0, :],
                                            in0=lnL[:, ti % 2, 0, :],
                                            in1=lnL[:, ti % 2, 1, :]))
                        sV.inc(e.tensor_add(out=lnA[:, 0, :],
                                            in0=lnA[:, 0, :],
                                            in1=lnL[:, ti % 2, 2, :]))
                        sv_add[ti] = sV.c
                        sV.inc(e.reduce_sum(out=lnc[:, 0:1], in_=lnA[:, 0, :],
                                            axis=AX))
                        e.wait_ge(sV.h, sV.c)
                        sV.inc(e.tensor_scalar(out=lnc[:, 1:2], in0=lnc[:, 0:1],
                                               scalar1=1.0 / DM, scalar2=None,
                                               op0=ALU.mult))
                        e.wait_ge(sV.h, sV.c)
                        sV.inc(e.tensor_scalar(out=lnB[:, 1, :],
                                               in0=lnA[:, 0, :],
                                               scalar1=lnc[:, 1:2],
                                               scalar2=None,
                                               op0=ALU.subtract))
                        sV.inc(e.tensor_mul(out=lnA[:, 1, :],
                                            in0=lnB[:, 1, :],
                                            in1=lnB[:, 1, :]))
                        sV.inc(e.reduce_sum(out=lnc[:, 2:3], in_=lnA[:, 1, :],
                                            axis=AX))
                        e.wait_ge(sV.h, sV.c)
                        sV.inc(e.tensor_scalar(out=lnc[:, 3:4], in0=lnc[:, 2:3],
                                               scalar1=1.0 / DM, scalar2=None,
                                               op0=ALU.mult))

                    @block.scalar
                    def _(e):
                        w(e, sV)
                        sA.inc(e.activation(out=lnc[:, 4:5], in_=lnc[:, 3:4],
                                            func=AF.Ln, bias=lncst[:, 0:1]))
                        e.wait_ge(sA.h, sA.c)
                        sA.inc(e.activation(out=lnc[:, 5:6], in_=lnc[:, 4:5],
                                            func=AF.Exp, scale=lncst[:, 1:2]))

                    @block.vector
                    def _(e, tt=tt):
                        w(e, sA)
                        sV.inc(e.tensor_scalar(out=lnA[:, 0, :],
                                               in0=lnB[:, 1, :],
                                               scalar1=lnc[:, 5:6],
                                               scalar2=None,
                                               op0=ALU.mult))
                        e.wait_ge(sV.h, sV.c)
                        sV.inc(e.tensor_copy(out=x1b[:, tt, :],
                                             in_=lnA[:, 0, :]))

                    @block.gpsimd
                    def _(e, tok0=tok0, g=g, tt=tt):
                        w(e, sV)
                        qG.inc(e.dma_start(out=x1out[tok0:tok0 + 128, :],
                                           in_=lnA[:, 0, :]), 16)
                        qg_st[ti] = qG.c
                        if dbg and g == 0 and tt == 0:
                            qG.inc(e.dma_start(out=dbg_ln[:, 0:8],
                                               in_=lnc[:]), 16)
                            qG.inc(e.dma_start(out=dbg_ln[:, 8:8 + DM],
                                               in_=lnB[:, 1, :]), 16)
                            qG.inc(e.dma_start(out=dbg_ln[:, 8 + DM:],
                                               in_=lnA[:, 1, :]), 16)

                @block.tensor
                def _(e, g=g):
                    w(e, sV)
                    svw = sV.c
                    for j in range(32):
                        tt, mc = j // 8, j % 8
                        if j >= 2:
                            e.wait_ge(sV.h, svw + j - 1)
                        sT.inc(e.transpose(pst[:, j % 2, 0:128],
                                           x1b[:, tt, mc * 128:(mc + 1) * 128],
                                           id16[:]))
                st_grp[g] = sT.c

                @block.vector
                def _(e, g=g):
                    base_t = sT.c - 32
                    for j in range(32):
                        tt, mc = j // 8, j % 8
                        e.wait_ge(sT.h, base_t + j + 1)
                        sV.inc(e.tensor_copy(
                            out=x1T[:, mc, tt * 128:(tt + 1) * 128],
                            in_=pst[:, j % 2, 0:128]))

                @block.tensor
                def _(e, g=g):
                    w(e, sV)
                    saw = sA.c
                    for hc in range(32):
                        if hc >= 2:
                            e.wait_ge(sA.h, saw + hc - 1)
                        for k in range(8):
                            ins = e.matmul(psh[:, hc % 2, :],
                                           rw1s[:, k, hc * 128:(hc + 1) * 128],
                                           x1T[:, k, :],
                                           start=(k == 0), stop=(k == 7))
                        sT.inc(ins)

                @block.scalar
                def _(e, g=g):
                    base_t = sT.c - 32
                    for hc in range(32):
                        e.wait_ge(sT.h, base_t + hc + 1)
                        sA.inc(e.activation(out=h1b[:, hc, :],
                                            in_=psh[:, hc % 2, :],
                                            func=AF.Relu,
                                            bias=rb1s[:, hc:hc + 1]))

                @block.tensor
                def _(e, g=g):
                    base_a = sA.c - 32
                    for k2 in range(32):
                        e.wait_ge(sA.h, base_a + k2 + 1)
                        ins = e.matmul(pslg[:], rw2s[:, k2, :], h1b[:, k2, :],
                                       start=(k2 == 0), stop=(k2 == 31))
                    sT.inc(ins)

                @block.vector
                def _(e, g=g):
                    w(e, sT)
                    sV.inc(e.tensor_scalar(out=lgct[:], in0=pslg[:],
                                           scalar1=rb2s[:, 0:1], scalar2=None,
                                           op0=ALU.add))

                @block.tensor
                def _(e, g=g):
                    w(e, sV)
                    svw = sV.c
                    for tt in range(4):
                        if tt >= 2:
                            e.wait_ge(sV.h, svw + tt - 1)
                        sT.inc(e.transpose(pstf[:, tt % 2, 0:NE],
                                           lgct[:, tt * 128:(tt + 1) * 128],
                                           idf[0:NE, 0:NE]))

                @block.vector
                def _(e, g=g):
                    base_t = sT.c - 4
                    for tt in range(4):
                        e.wait_ge(sT.h, base_t + tt + 1)
                        sV.inc(e.tensor_copy(out=lgtm[:, tt, :],
                                             in_=pstf[:, tt % 2, 0:NE]))

                @block.vector
                def _(e, g=g):
                    def b(ins):
                        sV.inc(ins)
                        e.wait_ge(sV.h, sV.c)
                    for tt in range(4):
                        lg = lgtm[:, tt, :]
                        b(e.reduce_max(out=lnc[:, 0:1], in_=lg, axis=AX))
                        b(e.tensor_scalar(out=tka[:], in0=lg,
                                          scalar1=lnc[:, 0:1],
                                          scalar2=None, op0=ALU.is_ge))
                        b(e.scalar_tensor_tensor(out=tka[:], in0=tka[:],
                                                 scalar=-1024.0, in1=io8b[:],
                                                 op0=ALU.mult, op1=ALU.add))
                        b(e.tensor_reduce(out=lnc[:, 1:2], in_=tka[:], axis=AX,
                                          op=ALU.min))
                        b(e.tensor_scalar(out=ivt[:, tt, 0:1], in0=lnc[:, 1:2],
                                          scalar1=1024.0, scalar2=None,
                                          op0=ALU.add))
                        b(e.tensor_scalar(out=tka[:], in0=io8b[:],
                                          scalar1=ivt[:, tt, 0:1], scalar2=None,
                                          op0=ALU.is_equal))
                        b(e.scalar_tensor_tensor(out=tkb[:], in0=tka[:],
                                                 scalar=-2e30, in1=lg,
                                                 op0=ALU.mult, op1=ALU.add))
                        b(e.reduce_max(out=lnc[:, 2:3], in_=tkb[:], axis=AX))
                        b(e.tensor_scalar(out=tka[:], in0=tkb[:],
                                          scalar1=lnc[:, 2:3], scalar2=None,
                                          op0=ALU.is_ge))
                        b(e.scalar_tensor_tensor(out=tka[:], in0=tka[:],
                                                 scalar=-1024.0, in1=io8b[:],
                                                 op0=ALU.mult, op1=ALU.add))
                        b(e.tensor_reduce(out=lnc[:, 3:4], in_=tka[:], axis=AX,
                                          op=ALU.min))
                        b(e.tensor_scalar(out=ivt[:, tt, 1:2], in0=lnc[:, 3:4],
                                          scalar1=1024.0, scalar2=None,
                                          op0=ALU.add))
                        b(e.tensor_sub(out=pdif[:, tt:tt + 1],
                                       in0=lnc[:, 2:3], in1=lnc[:, 0:1]))

                @block.scalar
                def _(e, g=g):
                    w(e, sV)
                    sA.inc(e.activation(out=pexp[:], in_=pdif[:], func=AF.Exp))

                @block.vector
                def _(e, g=g):
                    w(e, sA)
                    sV.inc(e.tensor_scalar(out=tka[:, 0:4], in0=pexp[:],
                                           scalar1=1.0, scalar2=None,
                                           op0=ALU.add))
                    e.wait_ge(sV.h, sV.c)
                    sV.inc(e.reciprocal(out=pdif[:], in_=tka[:, 0:4]))
                    e.wait_ge(sV.h, sV.c)
                    for tt in range(4):
                        e.tensor_copy(out=ivt[:, tt, 2:3],
                                      in_=pdif[:, tt:tt + 1])
                        ins = e.tensor_mul(out=ivt[:, tt, 3:4],
                                           in0=pexp[:, tt:tt + 1],
                                           in1=pdif[:, tt:tt + 1])
                    sV.inc(ins)

                @block.gpsimd
                def _(e, g=g):
                    w(e, sV)
                    for tt in range(4):
                        tok0 = g * 512 + tt * 128
                        qG.inc(e.dma_start(out=ivout[tok0:tok0 + 128, :],
                                           in_=ivt[:, tt, :]), 16)

            if dbg:
                @block.sync
                def _(e):
                    w(e, qG, sA, sV)
                    qL.inc(e.dma_start(out=dbg_u, in_=u_d.ap()), 16)
                    qL.inc(e.dma_start(out=dbg_z, in_=zsil_d.ap()), 16)
                    qL.inc(e.dma_start(out=dbg_y2, in_=y2_d.ap()), 16)
                    qL.inc(e.dma_start(out=dbg_mix, in_=mix_sh.ap()), 16)
                    e.wait_ge(qL.h, qL.c)

            @block.gpsimd
            def _(e):
                e.wait_ge(qG.h, qG.c)

        esR.close()
    es.close()
    return nc


def build_L2(N=32768, stop='full'):
    NK = 2 * N
    NSTRIP = 16
    SW = NK // NSTRIP
    nc = bass.Bass("TRN2", target_bir_lowering=False, debug=False,
                   detect_race_conditions=False)

    ein = lambda n, s, d=F32: nc.dram_tensor(n, s, d, kind="ExternalInput").ap()
    x1f = ein("x1f", [N, DM])
    idsf = ein("idsf", [1, NK])
    tokv = ein("tokv", [1, NK], I32)
    ew1in = ein("ew1in", [DM, HF])
    ew2in = ein("ew2in", [HF, DM])
    eb1v = ein("eb1v", [128, 32])
    eb2v = ein("eb2v", [1, DM])
    emodv = ein("emodv", [128, 1])
    spatv = ein("spatv", [128, 128])
    selv = ein("selv", [128, 16])
    identv = ein("identv", [128, 128])

    eoout = nc.dram_tensor("eoout", [CAP, DM], F32, kind="ExternalOutput").ap()

    slot_d = nc.dram_tensor("slot_d", [NK, 1], I32)
    src_d = nc.dram_tensor("src_d", [CAP + 128, 1], I32)
    disp_d = nc.dram_tensor("disp_d", [CAP, DM], BF16)
    ew1b_d = nc.dram_tensor("ew1b_d", [8, 128, HF], BF16)
    ew2b_d = nc.dram_tensor("ew2b_d", [32, 128, DM], BF16)

    es = ExitStack()
    _e = es.enter_context
    emod = _e(nc.sbuf_tensor("emod", [128, 1], F32))
    spat = _e(nc.sbuf_tensor("spat", [128, 128], F32))
    sel = _e(nc.sbuf_tensor("sel", [128, 16], F32))
    id16 = _e(nc.sbuf_tensor("id16", [128, 128], BF16))
    idf = _e(nc.sbuf_tensor("idf", [128, 128], F32))
    eb1s = _e(nc.sbuf_tensor("eb1s", [128, 32], F32))
    eb2b = _e(nc.sbuf_tensor("eb2b", [128, DM], F32))
    _qL = _e(nc.semaphore("qL")); _qG = _e(nc.semaphore("qG"))
    _sV = _e(nc.semaphore("sV")); _sA = _e(nc.semaphore("sA"))
    _sT = _e(nc.semaphore("sT"))
    block = _e(nc.Block())
    if True:
        qL, qG, sV, sA, sT = Sem(_qL), Sem(_qG), Sem(_sV), Sem(_sA), Sem(_sT)
        rbnd_box = {}

        @block.gpsimd
        def _(e):
            r1 = e.alloc_register(rn("rbndcap"))
            e.reg_mov(r1, CAP - 1)
            r2 = e.alloc_register(rn("rbndn"))
            e.reg_mov(r2, N - 1)
            rbnd_box["cap"] = r1
            rbnd_box["n"] = r2

        @block.sync
        def _(e):
            for t_, src in ((emod, emodv), (spat, spatv), (sel, selv),
                            (idf, identv), (eb1s, eb1v)):
                qL.inc(e.dma_start(out=t_[:], in_=src), 16)
            qL.inc(e.dma_start(out=eb2b[:],
                               in_=eb2v.partition_broadcast(128)), 16)

        @block.vector
        def _(e):
            w(e, qL)
            sV.inc(e.tensor_copy(out=id16[:], in_=idf[:]))

        with nc.sbuf_tensor("wt32", [128, 2, HF], F32) as wt32, \
             nc.sbuf_tensor("wt16", [128, 2, HF], BF16) as wt16:
            w1v = ew1in.rearrange("(k p) h -> p k h", p=128)
            w2v = ew2in.rearrange("(k p) m -> p k m", p=128)
            seq = [(w1v, ew1b_d, HF, k) for k in range(8)] + \
                  [(w2v, ew2b_d, DM, k) for k in range(32)]
            ldq, csv, stg = {}, {}, {}
            pl_, pv_, pg_ = qL.c, sV.c, qG.c
            for j in range(len(seq)):
                pl_ += 16; ldq[j] = pl_
                pv_ += 1; csv[j] = pv_
                pg_ += 16; stg[j] = pg_
            for j, (src, dst, cols, k) in enumerate(seq):
                @block.sync
                def _(e, j=j, src=src, cols=cols, k=k):
                    if j == 0:
                        w(e, sV, qG)
                    if j >= 2:
                        e.wait_ge(sV.h, csv[j - 2])
                    qL.inc(e.dma_start(out=wt32[:, j % 2, 0:cols],
                                       in_=src[:, k, :]), 16)
                    assert qL.c == ldq[j]

                @block.vector
                def _(e, j=j, cols=cols):
                    e.wait_ge(qL.h, ldq[j])
                    if j >= 2:
                        e.wait_ge(qG.h, stg[j - 2])
                    sV.inc(e.tensor_copy(out=wt16[:, j % 2, 0:cols],
                                         in_=wt32[:, j % 2, 0:cols]))
                    assert sV.c == csv[j]

                @block.gpsimd
                def _(e, j=j, dst=dst, cols=cols, k=k):
                    e.wait_ge(sV.h, csv[j])
                    qG.inc(e.dma_start(out=dst.ap()[k, :, :],
                                       in_=wt16[:, j % 2, 0:cols]), 16)
                    assert qG.c == stg[j]

        esr = ExitStack()
        _e2 = esr.enter_context
        idt = _e2(nc.sbuf_tensor("idt", [128, SW], F32))
        ones = _e2(nc.sbuf_tensor("ones", [128, SW], F32))
        oneh = _e2(nc.sbuf_tensor("oneh", [128, SW], F32))
        cums = _e2(nc.sbuf_tensor("cums", [128, SW], F32))
        ocol = _e2(nc.sbuf_tensor("ocol", [128, 1], F32))
        mpos = _e2(nc.sbuf_tensor("mpos", [16, SW], F32))
        mhot = _e2(nc.sbuf_tensor("mhot", [16, SW], F32))
        slotf = _e2(nc.sbuf_tensor("slotf", [16, SW], F32))
        sloti = _e2(nc.sbuf_tensor("sloti", [16, SW], I32))
        ztile = _e2(nc.sbuf_tensor("ztile", [1, CAP + 128], I32))
        pso = _e2(nc.psum_tensor("pso", [128, 1], F32))
        psel = _e2(nc.psum_tensor("psel", [16, 512], F32))
        if True:
            @block.sync
            def _(e):
                w(e, sV, qG)
                for st in range(NSTRIP):
                    qL.inc(e.dma_start(
                        out=idt[st * 8:(st + 1) * 8, :],
                        in_=idsf[:, st * SW:(st + 1) * SW]
                            .partition_broadcast(8)), 16)

            @block.vector
            def _(e):
                w(e, qL)
                sV.inc(e.memset(ones[:], 1.0))
                e.wait_ge(sV.h, sV.c)
                sV.inc(e.tensor_scalar(out=oneh[:], in0=idt[:],
                                       scalar1=emod[:, 0:1],
                                       scalar2=None, op0=ALU.is_equal))
                e.wait_ge(sV.h, sV.c)
                sV.inc(e.tensor_tensor_scan(out=cums[:], data0=ones[:],
                                            data1=oneh[:], initial=0.0,
                                            op0=ALU.mult, op1=ALU.add))

            @block.tensor
            def _(e):
                w(e, sV)
                sT.inc(e.matmul(pso[:], spat[:], cums[:, SW - 1:SW],
                                start=True, stop=True))

            @block.vector
            def _(e):
                w(e, sT)
                sV.inc(e.tensor_copy(out=ocol[:], in_=pso[:]))
                e.wait_ge(sV.h, sV.c)
                sV.inc(e.tensor_scalar(out=cums[:], in0=cums[:],
                                       scalar1=ocol[:, 0:1], scalar2=None,
                                       op0=ALU.add))
                e.wait_ge(sV.h, sV.c)
                sV.inc(e.tensor_mul(out=ones[:], in0=oneh[:], in1=cums[:]))

            @block.tensor
            def _(e):
                w(e, sV)
                for j in range(SW // 512):
                    sT.inc(e.matmul(psel[:], sel[:],
                                    ones[:, j * 512:(j + 1) * 512],
                                    start=True, stop=True))
                    e.wait_ge(sV.h, sV.c + j + 1)
                for j in range(SW // 512):
                    sT.inc(e.matmul(psel[:], sel[:],
                                    oneh[:, j * 512:(j + 1) * 512],
                                    start=True, stop=True))
                    e.wait_ge(sV.h, sV.c + SW // 512 + j + 1)

            @block.vector
            def _(e):
                base_t = sT.c - 2 * (SW // 512)
                for j in range(SW // 512):
                    e.wait_ge(sT.h, base_t + j + 1)
                    sV.inc(e.tensor_copy(out=mpos[:, j * 512:(j + 1) * 512],
                                         in_=psel[:]))
                for j in range(SW // 512):
                    e.wait_ge(sT.h, base_t + SW // 512 + j + 1)
                    sV.inc(e.tensor_copy(out=mhot[:, j * 512:(j + 1) * 512],
                                         in_=psel[:]))

            @block.vector
            def _(e):
                w(e, sV)
                def b(ins):
                    sV.inc(ins)
                    e.wait_ge(sV.h, sV.c)
                b(e.tensor_scalar(out=mpos[:], in0=mpos[:], scalar1=1.0,
                                  scalar2=None, op0=ALU.subtract))
                b(e.tensor_scalar(out=slotf[:], in0=mpos[:],
                                  scalar1=float(CAP),
                                  scalar2=None, op0=ALU.is_lt))
                b(e.tensor_mul(out=slotf[:], in0=slotf[:], in1=mhot[:]))
                b(e.tensor_mul(out=mpos[:], in0=mpos[:], in1=slotf[:]))
                b(e.tensor_scalar(out=slotf[:], in0=slotf[:], scalar1=-1.0,
                                  scalar2=1.0, op0=ALU.mult, op1=ALU.add))
                b(e.scalar_tensor_tensor(out=slotf[:], in0=slotf[:],
                                         scalar=BIGI * 1.0, in1=mpos[:],
                                         op0=ALU.mult, op1=ALU.add))
                b(e.tensor_copy(out=sloti[:], in_=slotf[:]))
                sV.inc(e.memset(ztile[:], 0))

            @block.gpsimd
            def _(e):
                w(e, sV)
                qG.inc(e.dma_start(
                    out=slot_d.ap().rearrange("(s f) b -> s (f b)", s=16),
                    in_=sloti[:]), 16)
                qG.inc(e.dma_start(out=src_d.ap().rearrange("a b -> b a"),
                                   in_=ztile[:]), 16)

        if stop == "r":
            @block.gpsimd
            def _(e):
                w(e, sA, sV, sT)
                e.wait_ge(qG.h, qG.c)
            esr.close()
            es.close()
            return nc
        esr.close()
        NG8 = NK // 1024
        ess = ExitStack()
        ic8 = ess.enter_context(nc.sbuf_tensor("ic8", [128, 2, 8], I32))
        vc8 = ess.enter_context(nc.sbuf_tensor("vc8", [128, 2, 8], I32))
        if True:
            slot_v = slot_d.ap().rearrange("(G p f) b -> p G (f b)",
                                           p=128, f=8)
            tok_v = tokv.rearrange("a (G p f) -> a p G f", p=128, f=8)
            for g8 in range(NG8):
                @block.sync
                def _(e, g8=g8):
                    if g8 == 0:
                        w(e, qG)
                    elif g8 >= 2:
                        e.wait_ge(qG.h, qG.c - 16 * 8)
                    qL.inc(e.dma_start(
                        out=ic8[:, g8 % 2, :],
                        in_=slot_v[:, g8, :]), 16)
                    qL.inc(e.dma_start(
                        out=vc8[:, g8 % 2, :],
                        in_=tok_v[0, :, g8, :]), 16)

                @block.gpsimd
                def _(e, g8=g8):
                    w(e, qL)
                    for f in range(8):
                        qG.inc(e.indirect_dma_start(
                            out=src_d.ap(),
                            out_offset=IndirectOffsetOnAxis(
                                ap=ic8[:, g8 % 2, f:f + 1], axis=0),
                            in_=vc8[:, g8 % 2, f:f + 1], in_offset=None,
                            bounds_check=rbnd_box["cap"], oob_is_err=False),
                            16)

        if stop == "sc":
            @block.gpsimd
            def _(e):
                w(e, sA, sV, sT)
                e.wait_ge(qG.h, qG.c)
            ess.close()
            es.close()
            return nc
        ess.close()
        esg = ExitStack()
        sidx = esg.enter_context(nc.sbuf_tensor("sidx", [128, 32], I32))
        xg = esg.enter_context(nc.sbuf_tensor("xg", [128, 2, DM], F32))
        xg16 = esg.enter_context(nc.sbuf_tensor("xg16", [128, 2, DM], BF16))
        if True:
            @block.sync
            def _(e):
                w(e, qG)
                for sg in range(32):
                    qL.inc(e.dma_start(
                        out=sidx[:, sg:sg + 1],
                        in_=src_d.ap()[sg * 128:(sg + 1) * 128, :]), 16)

            with nc.sbuf_tensor("eq0", [128, 32], I32) as eq0:
                @block.vector
                def _(e):
                    w(e, qL)
                    sV.inc(e.tensor_scalar(out=eq0[:], in0=sidx[:], scalar1=0,
                                           scalar2=None, op0=ALU.is_equal))
                    e.wait_ge(sV.h, sV.c)
                    sV.inc(e.scalar_tensor_tensor(out=sidx[:], in0=eq0[:],
                                                  scalar=BIGI, in1=sidx[:],
                                                  op0=ALU.mult, op1=ALU.add))
                    e.wait_ge(sV.h, sV.c)
                    sV.inc(e.tensor_scalar(out=sidx[:], in0=sidx[:], scalar1=1,
                                           scalar2=None, op0=ALU.subtract))

            for sg in range(32):
                @block.vector
                def _(e, sg=sg):
                    if sg >= 2:
                        w(e, qG)
                    sV.inc(e.memset(xg[:, sg % 2, :], 0.0))

                @block.gpsimd
                def _(e, sg=sg):
                    w(e, sV)
                    qG.inc(e.indirect_dma_start(
                        out=xg[:, sg % 2, :], out_offset=None,
                        in_=x1f,
                        in_offset=IndirectOffsetOnAxis(ap=sidx[:, sg:sg + 1],
                                                       axis=0),
                        bounds_check=rbnd_box["n"], oob_is_err=False), 16)

                @block.vector
                def _(e, sg=sg):
                    w(e, qG)
                    sV.inc(e.tensor_copy(out=xg16[:, sg % 2, :],
                                         in_=xg[:, sg % 2, :]))

                @block.gpsimd
                def _(e, sg=sg):
                    w(e, sV)
                    qG.inc(e.dma_start(
                        out=disp_d.ap()[sg * 128:(sg + 1) * 128, :],
                        in_=xg16[:, sg % 2, :]), 16)

        if stop == "g":
            @block.gpsimd
            def _(e):
                w(e, sA, sV, sT)
                e.wait_ge(qG.h, qG.c)
            esg.close()
            es.close()
            return nc
        esg.close()
        esx = ExitStack()
        _e3 = esx.enter_context
        ew1s = _e3(nc.sbuf_tensor("ew1s", [128, 8, HF], BF16))
        dtile = _e3(nc.sbuf_tensor("dtile", [128, 4, DM], BF16))
        dT = _e3(nc.sbuf_tensor("dT", [128, 8, 512], BF16))
        h1s = _e3(nc.sbuf_tensor("h1s", [128, 32, 512], BF16))
        ew2r = _e3(nc.sbuf_tensor("ew2r", [128, 32, DM], BF16))
        eot = _e3(nc.sbuf_tensor("eot", [128, 2, 512], F32))
        pst2 = _e3(nc.psum_tensor("pst2", [128, 2, 128], BF16))
        psh2 = _e3(nc.psum_tensor("psh2", [128, 2, 512], F32))
        pse = _e3(nc.psum_tensor("pse", [128, 2, 512], F32))
        if True:
            @block.sync
            def _(e):
                w(e, qG)
                for k in range(8):
                    qL.inc(e.dma_start(out=ew1s[:, k, :],
                                       in_=ew1b_d.ap()[k, :, :]), 16)
                for k2 in range(32):
                    qL.inc(e.dma_start(out=ew2r[:, k2, :],
                                       in_=ew2b_d.ap()[k2, :, :]), 16)

            for st in range(8):
                @block.sync
                def _(e, st=st):
                    w(e, sT)
                    for j in range(4):
                        qL.inc(e.dma_start(
                            out=dtile[:, j, :],
                            in_=disp_d.ap()[st * 512 + j * 128:
                                            st * 512 + (j + 1) * 128, :]), 16)

                @block.sync
                def _(e, st=st):
                    w(e, sT, qL)
                    for j in range(32):
                        tt, mc = j // 8, j % 8
                        qL.inc(e.dma_start(
                            out=dT[:, mc, tt * 128:(tt + 1) * 128],
                            in_=dtile[:, tt, mc * 128:(mc + 1) * 128],
                            transpose=True), 16)

                @block.tensor
                def _(e, st=st):
                    w(e, sV, qL)
                    saw = sA.c
                    for hc in range(32):
                        if hc >= 2:
                            e.wait_ge(sA.h, saw + hc - 1)
                        for k in range(8):
                            ins = e.matmul(psh2[:, hc % 2, :],
                                           ew1s[:, k, hc * 128:(hc + 1) * 128],
                                           dT[:, k, :],
                                           start=(k == 0), stop=(k == 7))
                        sT.inc(ins)

                @block.scalar
                def _(e, st=st):
                    base_t = sT.c - 32
                    for hc in range(32):
                        e.wait_ge(sT.h, base_t + hc + 1)
                        sA.inc(e.activation(out=h1s[:, hc, :],
                                            in_=psh2[:, hc % 2, :],
                                            func=AF.Relu,
                                            bias=eb1s[:, hc:hc + 1]))

                for mh in range(2):
                    @block.tensor
                    def _(e, st=st, mh=mh):
                        w(e, qL, sA)
                        svw = sV.c
                        for ss in range(4):
                            if ss >= 2:
                                e.wait_ge(sV.h, svw + ss - 1)
                            for k2 in range(32):
                                ins = e.matmul(
                                    pse[:, ss % 2, :],
                                    h1s[:, k2, ss * 128:(ss + 1) * 128],
                                    ew2r[:, k2, mh * 512:(mh + 1) * 512],
                                    start=(k2 == 0), stop=(k2 == 31))
                            sT.inc(ins)

                    @block.vector
                    def _(e, st=st, mh=mh):
                        base_t = sT.c - 4
                        for ss in range(4):
                            e.wait_ge(sT.h, base_t + ss + 1)
                            if ss >= 2:
                                w(e, qG)
                            sV.inc(e.tensor_add(
                                out=eot[:, ss % 2, :], in0=pse[:, ss % 2, :],
                                in1=eb2b[:, mh * 512:(mh + 1) * 512]))

                    @block.gpsimd
                    def _(e, st=st, mh=mh):
                        base_v = sV.c - 4
                        for ss in range(4):
                            e.wait_ge(sV.h, base_v + ss + 1)
                            qG.inc(e.dma_start(
                                out=eoout[st * 512 + ss * 128:
                                          st * 512 + (ss + 1) * 128,
                                          mh * 512:(mh + 1) * 512],
                                in_=eot[:, ss % 2, :]), 16)

            @block.gpsimd
            def _(e):
                w(e, sA, sV, sT, qL)
                e.wait_ge(qG.h, qG.c)

        esx.close()
    es.close()
    return nc


def build_L3(N=32768, NTOK=4096):
    NK = 2 * N
    NSTRIP = 16
    SW = NK // NSTRIP
    nc = bass.Bass("TRN2", target_bir_lowering=False, debug=False,
                   detect_race_conditions=False)

    ein = lambda n, s, d=F32: nc.dram_tensor(n, s, d, kind="ExternalInput").ap()
    eof = ein("eof", [NE * CAP, DM])
    x1s = ein("x1s", [NTOK, DM])
    idsf = ein("idsf", [1, NK])
    p0s = ein("p0s", [16, N // 16])
    p1s = ein("p1s", [16, N // 16])
    emodv = ein("emodv", [128, 1])
    spatv = ein("spatv", [128, 128])
    sumpv = ein("sumpv", [128, 16])
    selpv = ein("selpv", [128, 16])
    stripev = ein("stripev", [1, 1], I32)
    iotacv = ein("iotacv", [128, 1], I32)
    lncv3 = ein("lncv3", [128, 2])

    out = nc.dram_tensor("out", [NTOK, DM], F32, kind="ExternalOutput").ap()

    idx0_d = nc.dram_tensor("idx0_d", [N, 1], I32)
    idx1_d = nc.dram_tensor("idx1_d", [N, 1], I32)
    pk0_d = nc.dram_tensor("pk0_d", [N, 1], F32)
    pk1_d = nc.dram_tensor("pk1_d", [N, 1], F32)

    es = ExitStack()
    _e = es.enter_context
    emod = _e(nc.sbuf_tensor("emod", [128, 1], F32))
    spat = _e(nc.sbuf_tensor("spat", [128, 128], F32))
    sump = _e(nc.sbuf_tensor("sump", [128, 16], F32))
    selp = _e(nc.sbuf_tensor("selp", [128, 16], F32))
    lncst3 = _e(nc.sbuf_tensor("lncst3", [128, 2], F32))
    _qL = _e(nc.semaphore("qL")); _qG = _e(nc.semaphore("qG"))
    _sV = _e(nc.semaphore("sV")); _sA = _e(nc.semaphore("sA"))
    _sT = _e(nc.semaphore("sT"))
    block = _e(nc.Block())
    if True:
        qL, qG, sV, sA, sT = Sem(_qL), Sem(_qG), Sem(_sV), Sem(_sA), Sem(_sT)

        @block.sync
        def _(e):
            for t_, src in ((emod, emodv), (spat, spatv), (sump, sumpv),
                            (selp, selpv), (lncst3, lncv3)):
                qL.inc(e.dma_start(out=t_[:], in_=src), 16)

        esr = ExitStack()
        _e2 = esr.enter_context
        idt = _e2(nc.sbuf_tensor("idt", [128, SW], F32))
        ones = _e2(nc.sbuf_tensor("ones", [128, SW], F32))
        oneh = _e2(nc.sbuf_tensor("oneh", [128, SW], F32))
        cums = _e2(nc.sbuf_tensor("cums", [128, SW], F32))
        ocol = _e2(nc.sbuf_tensor("ocol", [128, 1], F32))
        post = _e2(nc.sbuf_tensor("post", [16, SW], F32))
        ids16 = _e2(nc.sbuf_tensor("ids16", [16, SW], F32))
        keep = _e2(nc.sbuf_tensor("keep", [16, SW], F32))
        dei = _e2(nc.sbuf_tensor("dei", [16, SW], F32))
        deii = _e2(nc.sbuf_tensor("deii", [16, SW], I32))
        pso = _e2(nc.psum_tensor("pso", [128, 1], F32))
        psel = _e2(nc.psum_tensor("psel", [16, 512], F32))
        if True:
            @block.sync
            def _(e):
                w(e, qL)
                for st in range(NSTRIP):
                    qL.inc(e.dma_start(
                        out=idt[st * 8:(st + 1) * 8, :],
                        in_=idsf[:, st * SW:(st + 1) * SW]
                            .partition_broadcast(8)), 16)

            @block.vector
            def _(e):
                w(e, qL)
                sV.inc(e.memset(ones[:], 1.0))
                e.wait_ge(sV.h, sV.c)
                sV.inc(e.tensor_scalar(out=oneh[:], in0=idt[:],
                                       scalar1=emod[:, 0:1],
                                       scalar2=None, op0=ALU.is_equal))
                e.wait_ge(sV.h, sV.c)
                sV.inc(e.tensor_tensor_scan(out=cums[:], data0=ones[:],
                                            data1=oneh[:], initial=0.0,
                                            op0=ALU.mult, op1=ALU.add))

            @block.tensor
            def _(e):
                w(e, sV)
                sT.inc(e.matmul(pso[:], spat[:], cums[:, SW - 1:SW],
                                start=True, stop=True))

            @block.vector
            def _(e):
                w(e, sT)
                sV.inc(e.tensor_copy(out=ocol[:], in_=pso[:]))
                e.wait_ge(sV.h, sV.c)
                sV.inc(e.tensor_scalar(out=cums[:], in0=cums[:],
                                       scalar1=ocol[:, 0:1],
                                       scalar2=None, op0=ALU.add))
                e.wait_ge(sV.h, sV.c)
                sV.inc(e.tensor_mul(out=ones[:], in0=oneh[:], in1=cums[:]))

            @block.tensor
            def _(e):
                w(e, sV)
                for j in range(SW // 512):
                    sT.inc(e.matmul(psel[:], sump[:],
                                    ones[:, j * 512:(j + 1) * 512],
                                    start=True, stop=True))
                    e.wait_ge(sV.h, sV.c + j + 1)
                for j in range(SW // 512):
                    sT.inc(e.matmul(psel[:], selp[:],
                                    idt[:, j * 512:(j + 1) * 512],
                                    start=True, stop=True))
                    e.wait_ge(sV.h, sV.c + SW // 512 + j + 1)

            @block.vector
            def _(e):
                base_t = sT.c - 2 * (SW // 512)
                for j in range(SW // 512):
                    e.wait_ge(sT.h, base_t + j + 1)
                    sV.inc(e.tensor_copy(out=post[:, j * 512:(j + 1) * 512],
                                         in_=psel[:]))
                for j in range(SW // 512):
                    e.wait_ge(sT.h, base_t + SW // 512 + j + 1)
                    sV.inc(e.tensor_copy(out=ids16[:, j * 512:(j + 1) * 512],
                                         in_=psel[:]))

            @block.vector
            def _(e):
                w(e, sV)
                def b(ins):
                    sV.inc(ins)
                    e.wait_ge(sV.h, sV.c)
                b(e.tensor_scalar(out=post[:], in0=post[:], scalar1=1.0,
                                  scalar2=None, op0=ALU.subtract))
                b(e.tensor_scalar(out=keep[:], in0=post[:], scalar1=float(CAP),
                                  scalar2=None, op0=ALU.is_lt))
                b(e.tensor_scalar(out=post[:], in0=post[:],
                                  scalar1=float(CAP - 1),
                                  scalar2=None, op0=ALU.min))
                b(e.scalar_tensor_tensor(out=post[:], in0=ids16[:],
                                         scalar=float(CAP), in1=post[:],
                                         op0=ALU.mult, op1=ALU.add))

            @block.vector
            def _(e):
                sV.inc(e.tensor_copy(out=dei[:, 0:SW // 2],
                                     in_=post[:].rearrange("p (f k) -> p k f",
                                                           k=2)[:, 0, :]))
                e.wait_ge(sV.h, sV.c)
                sV.inc(e.tensor_copy(out=deii[:, 0:SW // 2],
                                     in_=dei[:, 0:SW // 2]))

            @block.gpsimd
            def _(e):
                w(e, sV)
                qG.inc(e.dma_start(
                    out=idx0_d.ap().rearrange("(s f) b -> s (f b)", s=16),
                    in_=deii[:, 0:SW // 2]), 16)

            @block.vector
            def _(e):
                w(e, qG)
                sV.inc(e.tensor_copy(out=dei[:, 0:SW // 2],
                                     in_=post[:].rearrange("p (f k) -> p k f",
                                                           k=2)[:, 1, :]))
                e.wait_ge(sV.h, sV.c)
                sV.inc(e.tensor_copy(out=deii[:, 0:SW // 2],
                                     in_=dei[:, 0:SW // 2]))

            @block.gpsimd
            def _(e):
                w(e, sV, qG)
                qG.inc(e.dma_start(
                    out=idx1_d.ap().rearrange("(s f) b -> s (f b)", s=16),
                    in_=deii[:, 0:SW // 2]), 16)

            pin = esr.enter_context(nc.sbuf_tensor("pin", [16, SW // 2], F32))
            if True:
                @block.sync
                def _(e):
                    w(e, sV)
                    qL.inc(e.dma_start(out=pin[:], in_=p0s), 16)

                @block.vector
                def _(e):
                    w(e, qL, qG)
                    sV.inc(e.tensor_copy(out=dei[:, 0:SW // 2],
                                         in_=keep[:].rearrange(
                                             "p (f k) -> p k f", k=2)[:, 0, :]))
                    e.wait_ge(sV.h, sV.c)
                    sV.inc(e.tensor_mul(out=dei[:, 0:SW // 2],
                                        in0=dei[:, 0:SW // 2], in1=pin[:]))

                @block.gpsimd
                def _(e):
                    w(e, sV)
                    qG.inc(e.dma_start(
                        out=pk0_d.ap().rearrange("(s f) b -> s (f b)", s=16),
                        in_=dei[:, 0:SW // 2]), 16)

                @block.sync
                def _(e):
                    w(e, sV, qG)
                    qL.inc(e.dma_start(out=pin[:], in_=p1s), 16)

                @block.vector
                def _(e):
                    w(e, qL, qG)
                    sV.inc(e.tensor_copy(out=dei[:, 0:SW // 2],
                                         in_=keep[:].rearrange(
                                             "p (f k) -> p k f", k=2)[:, 1, :]))
                    e.wait_ge(sV.h, sV.c)
                    sV.inc(e.tensor_mul(out=dei[:, 0:SW // 2],
                                        in0=dei[:, 0:SW // 2], in1=pin[:]))

                @block.gpsimd
                def _(e):
                    w(e, sV)
                    qG.inc(e.dma_start(
                        out=pk1_d.ap().rearrange("(s f) b -> s (f b)", s=16),
                        in_=dei[:, 0:SW // 2]), 16)

        esr.close()
        esa = ExitStack()
        _e3 = esa.enter_context
        g0 = _e3(nc.sbuf_tensor("g0", [128, DM], F32))
        g1 = _e3(nc.sbuf_tensor("g1", [128, DM], F32))
        yv = _e3(nc.sbuf_tensor("yv", [128, DM], F32))
        x1t = _e3(nc.sbuf_tensor("x1t", [128, DM], F32))
        ic0 = _e3(nc.sbuf_tensor("ic0", [128, 1], I32))
        ic1 = _e3(nc.sbuf_tensor("ic1", [128, 1], I32))
        pc0 = _e3(nc.sbuf_tensor("pc0", [128, 1], F32))
        pc1 = _e3(nc.sbuf_tensor("pc1", [128, 1], F32))
        lncc = _e3(nc.sbuf_tensor("lncc", [128, 8], F32))
        sstr = _e3(nc.sbuf_tensor("sstr", [1, 1], I32))
        iocol = _e3(nc.sbuf_tensor("iocol", [128, 1], I32))
        strc = _e3(nc.sbuf_tensor("strc", [128, 1], I32))
        bcol = _e3(nc.sbuf_tensor("bcol", [128, 1], I32))
        tcol = _e3(nc.sbuf_tensor("tcol", [128, 2, 1], I32))
        if True:
            @block.sync
            def _(e):
                qL.inc(e.dma_start(out=sstr[:], in_=stripev), 16)
                qL.inc(e.dma_start(out=iocol[:], in_=iotacv), 16)
                qL.inc(e.dma_start(out=strc[:],
                                   in_=stripev.partition_broadcast(128)), 16)

            @block.vector
            def _(e):
                w(e, qL)
                sV.inc(e.tensor_scalar(out=bcol[:], in0=strc[:], scalar1=NTOK,
                                       scalar2=None, op0=ALU.mult))
                e.wait_ge(sV.h, sV.c)
                sV.inc(e.tensor_tensor(out=bcol[:], in0=bcol[:], in1=iocol[:],
                                       op=ALU.add))

            for tt in range(NTOK // 128):
                @block.sync
                def _(e, tt=tt):
                    w(e, sV, qG)
                    qL.inc(e.dma_start(out=x1t[:],
                                       in_=x1s[tt * 128:(tt + 1) * 128, :]), 16)

                @block.vector
                def _(e, tt=tt):
                    w(e, qG)
                    sV.inc(e.tensor_scalar(out=tcol[:, tt % 2, :], in0=bcol[:],
                                           scalar1=tt * 128, scalar2=None,
                                           op0=ALU.add))

                @block.gpsimd
                def _(e, tt=tt):
                    w(e, qL, sV)
                    tc = tcol[:, tt % 2, :]
                    qG.inc(e.indirect_dma_start(
                        out=ic0[:], out_offset=None, in_=idx0_d.ap(),
                        in_offset=IndirectOffsetOnAxis(ap=tc, axis=0)), 16)
                    qG.inc(e.indirect_dma_start(
                        out=ic1[:], out_offset=None, in_=idx1_d.ap(),
                        in_offset=IndirectOffsetOnAxis(ap=tc, axis=0)), 16)
                    qG.inc(e.indirect_dma_start(
                        out=pc0[:], out_offset=None, in_=pk0_d.ap(),
                        in_offset=IndirectOffsetOnAxis(ap=tc, axis=0)), 16)
                    qG.inc(e.indirect_dma_start(
                        out=pc1[:], out_offset=None, in_=pk1_d.ap(),
                        in_offset=IndirectOffsetOnAxis(ap=tc, axis=0)), 16)
                    e.wait_ge(qG.h, qG.c)
                    qG.inc(e.indirect_dma_start(
                        out=g0[:], out_offset=None, in_=eof,
                        in_offset=IndirectOffsetOnAxis(ap=ic0[:], axis=0)), 16)
                    qG.inc(e.indirect_dma_start(
                        out=g1[:], out_offset=None, in_=eof,
                        in_offset=IndirectOffsetOnAxis(ap=ic1[:], axis=0)), 16)

                @block.vector
                def _(e, tt=tt):
                    w(e, qG)
                    sV.inc(e.tensor_scalar(out=g0[:], in0=g0[:],
                                            scalar1=pc0[:, 0:1],
                                            scalar2=None, op0=ALU.mult))
                    e.wait_ge(sV.h, sV.c)
                    sV.inc(e.scalar_tensor_tensor(out=yv[:], in0=g1[:],
                                                  scalar=pc1[:, 0:1],
                                                  in1=g0[:],
                                                  op0=ALU.mult, op1=ALU.add))
                    e.wait_ge(sV.h, sV.c)
                    sV.inc(e.tensor_add(out=yv[:], in0=yv[:], in1=x1t[:]))
                    e.wait_ge(sV.h, sV.c)
                    sV.inc(e.reduce_sum(out=lncc[:, 0:1], in_=yv[:], axis=AX))
                    e.wait_ge(sV.h, sV.c)
                    sV.inc(e.tensor_scalar(out=lncc[:, 1:2], in0=lncc[:, 0:1],
                                           scalar1=1.0 / DM, scalar2=None,
                                           op0=ALU.mult))
                    e.wait_ge(sV.h, sV.c)
                    sV.inc(e.tensor_scalar(out=g0[:], in0=yv[:],
                                           scalar1=lncc[:, 1:2], scalar2=None,
                                           op0=ALU.subtract))

                @block.scalar
                def _(e, tt=tt):
                    w(e, sV)
                    sA.inc(e.activation(out=g1[:], in_=g0[:], func=AF.Square))

                @block.vector
                def _(e, tt=tt):
                    w(e, sA)
                    sV.inc(e.reduce_sum(out=lncc[:, 2:3], in_=g1[:], axis=AX))
                    e.wait_ge(sV.h, sV.c)
                    sV.inc(e.tensor_scalar(out=lncc[:, 3:4], in0=lncc[:, 2:3],
                                           scalar1=1.0 / DM, scalar2=None,
                                           op0=ALU.mult))

                @block.scalar
                def _(e, tt=tt):
                    w(e, sV)
                    sA.inc(e.activation(out=lncc[:, 4:5], in_=lncc[:, 3:4],
                                        func=AF.Ln, bias=lncst3[:, 0:1]))
                    e.wait_ge(sA.h, sA.c)
                    sA.inc(e.activation(out=lncc[:, 5:6], in_=lncc[:, 4:5],
                                        func=AF.Exp, scale=lncst3[:, 1:2]))

                @block.vector
                def _(e, tt=tt):
                    w(e, sA)
                    sV.inc(e.tensor_scalar(out=yv[:], in0=g0[:],
                                           scalar1=lncc[:, 5:6], scalar2=None,
                                           op0=ALU.mult))

                @block.gpsimd
                def _(e, tt=tt):
                    w(e, sV)
                    qG.inc(e.dma_start(out=out[tt * 128:(tt + 1) * 128, :],
                                       in_=yv[:]), 16)

            @block.gpsimd
            def _(e):
                e.wait_ge(qG.h, qG.c)

        esa.close()
    es.close()
    return nc


_cache = {}


def _get(key, builder):
    if key not in _cache:
        _cache[key] = builder()
    return _cache[key]


def _rs(a, nch=NCH, inner=None):
    a = np.asarray(a)
    k = a.shape[1] if a.ndim == 2 else 1
    a = a.reshape(nch, 128, -1).transpose(1, 0, 2).reshape(128, -1)
    return np.ascontiguousarray(a)


def kernel(L0_attn_x, L0_attn_W_in, L0_attn_conv_w, L0_attn_conv_b,
           L0_attn_W_x, L0_attn_W_dt, L0_attn_b_dt, L0_attn_A_log,
           L0_attn_D_param, L0_attn_W_out, L0_ffn_rw1, L0_ffn_rb1,
           L0_ffn_rw2, L0_ffn_rb2, L0_ffn_ew1, L0_ffn_eb1, L0_ffn_ew2,
           L0_ffn_eb2):
    x = np.asarray(L0_attn_x, np.float32)
    B, T, _ = x.shape
    TH = T // 2
    N = B * T
    NK = 2 * N

    ident = np.eye(128, dtype=np.float32)
    iota8 = np.arange(NE, dtype=np.float32)[None, :]
    nonce = np.array([[np.random.randint(1, 1 << 30)]], np.int32)

    nc1 = _get(("L1", T), lambda: build_L1(T=T))
    in1 = []
    for c in range(NCORES):
        b, dh = c >> 1, c & 1
        sl = slice(dh * DH, (dh + 1) * DH)
        m = dict(
            xT=np.ascontiguousarray(x[b].T),
            x_tm=np.ascontiguousarray(x[b, dh * TH:(dh + 1) * TH]),
            winT=np.ascontiguousarray(
                np.concatenate([L0_attn_W_in[sl], L0_attn_W_in[2048:][sl]],
                               0).T.astype(np.float32)),
            cwv=_rs(np.asarray(L0_attn_conv_w)[sl, 0, :]),
            cbv=_rs(np.asarray(L0_attn_conv_b)[sl, None]),
            wxT=np.ascontiguousarray(np.asarray(L0_attn_W_x)[:, sl].T),
            wdtv=_rs(np.asarray(L0_attn_W_dt)[sl]),
            bdtv=_rs(np.asarray(L0_attn_b_dt)[sl, None]),
            alogv=_rs(np.asarray(L0_attn_A_log)[sl]),
            dpv=_rs(np.asarray(L0_attn_D_param)[sl, None]),
            woutT=np.ascontiguousarray(np.asarray(L0_attn_W_out)[:, sl].T),
            rw1=np.asarray(L0_ffn_rw1, np.float32),
            rb1v=np.ascontiguousarray(
                np.asarray(L0_ffn_rb1).reshape(32, 128).T),
            rw2=np.asarray(L0_ffn_rw2, np.float32),
            rb2v=np.asarray(L0_ffn_rb2, np.float32).reshape(NE, 1),
            identv=ident, iotav=iota8, noncev=nonce,
            slotv=np.array([[dh]], np.int32),
            lncv=np.tile(np.array([[1e-5, -0.5]], np.float32), (128, 1)),
        )
        in1.append({k: np.ascontiguousarray(v) for k, v in m.items()})
    r1 = run_bass_kernel_spmd(nc1, in1, core_ids=list(range(NCORES)))
    x1f = np.concatenate([r1.results[c]["x1out"] for c in range(NCORES)], 0)
    iv = np.concatenate([r1.results[c]["ivout"] for c in range(NCORES)], 0)

    idsf = np.ascontiguousarray(
        np.stack([iv[:, 0], iv[:, 1]], 1).reshape(1, NK).astype(np.float32))
    tokv = np.ascontiguousarray(
        (np.arange(NK, dtype=np.int64) // 2 + 1).astype(np.int32)[None, :])
    emodv = (np.arange(128, dtype=np.float32) % 8)[:, None]
    spatv = np.zeros((128, 128), np.float32)
    for rp in range(128):
        for r in range(128):
            if rp % 8 == r % 8 and rp // 8 < r // 8:
                spatv[rp, r] = 1.0

    nc2 = _get(("L2", N), lambda: build_L2(N=N))
    in2 = []
    for e_ in range(NCORES):
        selv = np.zeros((128, 16), np.float32)
        for q in range(16):
            selv[q * 8 + e_, q] = 1.0
        m = dict(
            x1f=x1f, idsf=idsf, tokv=tokv,
            ew1in=np.asarray(L0_ffn_ew1)[e_].astype(np.float32),
            ew2in=np.asarray(L0_ffn_ew2)[e_].astype(np.float32),
            eb1v=np.ascontiguousarray(
                np.asarray(L0_ffn_eb1)[e_].reshape(32, 128).T),
            eb2v=np.asarray(L0_ffn_eb2)[e_].astype(np.float32)[None, :],
            emodv=emodv, spatv=spatv, selv=selv, identv=ident,
        )
        in2.append({k: np.ascontiguousarray(v) for k, v in m.items()})
    r2 = run_bass_kernel_spmd(nc2, in2, core_ids=list(range(NCORES)))
    eof = np.concatenate([r2.results[e_]["eoout"] for e_ in range(NCORES)], 0)

    sumpv = np.zeros((128, 16), np.float32)
    selpv = np.zeros((128, 16), np.float32)
    for q in range(16):
        for e_ in range(8):
            sumpv[q * 8 + e_, q] = 1.0
        selpv[q * 8, q] = 1.0
    p0s = np.ascontiguousarray(iv[:, 2].reshape(16, N // 16))
    p1s = np.ascontiguousarray(iv[:, 3].reshape(16, N // 16))

    nc3 = _get(("L3", N), lambda: build_L3(N=N, NTOK=N // NCORES))
    NTOK = N // NCORES
    in3 = []
    for c in range(NCORES):
        m = dict(
            eof=eof, x1s=x1f[c * NTOK:(c + 1) * NTOK], idsf=idsf,
            p0s=p0s, p1s=p1s, emodv=emodv, spatv=spatv,
            sumpv=sumpv, selpv=selpv,
            stripev=np.array([[c]], np.int32),
            iotacv=np.arange(128, dtype=np.int32)[:, None],
            lncv3=np.tile(np.array([[1e-5, -0.5]], np.float32), (128, 1)),
        )
        in3.append({k: np.ascontiguousarray(v) for k, v in m.items()})
    r3 = run_bass_kernel_spmd(nc3, in3, core_ids=list(range(NCORES)))
    y = np.concatenate([r3.results[c]["out"] for c in range(NCORES)], 0)
    return y.reshape(B, T, DM)
